# version 13
# baseline (speedup 1.0000x reference)
import contextlib
import numpy as np
import ml_dtypes
import concourse.bass as bass
import concourse.mybir as mybir
from concourse.bass_utils import run_bass_kernel_spmd

F32 = mybir.dt.float32
BF16 = mybir.dt.bfloat16
I32 = mybir.dt.int32
U8 = mybir.dt.uint8
AF = mybir.ActivationFunctionType
ALU = mybir.AluOpType
AX = mybir.AxisListType

NCORES = 8
B = 16
T = 2048
D = 1024
SPC = B // NCORES
N = SPC * T
NH = 16
HD = 64
TC = 512
NTC = N // TC
DCH = D // 128

EXP_SHIFT = 16.0



class Sem:
    def __init__(self, h):
        self.h = h
        self.n = 0

    def inc(self, instr, k=1):
        instr.then_inc(self.h, k)
        self.n += k
        return self.n


SEM_ALIAS = {
    "a_ev": "PS5", "a_lb": "PS1", "a_ld": "PS0", "a_mm": "PS4",
    "a_ro": "PS6", "a_te0": "PS14", "a_te1": "PS15", "a_tp": "PS2",
    "a_tr": "PS3", "a_w": "PG0", "a_w0": "PG1", "a_w1": "PG2", "d_ex": "PS8",
    "d_ly": "PS10", "d_mm": "PS12", "d_no": "PS11", "d_pr": "PS9",
    "d_s": "PS13", "d_sc": "PS7", "d_v": "PS14", "e_ev": "PS13",
    "e_mm": "PS12", "e_so0": "PG3", "e_so1": "PG4", "epi_ev": "PS2",
    "epi_ld": "PS0", "epi_so": "PG0", "epi_tr": "PS1", "f2_bc": "PS4",
    "f2_bl": "PS1", "f2_bt": "PS2", "f2_cc": "PS12", "f2_cg": "PS11",
    "f2_cl0": "PS13", "f2_cl1": "PS14", "f2_cv": "PS3", "f2_dq": "PS5",
    "f2_gr": "PS10", "f2_h1": "PS7", "f2_hs": "PS8", "f2_ld": "PS0",
    "f2_p2": "PS9", "f2_so0": "PG1", "f2_so1": "PG2", "f2_sr": "PG0",
    "f2_xl": "PS6", "g_bd": "PS8", "g_bl": "PS2", "g_bt": "PS3",
    "g_eev": "PS14", "g_emm": "PS13", "g_es0": "PG2", "g_es1": "PG3",
    "g_ev": "PS5", "g_init": "PS0", "g_ks": "PS7", "g_kv": "PS6",
    "g_ld": "PS0", "g_mm": "PS4", "g_ms": "PS12", "g_rm": "PS9",
    "g_rs": "PS11", "g_rv": "PS10", "g_te0": "PS15", "g_te1": "PS16",
    "g_tr": "PS1", "g_w0": "PG0", "g_w1": "PG1", "ln_ap": "PS6",
    "ln_bc": "PS4", "ln_cb": "PS1", "ln_gb": "PS8", "ln_gbt": "PS9",
    "ln_ld": "PS0", "ln_mm": "PS2", "ln_rs": "PS5", "ln_rv": "PS7",
    "ln_so": "PG0", "ln_st": "PS3", "m_ac": "PS9", "m_b": "PS1",
    "m_b2": "PG0", "m_bt": "PS2", "m_cg": "PS10", "m_cv": "PS11",
    "m_h1": "PS5", "m_hs": "PS6", "m_hv": "PS7", "m_ld": "PS0",
    "m_p2": "PS8", "m_so0": "PG2", "m_so1": "PG3", "m_w10": "PS12",
    "m_w11": "PS13", "m_w2": "PG1", "m_wb": "PS3", "m_wv": "PS4",
    "m_xl0": "PS14", "m_xl1": "PS15", "n_cc": "PS14", "n_cg": "PS13",
    "n_cl0": "PS15", "n_cl1": "PS16", "n_gr": "PS12", "n_gv": "PS10",
    "n_io": "PS3", "n_ld": "PS4", "n_lg": "PS5", "n_lv": "PS6",
    "n_pre": "PS0", "n_prg": "PG0", "n_prw": "PG1", "n_px": "PS9",
    "n_rm": "PS11", "n_so0": "PG4", "n_so1": "PG5", "n_te": "PS2",
    "n_tr": "PS1", "n_w10": "PG2", "n_w11": "PG3", "n_y1": "PS7",
    "n_ys": "PS8", "pro_ev": "PS2", "pro_ld": "PS0", "pro_so": "PG0",
    "pro_tr": "PS1", "q_c": "PS20", "q_gw": "PG5", "q_ld": "PS18",
    "q_mm": "PS19", "q_so": "PG6", "q_tr": "PS21", "r_c": "PS17",
    "r_ld": "PS16", "r_so": "PG4", "z_ae": "PS4", "z_b2": "PS3",
    "z_bt": "PS2", "z_cv": "PS8", "z_ev": "PS6", "z_ga": "PS21",
    "z_ld": "PS0", "z_mm": "PS5", "z_oe": "PS23", "z_om": "PS22",
    "z_pb": "PS15", "z_pe": "PS16", "z_rs": "PS7", "z_sc": "PS18",
    "z_si": "PS9", "z_sl": "PS20", "z_so0": "PG2", "z_so1": "PG3",
    "z_sp": "PS19", "z_sr": "PG1", "z_ss": "PS17", "z_w0": "PS24",
    "z_w1": "PS25", "z_wg": "PG0", "z_wx": "PS1", "z_xd": "PS10",
    "z_xe": "PS11",
}


class Ctx:
    def __init__(self, nc, stack):
        self.nc = nc
        self.stack = stack
        self._sems = {}

    def sem(self, name):
        name = SEM_ALIAS.get(name, name)
        if name not in self._sems:
            h = self.stack.enter_context(self.nc.semaphore(name))
            self._sems[name] = Sem(h)
        return self._sems[name]

    def sbuf(self, name, shape, dt):
        return self.stack.enter_context(self.nc.sbuf_tensor(name, shape, dt))


CONSTS = [1e-5, -EXP_SHIFT, 0.5, -128.0]


def common_setup(C):
    nc = C.nc
    C.ident_f = C.sbuf("g_ident_f", [128, 128], F32)
    C.ident_b = C.sbuf("g_ident_b", [128, 128], BF16)
    C.ones_f = C.sbuf("g_ones_f", [128, 128], F32)
    C.ones_b = C.sbuf("g_ones_b", [128, 128], BF16)
    C.ones512_b = C.sbuf("g_ones512b", [128, 512], BF16)
    C.cmask = [C.sbuf(f"g_cmask{i}", [128, 512], BF16) for i in range(4)]
    C.cvals = {}
    for v in CONSTS:
        t = C.sbuf(f"g_const_{len(C.cvals)}", [128, 1], F32)
        nc.const_aps.register(F32, v, t[:])
        C.cvals[v] = t
    ini = C.sem("g_init")
    with nc.Block() as blk:
        @blk.gpsimd
        def _(g):
            for v, t in C.cvals.items():
                ini.inc(g.memset(t[:], v), 1)
            ini.inc(g.memset(C.ones_f[:], 1.0), 1)
            ini.inc(g.memset(C.ones_b[:], 1.0), 1)
            ini.inc(g.memset(C.ones512_b[:], 1.0), 1)
            g.wait_ge(ini.h, ini.n)
            g.affine_select(out=C.ident_f[:], in_=C.ones_f[:], pattern=[[1, 128]],
                            compare_op=ALU.is_equal, fill=0.0, base=0,
                            channel_multiplier=-1)
            g.affine_select(out=C.ident_b[:], in_=C.ones_b[:], pattern=[[1, 128]],
                            compare_op=ALU.is_equal, fill=0.0, base=0,
                            channel_multiplier=-1)
            for o in range(4):
                g.affine_select(out=C.cmask[o][:], in_=C.ones512_b[:],
                                pattern=[[1, 512]], compare_op=ALU.is_ge,
                                fill=0.0, base=-128 * o, channel_multiplier=-1)



def emit_pro(C, x_ap, xfm_ap):
    nc = C.nc
    with contextlib.ExitStack() as st:
        E = st.enter_context
        ld = C.sem("pro_ld")
        tr = C.sem("pro_tr")
        ev = C.sem("pro_ev")
        so = C.sem("pro_so")
        ld0, tr0, ev0, so0 = ld.n, tr.n, ev.n, so.n
        xt = [E(nc.sbuf_tensor(f"pro_x{i}", [128, D], F32)) for i in range(8)]
        stg = [E(nc.sbuf_tensor(f"pro_s{c}", [128, 1024], F32)) for c in range(8)]
        pp = [E(nc.psum_tensor(f"pro_p{i}", [128, 512], F32)) for i in range(2)]
        NG = N // 1024

        with nc.Block() as blk:
            @blk.sync
            def _(sy):
                for g_ in range(NG):
                    if g_ > 0:
                        sy.wait_ge(so.h, so0 + g_ * 8 * 16)
                        sy.wait_ge(tr.h, tr0 + g_ * 64)
                    for i in range(8):
                        ld.inc(sy.dma_start(
                            out=xt[i][:],
                            in_=x_ap[g_ * 1024 + i * 128:g_ * 1024 + (i + 1) * 128, :]),
                            16)

            @blk.tensor
            def _(t):
                k = 0
                for g_ in range(NG):
                    t.wait_ge(ld.h, ld0 + (g_ + 1) * 128)
                    for i in range(8):
                        for c in range(8):
                            t.wait_ge(ev.h, ev0 + max(0, k - 1))
                            tr.inc(t.transpose(pp[k % 2][:, 0:128],
                                               xt[i][:, c * 128:(c + 1) * 128],
                                               C.ident_f[:]), 1)
                            k += 1

            @blk.vector
            def _(v):
                k = 0
                for g_ in range(NG):
                    for i in range(8):
                        for c in range(8):
                            v.wait_ge(tr.h, tr0 + k + 1)
                            ev.inc(v.tensor_copy(out=stg[c][:, i * 128:(i + 1) * 128],
                                                 in_=pp[k % 2][:, 0:128]), 1)
                            k += 1

            @blk.gpsimd
            def _(g):
                for g_ in range(NG):
                    g.wait_ge(ev.h, ev0 + (g_ + 1) * 64)
                    for c in range(8):
                        so.inc(g.dma_start(
                            out=xfm_ap[c * 128:(c + 1) * 128,
                                       g_ * 1024:(g_ + 1) * 1024],
                            in_=stg[c][:]), 16)
                g.wait_ge(so.h, so.n)



def emit_epi(C, xfm_ap, out_ap):
    nc = C.nc
    with contextlib.ExitStack() as st:
        E = st.enter_context
        ld = C.sem("epi_ld")
        tr = C.sem("epi_tr")
        ev = C.sem("epi_ev")
        so = C.sem("epi_so")
        ld0, tr0, ev0, so0 = ld.n, tr.n, ev.n, so.n
        xt = [E(nc.sbuf_tensor(f"epi_x{c}", [128, 1024], F32)) for c in range(8)]
        stg = [E(nc.sbuf_tensor(f"epi_s{i}", [128, 1024], F32)) for i in range(8)]
        pp = [E(nc.psum_tensor(f"epi_p{i}", [128, 512], F32)) for i in range(2)]
        NG = N // 1024

        with nc.Block() as blk:
            @blk.sync
            def _(sy):
                for g_ in range(NG):
                    if g_ > 0:
                        sy.wait_ge(so.h, so0 + g_ * 8 * 16)
                        sy.wait_ge(tr.h, tr0 + g_ * 64)
                    for c in range(8):
                        ld.inc(sy.dma_start(
                            out=xt[c][:],
                            in_=xfm_ap[c * 128:(c + 1) * 128,
                                       g_ * 1024:(g_ + 1) * 1024]), 16)

            @blk.tensor
            def _(t):
                k = 0
                for g_ in range(NG):
                    t.wait_ge(ld.h, ld0 + (g_ + 1) * 128)
                    for i in range(8):
                        for c in range(8):
                            t.wait_ge(ev.h, ev0 + max(0, k - 1))
                            tr.inc(t.transpose(pp[k % 2][:, 0:128],
                                               xt[c][:, i * 128:(i + 1) * 128],
                                               C.ident_f[:]), 1)
                            k += 1

            @blk.vector
            def _(v):
                k = 0
                for g_ in range(NG):
                    for i in range(8):
                        for c in range(8):
                            v.wait_ge(tr.h, tr0 + k + 1)
                            ev.inc(v.tensor_copy(out=stg[i][:, c * 128:(c + 1) * 128],
                                                 in_=pp[k % 2][:, 0:128]), 1)
                            k += 1

            @blk.gpsimd
            def _(g):
                for g_ in range(NG):
                    g.wait_ge(ev.h, ev0 + (g_ + 1) * 64)
                    for i in range(8):
                        so.inc(g.dma_start(
                            out=out_ap[g_ * 1024 + i * 128:
                                       g_ * 1024 + (i + 1) * 128, :],
                            in_=stg[i][:]), 16)
                g.wait_ge(so.h, so.n)



def emit_ln(C, tag, xfm_ap, xn_ap, xr_ap=None, in_bf=False,
            gamma_ap=None, beta_ap=None):
    nc = C.nc
    with contextlib.ExitStack() as st:
        E = st.enter_context
        ld = C.sem("ln_ld")
        cb = C.sem("ln_cb")
        mm = C.sem("ln_mm")
        stt = C.sem("ln_st")
        bc = C.sem("ln_bc")
        rs = C.sem("ln_rs")
        ap = C.sem("ln_ap")
        rv = C.sem("ln_rv")
        so = C.sem("ln_so")
        ld0, cb0, mm0, st0, bc0, rs0, ap0, rv0, so0 = (
            ld.n, cb.n, mm.n, stt.n, bc.n, rs.n, ap.n, rv.n, so.n)
        DTI = BF16 if in_bf else F32
        xt = [E(nc.sbuf_tensor(f"{tag}_x{c}", [128, TC], DTI)) for c in range(8)]
        sqf = [E(nc.sbuf_tensor(f"{tag}_sq{c}", [128, TC], DTI)) for c in range(8)]
        if gamma_ap is not None:
            gbr_ = E(nc.sbuf_tensor(f"{tag}_gbr", [16, 128], F32))
            gbc = E(nc.sbuf_tensor(f"{tag}_gbc", [128, 16], F32))
        xnb = [E(nc.sbuf_tensor(f"{tag}_xn{c}", [128, TC], BF16)) for c in range(8)]
        xf2 = [E(nc.sbuf_tensor(f"{tag}_x2{c}", [128, TC], F32)) for c in range(8)]
        x0f = E(nc.sbuf_tensor(f"{tag}_x0f", [2, TC], F32))
        srow = E(nc.sbuf_tensor(f"{tag}_stat", [1, 12 * TC], F32))
        SL = lambda i: srow[:, i * TC:(i + 1) * TC]
        rrow = E(nc.sbuf_tensor(f"{tag}_rr", [1, TC], F32))
        ps_sum = E(nc.psum_tensor(f"{tag}_psum", [1, TC], F32))
        ps_sq = E(nc.psum_tensor(f"{tag}_psq", [1, TC], F32))
        ps_rst = E(nc.psum_tensor(f"{tag}_prst", [128, TC], F32))
        ps_mrs = E(nc.psum_tensor(f"{tag}_pmrs", [128, TC], F32))
        ps_r = E(nc.psum_tensor(f"{tag}_pr", [1, TC], F32))
        if gamma_ap is not None:
            gb_l = C.sem("ln_gb")
            gb_t = C.sem("ln_gbt")
            with nc.Block() as blk:
                @blk.sync
                def _(sy):
                    gb_l.inc(sy.dma_start(
                        out=gbr_[0:8, :],
                        in_=gamma_ap.rearrange("(a b) -> a b", b=128)), 16)
                    gb_l.inc(sy.dma_start(
                        out=gbr_[8:16, :],
                        in_=beta_ap.rearrange("(a b) -> a b", b=128)), 16)

                @blk.tensor
                def _(t):
                    t.wait_ge(gb_l.h, gb_l.n)
                    gb_t.inc(t.transpose(ps_rst[:, 0:16], gbr_[:],
                                         C.ident_f[0:16, 0:16]), 1)

                @blk.vector
                def _(v):
                    v.wait_ge(gb_t.h, gb_t.n)
                    gb_t.inc(v.tensor_copy(out=gbc[:], in_=ps_rst[:, 0:16]), 1)
        P_AP = 17 if xr_ap is not None else 16
        if gamma_ap is not None:
            P_AP += 8
        P_ST = 12
        P_SO = 9 if xr_ap is not None else 8

        with nc.Block() as blk:
            @blk.sync
            def _(sy):
                for ch in range(NTC):
                    if ch > 0:
                        sy.wait_ge(ap.h, ap0 + ch * P_AP)
                        sy.wait_ge(mm.h, mm0 + ch)
                    for c in range(8):
                        ld.inc(sy.dma_start(out=xt[c][:],
                                            in_=xfm_ap[c * 128:(c + 1) * 128,
                                                       ch * TC:(ch + 1) * TC]), 16)

            @blk.scalar
            def _(s):
                for ch in range(NTC):
                    s.wait_ge(ld.h, ld0 + (ch + 1) * 128)
                    if ch > 0:
                        s.wait_ge(mm.h, mm0 + ch)
                    for c in range(8):
                        cb.inc(s.activation(out=sqf[c][:], in_=xt[c][:],
                                            func=AF.Square), 1)
                    s.wait_ge(stt.h, st0 + ch * P_ST + 4)
                    if ch > 0:
                        s.wait_ge(bc.h, bc0 + ch)
                    stt.inc(s.activation(out=SL(4), in_=SL(3), func=AF.Ln,
                                         bias=1e-5), 1)
                    s.wait_ge(stt.h, st0 + ch * P_ST + 5)
                    stt.inc(s.activation(out=SL(5), in_=SL(4),
                                         func=AF.Exp, scale=-0.5), 1)

            @blk.tensor
            def _(t):
                for ch in range(NTC):
                    if ch > 0:
                        t.wait_ge(stt.h, st0 + (ch - 1) * P_ST + 2)
                    t.wait_ge(ld.h, ld0 + (ch + 1) * 128)
                    for c in range(8):
                        t.wait_ge(cb.h, cb0 + ch * 8 + c + 1)
                        onesv = C.ones_b if in_bf else C.ones_f
                        t.matmul(ps_sum[:], onesv[:, 0:1], xt[c][:],
                                 start=(c == 0), stop=(c == 7))
                        i_ = t.matmul(ps_sq[:], onesv[:, 0:1], sqf[c][:],
                                      start=(c == 0), stop=(c == 7))
                    mm.inc(i_, 1)
                    t.wait_ge(stt.h, st0 + ch * P_ST + 12)
                    if ch > 0:
                        t.wait_ge(ap.h, ap0 + ch * P_AP)
                    t.matmul(ps_rst[:], C.ones_f[0:1, :], SL(11),
                             start=True, stop=True)
                    bc.inc(t.matmul(ps_mrs[:], C.ones_f[0:1, :], SL(6),
                                    start=True, stop=True), 1)
                    if xr_ap is not None:
                        t.wait_ge(ap.h, ap0 + ch * P_AP + 3)
                        if ch > 0:
                            t.wait_ge(rv.h, rv0 + ch)
                        rs.inc(t.matmul(ps_r[:], C.ones_f[0:2, 0:1], x0f[:],
                                        start=True, stop=True), 1)

            @blk.vector
            def _(v):
                for ch in range(NTC):
                    if ch > 0:
                        v.wait_ge(so.h, so0 + ch * P_SO * 16)
                    v.wait_ge(mm.h, mm0 + ch + 1)
                    if ch > 0:
                        v.wait_ge(stt.h, st0 + ch * P_ST)
                    stt.inc(v.tensor_scalar_mul(out=SL(0), in0=ps_sum[:],
                                                scalar1=1.0 / D), 1)
                    stt.inc(v.tensor_scalar_mul(out=SL(1), in0=ps_sq[:],
                                                scalar1=1.0 / D), 1)
                    v.wait_ge(stt.h, st0 + ch * P_ST + 2)
                    stt.inc(v.tensor_tensor(out=SL(2), in0=SL(0), in1=SL(0),
                                            op=ALU.mult), 1)
                    v.wait_ge(stt.h, st0 + ch * P_ST + 3)
                    stt.inc(v.tensor_tensor(out=SL(3), in0=SL(1), in1=SL(2),
                                            op=ALU.subtract), 1)
                    v.wait_ge(stt.h, st0 + ch * P_ST + 6)
                    stt.inc(v.tensor_scalar_add(out=SL(7), in0=SL(3),
                                                scalar1=1e-5), 1)
                    v.wait_ge(stt.h, st0 + ch * P_ST + 7)
                    stt.inc(v.tensor_tensor(out=SL(8), in0=SL(5), in1=SL(5),
                                            op=ALU.mult), 1)
                    v.wait_ge(stt.h, st0 + ch * P_ST + 8)
                    stt.inc(v.tensor_tensor(out=SL(9), in0=SL(7), in1=SL(8),
                                            op=ALU.mult), 1)
                    v.wait_ge(stt.h, st0 + ch * P_ST + 9)
                    stt.inc(v.tensor_scalar(out=SL(10), in0=SL(9), scalar1=-0.5,
                                            scalar2=1.5, op0=ALU.mult,
                                            op1=ALU.add), 1)
                    v.wait_ge(stt.h, st0 + ch * P_ST + 10)
                    stt.inc(v.tensor_tensor(out=SL(11), in0=SL(5), in1=SL(10),
                                            op=ALU.mult), 1)
                    v.wait_ge(stt.h, st0 + ch * P_ST + 11)
                    stt.inc(v.tensor_tensor(out=SL(6), in0=SL(0), in1=SL(11),
                                            op=ALU.mult), 1)
                    v.wait_ge(bc.h, bc0 + ch + 1)
                    for c in range(8):
                        ap.inc(v.tensor_tensor(out=xf2[c][:], in0=xt[c][:],
                                               in1=ps_rst[:], op=ALU.mult), 1)
                        if c == 0 and xr_ap is not None:
                            if ch > 0:
                                v.wait_ge(rs.h, rs0 + ch)
                            v.wait_ge(ap.h, ap0 + ch * P_AP + 1)
                            ap.inc(v.tensor_tensor(out=x0f[:], in0=xf2[0][0:2, :],
                                                   in1=ps_mrs[0:2, :],
                                                   op=ALU.subtract), 1)
                        v.wait_ge(ap.h, ap.n)
                        if gamma_ap is not None:
                            ap.inc(v.tensor_tensor(out=xt[c][:], in0=xf2[c][:],
                                                   in1=ps_mrs[:],
                                                   op=ALU.subtract), 1)
                            v.wait_ge(ap.h, ap.n)
                            ap.inc(v.tensor_scalar(out=xnb[c][:], in0=xt[c][:],
                                                   scalar1=gbc[:, c:c + 1],
                                                   scalar2=gbc[:, 8 + c:9 + c],
                                                   op0=ALU.mult, op1=ALU.add), 1)
                        else:
                            ap.inc(v.tensor_tensor(out=xnb[c][:], in0=xf2[c][:],
                                                   in1=ps_mrs[:],
                                                   op=ALU.subtract), 1)
                    if xr_ap is not None:
                        v.wait_ge(rs.h, rs0 + ch + 1)
                        rv.inc(v.tensor_copy(out=rrow[:], in_=ps_r[:]), 1)

            @blk.gpsimd
            def _(g):
                for ch in range(NTC):
                    g.wait_ge(ap.h, ap0 + (ch + 1) * P_AP)
                    for c in range(8):
                        so.inc(g.dma_start(out=xn_ap[c * 128:(c + 1) * 128,
                                                     ch * TC:(ch + 1) * TC],
                                           in_=xnb[c][:]), 16)
                    if xr_ap is not None:
                        g.wait_ge(rv.h, rv0 + ch + 1)
                        so.inc(g.dma_start(out=xr_ap[:, ch * TC:(ch + 1) * TC],
                                           in_=rrow[:]), 16)
                g.wait_ge(so.h, so.n)



def emit_attn0(C, tag, xn_ap, wq_ap, wk_ap, wv_ap, wo_ap, battn_ap, bproj_ap,
               s_ap, wt_trans=False, rope=False):
    nc = C.nc
    QB = 4
    with contextlib.ExitStack() as st:
        E = st.enter_context
        a_ld = C.sem("a_ld")
        a_lb = C.sem("a_lb")
        a_w = C.sem("a_w")
        a_wq = [C.sem("a_w0"), C.sem("a_w1")]
        a_tp = C.sem("a_tp")
        a_tr = C.sem("a_tr")
        a_te = [C.sem("a_te0"), C.sem("a_te1")]
        a_mm = C.sem("a_mm")
        a_ev = C.sem("a_ev")
        a_ro = C.sem("a_ro")
        d_sc = C.sem("d_sc")
        d_ex = C.sem("d_ex")
        d_pr = C.sem("d_pr")
        d_ly = C.sem("d_ly")
        d_no = C.sem("d_no")
        e_mm = C.sem("e_mm")
        e_ev = C.sem("e_ev")
        e_so = [C.sem("e_so0"), C.sem("e_so1")]
        lb0_, w0_, tp0_ = a_lb.n, a_w.n, a_tp.n

        at = [E(nc.sbuf_tensor(f"{tag}_a{c}", [128, T], BF16)) for c in range(8)]
        qt = [E(nc.sbuf_tensor(f"{tag}_q{c}", [128, T], BF16)) for c in range(8)]
        kt8 = [E(nc.sbuf_tensor(f"{tag}_k{c}", [128, T], BF16)) for c in range(8)]
        vt = [E(nc.sbuf_tensor(f"{tag}_v{i}", [128, D], BF16)) for i in range(16)]
        wv = [E(nc.sbuf_tensor(f"{tag}_wv{c}", [128, D], BF16)) for c in range(8)]
        wqk = [E(nc.sbuf_tensor(f"{tag}_wqk{i}", [128, D], BF16)) for i in range(2)]
        wp = [E(nc.sbuf_tensor(f"{tag}_wp{i}", [128, D], BF16)) for i in range(2)]
        if wt_trans:
            wstg = [E(nc.sbuf_tensor(f"{tag}_ws{i}", [128, D], BF16))
                    for i in range(2)]
        if battn_ap is not None or bproj_ap is not None:
            brawt = E(nc.sbuf_tensor(f"{tag}_braw", [32, 128], F32))
            bcols = E(nc.sbuf_tensor(f"{tag}_bcols", [128, 32], F32))
            vbrow = E(nc.sbuf_tensor(f"{tag}_vbrow", [1, D], BF16))
        if rope:
            ctab = E(nc.sbuf_tensor(f"{tag}_ctab", [128, T], BF16))
            stab = E(nc.sbuf_tensor(f"{tag}_stab", [128, T], BF16))
            rsc = [E(nc.sbuf_tensor(f"{tag}_rsc{i}", [32, T], BF16))
                   for i in range(4)]
        praw = [E(nc.sbuf_tensor(f"{tag}_pr{i}", [128, 512], BF16)) for i in range(2)]
        pmsk = [E(nc.sbuf_tensor(f"{tag}_pm{i}", [128, 512], BF16)) for i in range(2)]
        rsb = [E(nc.sbuf_tensor(f"{tag}_rsb{i}", [64, 512], F32)) for i in range(2)]
        sst = [E(nc.sbuf_tensor(f"{tag}_sst{i}", [128, 512], F32)) for i in range(2)]
        ps_p = [E(nc.psum_tensor(f"{tag}_psp{i}", [128, 512], F32)) for i in range(2)]
        ps_sc = [E(nc.psum_tensor(f"{tag}_pss{i}", [128, 512], F32)) for i in range(2)]
        ps_l = [E(nc.psum_tensor(f"{tag}_psl{i}", [64, 512], F32)) for i in range(2)]
        ps_y = [E(nc.psum_tensor(f"{tag}_psy{i}", [64, 512], F32)) for i in range(2)]

        if rope:
            W_ = 256
            with contextlib.ExitStack() as st2:
                E2 = st2.enter_context
                thc = E2(nc.sbuf_tensor(f"{tag}_thc", [64, 4], F32))
                sa_ = E2(nc.sbuf_tensor(f"{tag}_sa", [64, W_], F32))
                sb_ = E2(nc.sbuf_tensor(f"{tag}_sb", [64, W_], F32))
                sc_ = E2(nc.sbuf_tensor(f"{tag}_sc", [64, W_], F32))
                sd_ = E2(nc.sbuf_tensor(f"{tag}_sd", [64, W_], F32))
                si_ = E2(nc.sbuf_tensor(f"{tag}_si", [64, W_], I32))
                with nc.Block() as blk:
                    @blk.gpsimd
                    def _(g):
                        a_ro.inc(g.iota(thc[:, 0:1], pattern=[[1, 1]], base=0,
                                        channel_multiplier=1,
                                        allow_small_or_imprecise_dtypes=True), 1)

                    @blk.vector
                    def _(v):
                        def ve(instr):
                            a_ro.inc(instr, 1)
                            v.wait_ge(a_ro.h, a_ro.n)
                        v.wait_ge(a_ro.h, a_ro.n)
                        ve(v.tensor_single_scalar(out=thc[:, 1:2],
                                                  in_=thc[:, 0:1], scalar=32.0,
                                                  op=ALU.is_ge))
                        ve(v.tensor_scalar_mul(out=thc[:, 2:3], in0=thc[:, 1:2],
                                               scalar1=32.0))
                        ve(v.tensor_tensor(out=thc[:, 3:4], in0=thc[:, 0:1],
                                           in1=thc[:, 2:3], op=ALU.subtract))

                    @blk.scalar
                    def _(s):
                        import math
                        s.wait_ge(a_ro.h, a_ro.n)
                        a_ro.inc(s.activation(out=thc[:, 0:1], in_=thc[:, 3:4],
                                              func=AF.Exp,
                                              scale=-math.log(10000.0) / 32.0), 1)
                for tch in range(T // W_):
                    import math
                    with nc.Block() as blk:
                        @blk.gpsimd
                        def _(g):
                            a_ro.inc(g.iota(sa_[:], pattern=[[1, W_]],
                                            base=tch * W_, channel_multiplier=0,
                                            allow_small_or_imprecise_dtypes=True),
                                     1)

                        @blk.vector
                        def _(v):
                            def ve(instr):
                                a_ro.inc(instr, 1)
                                v.wait_ge(a_ro.h, a_ro.n)
                            v.wait_ge(a_ro.h, a_ro.n)
                            ve(v.tensor_scalar(out=sb_[:], in0=sa_[:],
                                               scalar1=thc[:, 0:1], scalar2=None,
                                               op0=ALU.mult))
                            ve(v.tensor_scalar_mul(out=sa_[:], in0=sb_[:],
                                                   scalar1=1.0 / (2 * math.pi)))

                    with nc.Block() as blk:
                        @blk.scalar
                        def _(s):
                            a_ro.inc(s.copy(out=si_[:], in_=sa_[:]), 1)
                            s.wait_ge(a_ro.h, a_ro.n)
                            a_ro.inc(s.copy(out=sb_[:], in_=si_[:]), 1)

                    with nc.Block() as blk:
                        @blk.vector
                        def _(v):
                            def ve(instr):
                                a_ro.inc(instr, 1)
                                v.wait_ge(a_ro.h, a_ro.n)
                            ve(v.tensor_tensor(out=sc_[:], in0=sa_[:],
                                               in1=sb_[:], op=ALU.subtract))
                            ve(v.tensor_single_scalar(out=sb_[:], in_=sc_[:],
                                                      scalar=0.5, op=ALU.is_ge))
                            ve(v.tensor_tensor(out=sd_[:], in0=sc_[:],
                                               in1=sb_[:], op=ALU.subtract))

                    with nc.Block() as blk:
                        @blk.scalar
                        def _(s):
                            import math
                            a_ro.inc(s.activation(
                                out=stab[0:64, tch * W_:(tch + 1) * W_],
                                in_=sd_[:], func=AF.Sin, scale=2 * math.pi), 1)

                    with nc.Block() as blk:
                        @blk.vector
                        def _(v):
                            a_ro.inc(v.tensor_scalar_add(out=sb_[:], in0=sa_[:],
                                                         scalar1=0.25), 1)

                    with nc.Block() as blk:
                        @blk.scalar
                        def _(s):
                            a_ro.inc(s.copy(out=si_[:], in_=sb_[:]), 1)
                            s.wait_ge(a_ro.h, a_ro.n)
                            a_ro.inc(s.copy(out=sc_[:], in_=si_[:]), 1)

                    with nc.Block() as blk:
                        @blk.vector
                        def _(v):
                            def ve(instr):
                                a_ro.inc(instr, 1)
                                v.wait_ge(a_ro.h, a_ro.n)
                            ve(v.tensor_tensor(out=sd_[:], in0=sb_[:],
                                               in1=sc_[:], op=ALU.subtract))
                            ve(v.tensor_single_scalar(out=sb_[:], in_=sd_[:],
                                                      scalar=0.5, op=ALU.is_ge))
                            ve(v.tensor_tensor(out=sa_[:], in0=sd_[:],
                                               in1=sb_[:], op=ALU.subtract))

                    with nc.Block() as blk:
                        @blk.scalar
                        def _(s):
                            import math
                            a_ro.inc(s.activation(
                                out=ctab[0:64, tch * W_:(tch + 1) * W_],
                                in_=sa_[:], func=AF.Sin, scale=2 * math.pi), 1)

        if rope:
            with nc.Block() as blk:
                @blk.vector
                def _(v):
                    def ve(instr):
                        a_ro.inc(instr, 1)
                        v.wait_ge(a_ro.h, a_ro.n)
                    ve(v.tensor_copy(out=ctab[64:128, :], in_=ctab[0:64, :]))
                    ve(v.tensor_copy(out=stab[64:96, :], in_=stab[32:64, :]))
                    ve(v.tensor_copy(out=stab[96:128, :], in_=stab[0:32, :]))
                    ve(v.tensor_copy(out=stab[32:64, :], in_=stab[0:32, :]))
                    ve(v.tensor_copy(out=stab[0:32, :], in_=stab[64:96, :]))

        CT = []
        for h in range(NH):
            for qb in range(QB):
                idx = h * QB + qb
                for kt in range(4 * (qb + 1)):
                    CT.append((idx, h, qb, kt, kt >= qb * 4))

        def wsrc(j):
            p_, f8 = divmod(j, 8)
            wap = wq_ap if p_ == 0 else wk_ap
            if wt_trans:
                return wap[f8 * 128:(f8 + 1) * 128, :]
            src = wap[:, f8 * 128:(f8 + 1) * 128]
            return src.rearrange("(c p) f -> p c f", p=128)

        for s_ in range(SPC):
            t0 = s_ * T
            ld0, mm0, ev0 = a_ld.n, a_mm.n, a_ev.n
            sc0, ex0, pr0, ly0, no0 = d_sc.n, d_ex.n, d_pr.n, d_ly.n, d_no.n

            sc_at, pr_at = {}, {}
            scn, prn = sc0, pr0
            for (idx, h, qb, kt, diag) in CT:
                scn += 1
                sc_at[(idx, kt)] = scn
                prn += 1
                pr_at[(idx, kt)] = prn
            wq_ready = []

            if wt_trans and s_ == 0:
                trW0 = a_tr.n
                teW0 = [a_te[0].n, a_te[1].n]
                wv_ready = []
                with nc.Block() as blk:
                    @blk.gpsimd
                    def _(g):
                        for cb in range(8):
                            if cb >= 2:
                                g.wait_ge(a_tr.h, trW0 + (cb - 1) * 8)
                            wv_ready.append(a_wq[cb % 2].inc(g.dma_start(
                                out=wstg[cb % 2][:],
                                in_=wv_ap[cb * 128:(cb + 1) * 128, :]), 16))

                    @blk.tensor
                    def _(t):
                        for cb in range(8):
                            t.wait_ge(a_wq[cb % 2].h, wv_ready[cb])
                            for c in range(8):
                                k = cb * 8 + c
                                if k >= 2:
                                    t.wait_ge(a_te[k % 2].h,
                                              teW0[k % 2] + k // 2)
                                a_tr.inc(t.transpose(
                                    ps_sc[k % 2][:].bitcast(BF16)[:, 0:128],
                                    wstg[cb % 2][:, c * 128:(c + 1) * 128],
                                    C.ident_b[:]), 1)

                    @blk.vector
                    def _(v):
                        for cb in range(8):
                            for c in range(8):
                                k = cb * 8 + c
                                v.wait_ge(a_tr.h, trW0 + k + 1)
                                a_te[k % 2].inc(v.tensor_copy(
                                    out=wv[c][:, cb * 128:(cb + 1) * 128],
                                    in_=ps_sc[k % 2][:].bitcast(BF16)[:, 0:128]),
                                    1)

            tr0 = a_tr.n
            te0 = [a_te[0].n, a_te[1].n]

            with nc.Block() as blk:
                @blk.sync
                def _(sy):
                    for c in range(8):
                        a_ld.inc(sy.dma_start(
                            out=at[c][:],
                            in_=xn_ap[c * 128:(c + 1) * 128, t0:t0 + T]), 16)
                    if s_ == 0 and battn_ap is not None:
                        a_lb.inc(sy.dma_start(
                            out=brawt[0:24, :],
                            in_=battn_ap.rearrange("(a b) -> a b", b=128)), 16)
                        a_lb.inc(sy.dma_start(
                            out=brawt[24:32, :],
                            in_=bproj_ap.rearrange("(a b) -> a b", b=128)), 16)

                @blk.gpsimd
                def _(g):
                    if s_ == 0 and not wt_trans:
                        for c in range(8):
                            a_w.inc(g.dma_start(
                                out=wv[c][:],
                                in_=wv_ap[c * 128:(c + 1) * 128, :]), 16)
                        if battn_ap is not None:
                            a_w.inc(g.dma_start(
                                out=vbrow[:],
                                in_=battn_ap[2048:3072].unsqueeze(0)), 16)
                    for j in range(16):
                        if j >= 2:
                            g.wait_ge(a_mm.h, mm0 + (j - 1) * 4)
                        if wt_trans:
                            dst = wstg[j % 2][:]
                        else:
                            dst = wqk[j % 2][:].rearrange("p (c f) -> p c f", f=128)
                        wq_ready.append(a_wq[j % 2].inc(
                            g.dma_start(out=dst, in_=wsrc(j)), 16))

                @blk.tensor
                def _(t):
                    if s_ == 0 and battn_ap is not None:
                        t.wait_ge(a_lb.h, lb0_ + 32)
                        a_tp.inc(t.transpose(ps_p[0][:, 0:32], brawt[:],
                                             C.ident_f[0:32, 0:32]), 1)
                        t.wait_ge(a_tp.h, a_tp.n + 1)
                    t.wait_ge(a_ld.h, ld0 + 8 * 16)
                    for j in range(16):
                        t.wait_ge(a_wq[j % 2].h, wq_ready[j])
                        if wt_trans:
                            if j >= 1:
                                t.wait_ge(a_te[(j - 1) % 2].h,
                                          te0[(j - 1) % 2] + ((j - 1) // 2 + 1) * 8)
                            for c in range(8):
                                if c >= 2:
                                    t.wait_ge(a_te[j % 2].h,
                                              te0[j % 2] + (j // 2) * 8 + c - 1)
                                a_tr.inc(t.transpose(
                                    ps_sc[c % 2][:].bitcast(BF16)[:, 0:128],
                                    wstg[j % 2][:, c * 128:(c + 1) * 128],
                                    C.ident_b[:]), 1)
                            t.wait_ge(a_te[j % 2].h, te0[j % 2] + (j // 2 + 1) * 8)
                        for tch in range(4):
                            g_ = j * 4 + tch
                            if g_ >= 2:
                                t.wait_ge(a_ev.h, ev0 + g_ - 1)
                            pb = ps_p[g_ % 2]
                            for c in range(8):
                                i_ = t.matmul(pb[:],
                                              wqk[j % 2][:, c * 128:(c + 1) * 128],
                                              at[c][:, tch * 512:(tch + 1) * 512],
                                              start=(c == 0), stop=(c == 7))
                            a_mm.inc(i_, 1)
                    if s_ == 0 and not wt_trans:
                        t.wait_ge(a_w.h, w0_ + (144 if battn_ap is not None else 128))
                    for i16 in range(16):
                        for fh in range(2):
                            g_ = 64 + i16 * 2 + fh
                            t.wait_ge(a_ev.h, ev0 + g_ - 1)
                            pb = ps_p[g_ % 2]
                            for c in range(8):
                                last = (c == 7 and battn_ap is None)
                                i_ = t.matmul(pb[:],
                                              at[c][:, i16 * 128:(i16 + 1) * 128],
                                              wv[c][:, fh * 512:(fh + 1) * 512],
                                              start=(c == 0), stop=last)
                            if battn_ap is not None:
                                i_ = t.matmul(pb[:], C.ones_b[0:1, :],
                                              vbrow[:, fh * 512:(fh + 1) * 512],
                                              start=False, stop=True)
                            a_mm.inc(i_, 1)

                @blk.vector
                def _(v):
                    if s_ == 0 and battn_ap is not None:
                        v.wait_ge(a_tp.h, tp0_ + 1)
                        a_tp.inc(v.tensor_copy(out=bcols[:], in_=ps_p[0][:, 0:32]),
                                 1)
                    for j in range(16):
                        if wt_trans:
                            for c in range(8):
                                v.wait_ge(a_tr.h, tr0 + j * 8 + c + 1)
                                a_te[j % 2].inc(v.tensor_copy(
                                    out=wqk[j % 2][:, c * 128:(c + 1) * 128],
                                    in_=ps_sc[c % 2][:].bitcast(BF16)[:, 0:128]),
                                    1)
                        p_, f8 = divmod(j, 8)
                        dst = (qt if p_ == 0 else kt8)[f8]
                        for tch in range(4):
                            g_ = j * 4 + tch
                            v.wait_ge(a_mm.h, mm0 + g_ + 1)
                            if battn_ap is not None:
                                bcol = bcols[:, j:j + 1]
                                a_ev.inc(v.tensor_scalar(
                                    out=dst[:, tch * 512:(tch + 1) * 512],
                                    in0=ps_p[g_ % 2][:],
                                    scalar1=bcol, scalar2=None, op0=ALU.add), 1)
                            else:
                                a_ev.inc(v.tensor_copy(
                                    out=dst[:, tch * 512:(tch + 1) * 512],
                                    in_=ps_p[g_ % 2][:]), 1)
                    for i16 in range(16):
                        for fh in range(2):
                            g_ = 64 + i16 * 2 + fh
                            v.wait_ge(a_mm.h, mm0 + g_ + 1)
                            a_ev.inc(v.tensor_copy(
                                out=vt[i16][:, fh * 512:(fh + 1) * 512],
                                in_=ps_p[g_ % 2][:]), 1)

            if rope:
                with nc.Block() as blk:
                    @blk.vector
                    def _(v):
                        def ve(instr):
                            a_ro.inc(instr, 1)
                            v.wait_ge(a_ro.h, a_ro.n)
                        for tiles in (qt, kt8):
                            for c in range(8):
                                for hb in (0, 64):
                                    lo = tiles[c][hb:hb + 32, :]
                                    hi = tiles[c][hb + 32:hb + 64, :]
                                    ve(v.tensor_tensor(out=rsc[0][:], in0=lo,
                                                       in1=ctab[hb:hb + 32, :],
                                                       op=ALU.mult))
                                    ve(v.tensor_tensor(out=rsc[1][:], in0=lo,
                                                       in1=stab[hb:hb + 32, :],
                                                       op=ALU.mult))
                                    ve(v.tensor_tensor(
                                        out=rsc[2][:], in0=hi,
                                        in1=stab[hb + 32:hb + 64, :],
                                        op=ALU.mult))
                                    ve(v.tensor_tensor(
                                        out=rsc[3][:], in0=hi,
                                        in1=ctab[hb + 32:hb + 64, :],
                                        op=ALU.mult))
                                    ve(v.tensor_tensor(out=lo, in0=rsc[0][:],
                                                       in1=rsc[2][:],
                                                       op=ALU.subtract))
                                    ve(v.tensor_tensor(out=hi, in0=rsc[3][:],
                                                       in1=rsc[1][:],
                                                       op=ALU.add))

            with nc.Block() as blk:
                @blk.tensor
                def _(t):
                    for h in range(NH):
                        hb = (h % 2) * 64
                        kfm = kt8[h // 2]
                        qfm = qt[h // 2]
                        for qb in range(QB):
                            idx = h * QB + qb
                            nkt = 4 * (qb + 1)
                            if idx >= 2:
                                t.wait_ge(d_no.h, no0 + 2 * (idx - 1))
                            for kt in range(nkt):
                                if kt >= 2:
                                    t.wait_ge(d_pr.h, pr_at[(idx, kt - 2)])
                                d_sc.inc(t.matmul(
                                    ps_sc[kt % 2][:],
                                    kfm[hb:hb + 64, kt * 128:(kt + 1) * 128],
                                    qfm[hb:hb + 64, qb * 512:(qb + 1) * 512],
                                    start=True, stop=True), 1)
                                if kt >= 1:
                                    kk = kt - 1
                                    t.wait_ge(d_pr.h, pr_at[(idx, kk)])
                                    pt_ = (pmsk if kk >= qb * 4 else praw)[kk % 2]
                                    t.matmul(ps_l[idx % 2][:], C.ones_b[:, 0:64],
                                             pt_[:], start=(kk == 0), stop=False)
                                    t.matmul(ps_y[idx % 2][:],
                                             vt[kk][:, h * 64:(h + 1) * 64],
                                             pt_[:], start=(kk == 0), stop=False)
                            kk = nkt - 1
                            t.wait_ge(d_pr.h, pr_at[(idx, kk)])
                            pt_ = (pmsk if kk >= qb * 4 else praw)[kk % 2]
                            t.matmul(ps_l[idx % 2][:], C.ones_b[:, 0:64], pt_[:],
                                     start=(kk == 0), stop=True)
                            d_ly.inc(t.matmul(ps_y[idx % 2][:],
                                              vt[kk][:, h * 64:(h + 1) * 64],
                                              pt_[:], start=(kk == 0), stop=True),
                                     1)

                @blk.scalar
                def _(s):
                    for (idx, h, qb, kt, diag) in CT:
                        s.wait_ge(d_sc.h, sc_at[(idx, kt)])
                        tgt = d_ex if diag else d_pr
                        tgt.inc(s.activation(out=praw[kt % 2][:],
                                             in_=ps_sc[kt % 2][:], func=AF.Exp,
                                             scale=0.125, bias=-EXP_SHIFT), 1)

                @blk.vector
                def _(v):
                    exn = ex0
                    for (idx, h, qb, kt, diag) in CT:
                        if diag:
                            exn += 1
                            v.wait_ge(d_ex.h, exn)
                            d_pr.inc(v.tensor_tensor(out=pmsk[kt % 2][:],
                                                     in0=praw[kt % 2][:],
                                                     in1=C.cmask[kt - qb * 4][:],
                                                     op=ALU.mult), 1)
                        if kt == 4 * (qb + 1) - 1:
                            hb = (h % 2) * 64
                            v.wait_ge(d_ly.h, ly0 + idx + 1)
                            d_no.inc(v.reciprocal(out=rsb[idx % 2][:],
                                                  in_=ps_l[idx % 2][:]), 1)
                            v.wait_ge(d_no.h, d_no.n)
                            d_no.inc(v.tensor_tensor(
                                out=at[h // 2][hb:hb + 64,
                                               qb * 512:(qb + 1) * 512],
                                in0=ps_y[idx % 2][:], in1=rsb[idx % 2][:],
                                op=ALU.mult), 1)

            mmE0, evE0 = e_mm.n, e_ev.n
            soE0 = [e_so[0].n, e_so[1].n]
            trE0 = a_tr.n
            teE0 = [a_te[0].n, a_te[1].n]
            wp_ready = []
            with nc.Block() as blk:
                @blk.gpsimd
                def _(g):
                    def wpdma(o8):
                        if wt_trans:
                            dstw = wstg[o8 % 2][:]
                            src = wo_ap[o8 * 128:(o8 + 1) * 128, :]
                        else:
                            src = wo_ap[:, o8 * 128:(o8 + 1) * 128]
                            src = src.rearrange("(c p) f -> p c f", p=128)
                            dstw = wp[o8 % 2][:].rearrange("p (c f) -> p c f",
                                                           f=128)
                        wp_ready.append(a_wq[o8 % 2].inc(
                            g.dma_start(out=dstw, in_=src), 16))
                    wpdma(0)
                    wpdma(1)
                    for o8 in range(8):
                        for tch in range(4):
                            j = o8 * 4 + tch
                            g.wait_ge(e_ev.h, evE0 + j + 1)
                            e_so[j % 2].inc(g.dma_start(
                                out=s_ap[o8 * 128:(o8 + 1) * 128,
                                         t0 + tch * 512:t0 + (tch + 1) * 512],
                                in_=sst[j % 2][:]), 16)
                        if o8 + 2 < 8:
                            g.wait_ge(e_mm.h, mmE0 + (o8 + 1) * 4)
                            wpdma(o8 + 2)

                @blk.tensor
                def _(t):
                    for o8 in range(8):
                        t.wait_ge(a_wq[o8 % 2].h, wp_ready[o8])
                        if wt_trans:
                            if o8 >= 1:
                                t.wait_ge(a_te[(o8 - 1) % 2].h,
                                          teE0[(o8 - 1) % 2] +
                                          ((o8 - 1) // 2 + 1) * 8)
                            for c in range(8):
                                if c >= 2:
                                    t.wait_ge(a_te[o8 % 2].h,
                                              teE0[o8 % 2] + (o8 // 2) * 8 + c - 1)
                                a_tr.inc(t.transpose(
                                    ps_sc[c % 2][:].bitcast(BF16)[:, 0:128],
                                    wstg[o8 % 2][:, c * 128:(c + 1) * 128],
                                    C.ident_b[:]), 1)
                            t.wait_ge(a_te[o8 % 2].h,
                                      teE0[o8 % 2] + (o8 // 2 + 1) * 8)
                        for tch in range(4):
                            j = o8 * 4 + tch
                            if j >= 2:
                                t.wait_ge(e_ev.h, evE0 + j - 1)
                            pb = ps_p[j % 2]
                            for c in range(8):
                                i_ = t.matmul(pb[:],
                                              wp[o8 % 2][:, c * 128:(c + 1) * 128],
                                              at[c][:, tch * 512:(tch + 1) * 512],
                                              start=(c == 0), stop=(c == 7))
                            e_mm.inc(i_, 1)

                @blk.vector
                def _(v):
                    for o8 in range(8):
                        if wt_trans:
                            for c in range(8):
                                v.wait_ge(a_tr.h, trE0 + o8 * 8 + c + 1)
                                a_te[o8 % 2].inc(v.tensor_copy(
                                    out=wp[o8 % 2][:, c * 128:(c + 1) * 128],
                                    in_=ps_sc[c % 2][:].bitcast(BF16)[:, 0:128]),
                                    1)
                        for tch in range(4):
                            j = o8 * 4 + tch
                            v.wait_ge(e_mm.h, mmE0 + j + 1)
                            if j >= 2:
                                v.wait_ge(e_so[j % 2].h, soE0[j % 2] +
                                          (j // 2) * 16)
                            if bproj_ap is not None:
                                bcol = bcols[:, 24 + o8:24 + o8 + 1]
                                e_ev.inc(v.tensor_scalar(out=sst[j % 2][:],
                                                         in0=ps_p[j % 2][:],
                                                         scalar1=bcol,
                                                         scalar2=None,
                                                         op0=ALU.add), 1)
                            else:
                                e_ev.inc(v.tensor_copy(out=sst[j % 2][:],
                                                       in_=ps_p[j % 2][:]), 1)

            with nc.Block() as blk:
                @blk.gpsimd
                def _(g):
                    g.wait_ge(e_so[0].h, e_so[0].n)
                    g.wait_ge(e_so[1].h, e_so[1].n)



def emit_moe(C, tag, xn_ap, w1_ap, b1_ap, w2_ap, b2_ap,
             x_ap, s_ap, xout_ap, act, emit_routing):
    nc = C.nc
    TCH = 2048
    NCH2 = N // TCH
    NT = TCH // 512
    with contextlib.ExitStack() as st:
        E = st.enter_context
        m_ld = C.sem("m_ld")
        m_b = C.sem("m_b")
        m_b2 = C.sem("m_b2")
        m_bt = C.sem("m_bt")
        m_w1 = [C.sem("m_w10"), C.sem("m_w11")]
        m_w2 = C.sem("m_w2")
        m_wb = C.sem("m_wb")
        m_wv = C.sem("m_wv")
        m_h1 = C.sem("m_h1")
        m_hs = C.sem("m_hs")
        m_hv = C.sem("m_hv")
        m_p2 = C.sem("m_p2")
        m_ac = C.sem("m_ac")
        m_xl = [C.sem("m_xl0"), C.sem("m_xl1")]
        m_cg = C.sem("m_cg")
        m_cv = C.sem("m_cv")
        m_so = [C.sem("m_so0"), C.sem("m_so1")]

        w8d = nc.dram_tensor(f"{tag}_w8d", [8, N], BF16, kind="Internal").ap()
        emit_routing(C, w8d)

        xnres = [E(nc.sbuf_tensor(f"{tag}_xn{c}", [128, TCH], BF16))
                 for c in range(8)]
        Ht = [E(nc.sbuf_tensor(f"{tag}_H{f}", [128, TCH], BF16)) for f in range(16)]
        accb = [E(nc.sbuf_tensor(f"{tag}_ac{c}", [128, TCH], BF16))
                for c in range(8)]
        w2res = [E(nc.sbuf_tensor(f"{tag}_w2{f}", [128, D], BF16))
                 for f in range(16)]
        w1s = [E(nc.sbuf_tensor(f"{tag}_w1{i}", [128, D], BF16)) for i in range(2)]
        b1raw = [E(nc.sbuf_tensor(f"{tag}_b1r{i}", [128, 128], F32))
                 for i in range(2)]
        b1cols = [E(nc.sbuf_tensor(f"{tag}_b1c{i}", [128, 128], F32))
                  for i in range(2)]
        b2rowe = E(nc.sbuf_tensor(f"{tag}_b2row", [1, 1024], BF16))
        w8row = E(nc.sbuf_tensor(f"{tag}_w8row", [1, TCH], BF16))
        wbc = E(nc.sbuf_tensor(f"{tag}_wbc", [128, TCH], BF16))
        hs = [E(nc.sbuf_tensor(f"{tag}_hs{i}", [128, 512], F32)) for i in range(2)]
        xtl = [E(nc.sbuf_tensor(f"{tag}_xtl{i}", [128, 512], F32)) for i in range(2)]
        stl = [E(nc.sbuf_tensor(f"{tag}_stl{i}", [128, 512], F32)) for i in range(2)]
        gtl = [E(nc.sbuf_tensor(f"{tag}_gtl{i}", [128, 512], BF16)) for i in range(2)]
        g2l = [E(nc.sbuf_tensor(f"{tag}_g2l{i}", [128, 512], F32)) for i in range(2)]
        otl = [E(nc.sbuf_tensor(f"{tag}_otl{i}", [128, 512], F32)) for i in range(2)]
        ps_a = [E(nc.psum_tensor(f"{tag}_psa{i}", [128, 512], F32)) for i in range(2)]
        ps_b = [E(nc.psum_tensor(f"{tag}_psb{i}", [128, 512], F32)) for i in range(2)]
        ps_w = E(nc.psum_tensor(f"{tag}_psw", [128, 512], F32))

        AFACT = AF.Relu if act == "relu" else AF.Silu

        with nc.Block() as blk:
            @blk.sync
            def _(sy):
                for i in range(2):
                    m_b.inc(sy.dma_start(
                        out=b1raw[i][:],
                        in_=b1_ap.rearrange("e (f p) -> (e f) p", p=128)
                        [i * 128:(i + 1) * 128, :]), 16)

            @blk.tensor
            def _(t):
                t.wait_ge(m_b.h, m_b.n)
                for i in range(2):
                    m_bt.inc(t.transpose(ps_a[i][:, 0:128], b1raw[i][:],
                                         C.ident_f[:]), 1)

            @blk.vector
            def _(v):
                bt0 = m_bt.n - 2
                for i in range(2):
                    v.wait_ge(m_bt.h, bt0 + i + 1)
                    m_b.inc(v.tensor_copy(out=b1cols[i][:], in_=ps_a[i][:, 0:128]),
                            1)

        for ch2 in range(NCH2):
            tb = ch2 * TCH
            ld0 = m_ld.n
            with nc.Block() as blk:
                @blk.sync
                def _(sy):
                    for c in range(8):
                        m_ld.inc(sy.dma_start(
                            out=xnres[c][:],
                            in_=xn_ap[c * 128:(c + 1) * 128, tb:tb + TCH]), 16)

            for e in range(8):
                wb0, wv0, b0_ = m_wb.n, m_wv.n, m_b.n
                with nc.Block() as blk:
                    @blk.sync
                    def _(sy):
                        m_b.inc(sy.dma_start(out=w8row[:],
                                             in_=w8d[e:e + 1, tb:tb + TCH]), 16)

                    @blk.gpsimd
                    def _(g):
                        m_b2.inc(g.dma_start(out=b2rowe[:],
                                             in_=b2_ap[e:e + 1, :]), 16)

                    @blk.tensor
                    def _(t):
                        t.wait_ge(m_b.h, b0_ + 16)
                        t.wait_ge(m_b2.h, m_b2.n)
                        for q in range(NT):
                            if q >= 1:
                                t.wait_ge(m_wv.h, wv0 + q)
                            m_wb.inc(t.matmul(
                                ps_w[:], C.ones_b[0:1, :],
                                w8row[:, q * 512:(q + 1) * 512],
                                start=True, stop=True), 1)

                    @blk.vector
                    def _(v):
                        for q in range(NT):
                            v.wait_ge(m_wb.h, wb0 + q + 1)
                            m_wv.inc(v.tensor_copy(
                                out=wbc[:, q * 512:(q + 1) * 512], in_=ps_w[:]), 1)

                for half in range(2):
                    fb = half * 2048
                    h10, hs0, hv0 = m_h1.n, m_hs.n, m_hv.n
                    p20, ac0, w20 = m_p2.n, m_ac.n, m_w2.n
                    w1_ready = []
                    with nc.Block() as blk:
                        @blk.gpsimd
                        def _(g):
                            for f in range(16):
                                m_w2.inc(g.dma_start(
                                    out=w2res[f][:],
                                    in_=w2_ap[e, fb + f * 128:fb + (f + 1) * 128,
                                              :]), 16)
                            for f in range(16):
                                if f >= 2:
                                    g.wait_ge(m_h1.h, h10 + (f - 1) * NT)
                                src = w1_ap[e, :, fb + f * 128:fb + (f + 1) * 128]
                                src = src.rearrange("(c p) f -> p c f", p=128)
                                dstw = w1s[f % 2][:].rearrange("p (c f) -> p c f",
                                                               f=128)
                                w1_ready.append(m_w1[f % 2].inc(
                                    g.dma_start(out=dstw, in_=src), 16))

                        @blk.tensor
                        def _(t):
                            t.wait_ge(m_ld.h, ld0 + 128)
                            for f in range(16):
                                t.wait_ge(m_w1[f % 2].h, w1_ready[f])
                                for q in range(NT):
                                    j = f * NT + q
                                    if j >= 2:
                                        t.wait_ge(m_hs.h, hs0 + j - 1)
                                    pb = ps_a[j % 2]
                                    for c in range(8):
                                        i_ = t.matmul(
                                            pb[:],
                                            w1s[f % 2][:, c * 128:(c + 1) * 128],
                                            xnres[c][:, q * 512:(q + 1) * 512],
                                            start=(c == 0), stop=(c == 7))
                                    m_h1.inc(i_, 1)
                            t.wait_ge(m_w2.h, w20 + 256)
                            t.wait_ge(m_hv.h, hv0 + 64)
                            for o8 in range(8):
                                for q in range(NT):
                                    j = o8 * NT + q
                                    if j >= 2:
                                        t.wait_ge(m_ac.h, ac0 + j - 1)
                                    pb = ps_b[j % 2]
                                    for f in range(16):
                                        t.matmul(pb[:],
                                                 w2res[f][:,
                                                          o8 * 128:(o8 + 1) * 128],
                                                 Ht[f][:, q * 512:(q + 1) * 512],
                                                 start=(f == 0), stop=False)
                                    i_ = t.matmul(
                                        pb[:],
                                        b2rowe[0:1, o8 * 128:(o8 + 1) * 128],
                                        w8row[:, q * 512:(q + 1) * 512],
                                        start=False, stop=True)
                                    m_p2.inc(i_, 1)

                        @blk.scalar
                        def _(s):
                            for f in range(16):
                                idx = e * 32 + fb // 128 + f
                                bcol = b1cols[idx // 128][:,
                                                          idx % 128:idx % 128 + 1]
                                for q in range(NT):
                                    j = f * NT + q
                                    s.wait_ge(m_h1.h, h10 + j + 1)
                                    if j >= 2:
                                        s.wait_ge(m_hv.h, hv0 + j - 1)
                                    m_hs.inc(s.activation(out=hs[j % 2][:],
                                                          in_=ps_a[j % 2][:],
                                                          func=AFACT,
                                                          bias=bcol), 1)

                        @blk.vector
                        def _(v):
                            for f in range(16):
                                for q in range(NT):
                                    j = f * NT + q
                                    v.wait_ge(m_hs.h, hs0 + j + 1)
                                    m_hv.inc(v.tensor_tensor(
                                        out=Ht[f][:, q * 512:(q + 1) * 512],
                                        in0=hs[j % 2][:],
                                        in1=wbc[:, q * 512:(q + 1) * 512],
                                        op=ALU.mult), 1)
                            for o8 in range(8):
                                for q in range(NT):
                                    j = o8 * NT + q
                                    v.wait_ge(m_p2.h, p20 + j + 1)
                                    if e == 0 and half == 0:
                                        m_ac.inc(v.tensor_copy(
                                            out=accb[o8][:, q * 512:(q + 1) * 512],
                                            in_=ps_b[j % 2][:]), 1)
                                    else:
                                        m_ac.inc(v.tensor_tensor(
                                            out=accb[o8][:, q * 512:(q + 1) * 512],
                                            in0=accb[o8][:, q * 512:(q + 1) * 512],
                                            in1=ps_b[j % 2][:], op=ALU.add), 1)

            cg0, cv0 = m_cg.n, m_cv.n
            xl0 = [m_xl[0].n, m_xl[1].n]
            so0 = [m_so[0].n, m_so[1].n]
            with nc.Block() as blk:
                @blk.sync
                def _(sy):
                    for c in range(8):
                        for q in range(NT):
                            j = c * NT + q
                            sy.wait_ge(m_cv.h, cv0 + max(0, 2 * (j - 1)))
                            m_xl[j % 2].inc(sy.dma_start(
                                out=xtl[j % 2][:],
                                in_=x_ap[c * 128:(c + 1) * 128,
                                         tb + q * 512:tb + (q + 1) * 512]), 16)
                            m_xl[j % 2].inc(sy.dma_start(
                                out=stl[j % 2][:],
                                in_=s_ap[c * 128:(c + 1) * 128,
                                         tb + q * 512:tb + (q + 1) * 512]), 16)

                @blk.scalar
                def _(s):
                    for c in range(8):
                        for q in range(NT):
                            j = c * NT + q
                            s.wait_ge(m_cv.h, cv0 + max(0, 2 * (j - 2) + 1))
                            m_cg.inc(s.activation(
                                out=gtl[j % 2][:],
                                in_=accb[c][:, q * 512:(q + 1) * 512],
                                func=AF.Silu), 1)

                @blk.vector
                def _(v):
                    for c in range(8):
                        for q in range(NT):
                            j = c * NT + q
                            v.wait_ge(m_xl[j % 2].h, xl0[j % 2] + (j // 2 + 1) * 32)
                            v.wait_ge(m_cg.h, cg0 + j + 1)
                            v.wait_ge(m_so[j % 2].h, so0[j % 2] + (j // 2) * 16)
                            m_cv.inc(v.tensor_tensor(out=g2l[j % 2][:],
                                                     in0=stl[j % 2][:],
                                                     in1=gtl[j % 2][:],
                                                     op=ALU.mult), 1)
                            v.wait_ge(m_cv.h, m_cv.n)
                            m_cv.inc(v.tensor_tensor(out=otl[j % 2][:],
                                                     in0=xtl[j % 2][:],
                                                     in1=g2l[j % 2][:],
                                                     op=ALU.add), 1)

                @blk.gpsimd
                def _(g):
                    for c in range(8):
                        for q in range(NT):
                            j = c * NT + q
                            g.wait_ge(m_cv.h, cv0 + 2 * (j + 1))
                            m_so[j % 2].inc(g.dma_start(
                                out=xout_ap[c * 128:(c + 1) * 128,
                                            tb + q * 512:tb + (q + 1) * 512],
                                in_=otl[j % 2][:]), 16)

        with nc.Block() as blk:
            @blk.gpsimd
            def _(g):
                g.wait_ge(m_so[0].h, m_so[0].n)
                g.wait_ge(m_so[1].h, m_so[1].n)



def mk_routing_l0(xr_ap):
    def emit(C, w8d_ap):
        nc = C.nc
        r_ld = C.sem("r_ld")
        r_c = C.sem("r_c")
        r_so = C.sem("r_so")
        with contextlib.ExitStack() as st:
            E = st.enter_context
            W = 512
            rr = E(nc.sbuf_tensor("r0_rr", [1, W], F32))
            ri = E(nc.sbuf_tensor("r0_ri", [1, W], I32))
            rif = E(nc.sbuf_tensor("r0_rif", [1, W], F32))
            gt = E(nc.sbuf_tensor("r0_gt", [1, W], F32))
            lt = E(nc.sbuf_tensor("r0_lt", [1, W], F32))
            pos = E(nc.sbuf_tensor("r0_pos", [1, W], F32))
            aa = E(nc.sbuf_tensor("r0_aa", [1, W], F32))
            bb = E(nc.sbuf_tensor("r0_bb", [1, W], F32))
            t1 = E(nc.sbuf_tensor("r0_t1", [1, W], F32))
            t2 = E(nc.sbuf_tensor("r0_t2", [1, W], F32))
            trf = E(nc.sbuf_tensor("r0_trf", [1, W], F32))
            hi = E(nc.sbuf_tensor("r0_hi", [1, W], I32))
            hm = E(nc.sbuf_tensor("r0_hm", [1, W], I32))
            hf = E(nc.sbuf_tensor("r0_hf", [1, W], F32))
            m1 = E(nc.sbuf_tensor("r0_m1", [1, W], F32))
            m2 = E(nc.sbuf_tensor("r0_m2", [1, W], F32))
            m3 = E(nc.sbuf_tensor("r0_m3", [1, W], F32))
            we = [E(nc.sbuf_tensor(f"r0_we{e}", [1, W], F32)) for e in range(8)]
            for ch in range(N // W):
                with nc.Block() as blk:
                    @blk.sync
                    def _(sy):
                        r_ld.inc(sy.dma_start(out=rr[:],
                                              in_=xr_ap[:, ch * W:(ch + 1) * W]),
                                 16)

                    @blk.scalar
                    def _(s):
                        s.wait_ge(r_ld.h, r_ld.n)
                        r_c.inc(s.copy(out=ri[:], in_=rr[:]), 1)
                        s.wait_ge(r_c.h, r_c.n)
                        r_c.inc(s.copy(out=rif[:], in_=ri[:]), 1)
                        s.wait_ge(r_c.h, r_c.n)

                with nc.Block() as blk:
                    @blk.vector
                    def _(v):
                        def ve(instr):
                            r_c.inc(instr, 1)
                            v.wait_ge(r_c.h, r_c.n)
                        ve(v.tensor_tensor(out=gt[:], in0=rif[:], in1=rr[:],
                                           op=ALU.is_gt))
                        ve(v.tensor_tensor(out=lt[:], in0=rif[:], in1=rr[:],
                                           op=ALU.is_lt))
                        ve(v.tensor_single_scalar(out=pos[:], in_=rr[:],
                                                  scalar=0.0, op=ALU.is_ge))
                        ve(v.tensor_tensor(out=aa[:], in0=gt[:], in1=pos[:],
                                           op=ALU.mult))
                        ve(v.tensor_tensor(out=bb[:], in0=lt[:], in1=pos[:],
                                           op=ALU.mult))
                        ve(v.tensor_tensor(out=t1[:], in0=rif[:], in1=aa[:],
                                           op=ALU.subtract))
                        ve(v.tensor_tensor(out=t2[:], in0=t1[:], in1=lt[:],
                                           op=ALU.add))
                        ve(v.tensor_tensor(out=trf[:], in0=t2[:], in1=bb[:],
                                           op=ALU.subtract))

                with nc.Block() as blk:
                    @blk.scalar
                    def _(s):
                        r_c.inc(s.copy(out=hi[:], in_=trf[:]), 1)
                        s.wait_ge(r_c.h, r_c.n)

                with nc.Block() as blk:
                    @blk.vector
                    def _(v):
                        r_c.inc(v.tensor_single_scalar(out=hm[:], in_=hi[:],
                                                       scalar=7,
                                                       op=ALU.bitwise_and), 1)
                        v.wait_ge(r_c.h, r_c.n)

                with nc.Block() as blk:
                    @blk.scalar
                    def _(s):
                        r_c.inc(s.copy(out=hf[:], in_=hm[:]), 1)
                        s.wait_ge(r_c.h, r_c.n)

                with nc.Block() as blk:
                    @blk.vector
                    def _(v):
                        def ve(instr):
                            r_c.inc(instr, 1)
                            v.wait_ge(r_c.h, r_c.n)
                        for e in range(8):
                            ve(v.tensor_single_scalar(out=m1[:], in_=hf[:],
                                                      scalar=float(e),
                                                      op=ALU.is_equal))
                            ve(v.tensor_single_scalar(out=m2[:], in_=hf[:],
                                                      scalar=float((e + 7) % 8),
                                                      op=ALU.is_equal))
                            ve(v.tensor_tensor(out=m3[:], in0=m1[:], in1=m2[:],
                                               op=ALU.add))
                            ve(v.tensor_scalar_mul(out=we[e][:], in0=m3[:],
                                                   scalar1=0.5))

                with nc.Block() as blk:
                    @blk.gpsimd
                    def _(g):
                        for e in range(8):
                            r_so.inc(g.dma_start(
                                out=w8d_ap[e:e + 1, ch * W:(ch + 1) * W],
                                in_=we[e][:]), 16)
                        g.wait_ge(r_so.h, r_so.n)
    return emit



def mk_routing_l3(xn_ap, gw_ap, gb_ap):
    def emit(C, w8d_ap):
        nc = C.nc
        q_ld = C.sem("q_ld")
        q_gw = C.sem("q_gw")
        q_mm = C.sem("q_mm")
        q_c = C.sem("q_c")
        q_tr = C.sem("q_tr")
        q_so = C.sem("q_so")
        gw0_ = q_gw.n
        with contextlib.ExitStack() as st:
            E = st.enter_context
            W = 512
            xnt = [E(nc.sbuf_tensor(f"r3_xn{c}", [128, W], BF16)) for c in range(8)]
            gwt = E(nc.sbuf_tensor("r3_gw", [128, 64], BF16))
            gbr = E(nc.sbuf_tensor("r3_gb", [1, 8], BF16))
            lg = E(nc.sbuf_tensor("r3_lg", [128, 8], F32))
            mx1 = E(nc.sbuf_tensor("r3_mx1", [128, 1], F32))
            mx2 = E(nc.sbuf_tensor("r3_mx2", [128, 1], F32))
            mk1 = E(nc.sbuf_tensor("r3_mk1", [128, 8], F32))
            mk1f = E(nc.sbuf_tensor("r3_mk1f", [128, 8], F32))
            mk2 = E(nc.sbuf_tensor("r3_mk2", [128, 8], F32))
            mk2f = E(nc.sbuf_tensor("r3_mk2f", [128, 8], F32))
            cs = E(nc.sbuf_tensor("r3_cs", [128, 8], F32))
            cse = E(nc.sbuf_tensor("r3_cse", [128, 8], F32))
            lg2 = E(nc.sbuf_tensor("r3_lg2", [128, 8], F32))
            dd = E(nc.sbuf_tensor("r3_dd", [128, 1], F32))
            wa = E(nc.sbuf_tensor("r3_wa", [128, 1], F32))
            wb_ = E(nc.sbuf_tensor("r3_wb", [128, 1], F32))
            wt1 = E(nc.sbuf_tensor("r3_wt1", [128, 8], F32))
            wt2 = E(nc.sbuf_tensor("r3_wt2", [128, 8], F32))
            wtok = E(nc.sbuf_tensor("r3_wtok", [128, 8], BF16))
            zz = E(nc.sbuf_tensor("r3_zz", [128, 8], F32))
            wrow = E(nc.sbuf_tensor("r3_wrow", [8, W], F32))
            ps_l = E(nc.psum_tensor("r3_psl", [128, 512], F32))
            ps_t = E(nc.psum_tensor("r3_pst", [8, 512], BF16))

            with nc.Block() as blk:
                @blk.gpsimd
                def _(g):
                    src = gw_ap.rearrange("(c p) e -> p c e", p=128)
                    q_gw.inc(g.dma_start(
                        out=gwt[:].rearrange("p (c e) -> p c e", e=8), in_=src), 16)
                    q_gw.inc(g.dma_start(out=gbr[:], in_=gb_ap.unsqueeze(0)), 16)

                @blk.vector
                def _(v):
                    v.memset(zz[:], 0.0)

            for ch in range(N // W):
                ld0, mm0, c0, tr0 = q_ld.n, q_mm.n, q_c.n, q_tr.n
                with nc.Block() as blk:
                    @blk.sync
                    def _(sy):
                        for c in range(8):
                            q_ld.inc(sy.dma_start(
                                out=xnt[c][:],
                                in_=xn_ap[c * 128:(c + 1) * 128,
                                          ch * W:(ch + 1) * W]), 16)

                for tt in range(4):
                    with nc.Block() as blk:
                        @blk.tensor
                        def _(t):
                            t.wait_ge(q_ld.h, ld0 + 128)
                            t.wait_ge(q_gw.h, gw0_ + 32)
                            for c in range(8):
                                t.matmul(ps_l[:, 0:8], xnt[c][:, tt * 128:
                                                              (tt + 1) * 128],
                                         gwt[:, c * 8:(c + 1) * 8],
                                         start=(c == 0), stop=False)
                            q_mm.inc(t.matmul(ps_l[:, 0:8], C.ones_b[0:1, :],
                                              gbr[:], start=False, stop=True), 1)

                        @blk.vector
                        def _(v):
                            def ve(instr):
                                q_c.inc(instr, 1)
                                v.wait_ge(q_c.h, q_c.n)
                            v.wait_ge(q_mm.h, q_mm.n)
                            ve(v.tensor_copy(out=lg[:], in_=ps_l[:, 0:8]))
                            ve(v.reduce_max(out=mx1[:], in_=lg[:], axis=AX.X))
                            ve(v.tensor_scalar(out=mk1[:], in0=lg[:],
                                               scalar1=mx1[:], scalar2=None,
                                               op0=ALU.is_equal))
                            ve(v.tensor_tensor_scan(out=cs[:], data0=mk1[:],
                                                    data1=zz[:], initial=0.0,
                                                    op0=ALU.add, op1=ALU.add))
                            ve(v.tensor_single_scalar(out=cse[:], in_=cs[:],
                                                      scalar=1.0,
                                                      op=ALU.is_equal))
                            ve(v.tensor_tensor(out=mk1f[:], in0=mk1[:],
                                               in1=cse[:], op=ALU.mult))
                            ve(v.scalar_tensor_tensor(out=lg2[:], in0=mk1f[:],
                                                      scalar=-1e30, in1=lg[:],
                                                      op0=ALU.mult, op1=ALU.add))
                            ve(v.reduce_max(out=mx2[:], in_=lg2[:], axis=AX.X))
                            ve(v.tensor_scalar(out=mk2[:], in0=lg2[:],
                                               scalar1=mx2[:], scalar2=None,
                                               op0=ALU.is_equal))
                            ve(v.tensor_tensor_scan(out=cs[:], data0=mk2[:],
                                                    data1=zz[:], initial=0.0,
                                                    op0=ALU.add, op1=ALU.add))
                            ve(v.tensor_single_scalar(out=cse[:], in_=cs[:],
                                                      scalar=1.0,
                                                      op=ALU.is_equal))
                            ve(v.tensor_tensor(out=mk2f[:], in0=mk2[:],
                                               in1=cse[:], op=ALU.mult))
                            ve(v.tensor_tensor(out=dd[:], in0=mx2[:], in1=mx1[:],
                                               op=ALU.subtract))

                    with nc.Block() as blk:
                        @blk.scalar
                        def _(s):
                            q_c.inc(s.activation(out=wa[:], in_=dd[:],
                                                 func=AF.Sigmoid, scale=-1.0), 1)
                            s.wait_ge(q_c.h, q_c.n)
                            q_c.inc(s.activation(out=wb_[:], in_=dd[:],
                                                 func=AF.Sigmoid), 1)

                    with nc.Block() as blk:
                        @blk.vector
                        def _(v):
                            def ve(instr):
                                q_c.inc(instr, 1)
                                v.wait_ge(q_c.h, q_c.n)
                            ve(v.tensor_scalar(out=wt1[:], in0=mk1f[:],
                                               scalar1=wa[:], scalar2=None,
                                               op0=ALU.mult))
                            ve(v.tensor_scalar(out=wt2[:], in0=mk2f[:],
                                               scalar1=wb_[:], scalar2=None,
                                               op0=ALU.mult))
                            ve(v.tensor_tensor(out=wtok[:], in0=wt1[:],
                                               in1=wt2[:], op=ALU.add))

                    with nc.Block() as blk:
                        @blk.tensor
                        def _(t):
                            q_tr.inc(t.transpose(ps_t[:, 0:128], wtok[:],
                                                 C.ident_b[:]), 1)

                        @blk.vector
                        def _(v):
                            v.wait_ge(q_tr.h, q_tr.n)
                            q_tr.inc(v.tensor_copy(
                                out=wrow[:, tt * 128:(tt + 1) * 128],
                                in_=ps_t[0:8, 0:128]), 1)

                with nc.Block() as blk:
                    @blk.gpsimd
                    def _(g):
                        q_so.inc(g.dma_start(
                            out=w8d_ap[:, ch * W:(ch + 1) * W], in_=wrow[:]), 16)
                        g.wait_ge(q_so.h, q_so.n)
    return emit



def emit_ffn2(C, tag, xn_ap, xn2_ap, q1_ap, s1_ap, b1_ap, q2_ap, s2_ap, b2_ap,
              x_ap, s_ap, xout_ap):
    nc = C.nc
    with contextlib.ExitStack() as st:
        E = st.enter_context
        f_ld = C.sem("f2_ld")
        f_sr = C.sem("f2_sr")
        f_bl = C.sem("f2_bl")
        f_bt = C.sem("f2_bt")
        f_cv = C.sem("f2_cv")
        f_bc = C.sem("f2_bc")
        f_dq = C.sem("f2_dq")
        f_xl = C.sem("f2_xl")
        f_h1 = C.sem("f2_h1")
        f_hs = C.sem("f2_hs")
        f_p2 = C.sem("f2_p2")
        f_gr = C.sem("f2_gr")
        f_cl = [C.sem("f2_cl0"), C.sem("f2_cl1")]
        f_cg = C.sem("f2_cg")
        f_cc = C.sem("f2_cc")
        f_so = [C.sem("f2_so0"), C.sem("f2_so1")]

        w1b = [E(nc.sbuf_tensor(f"{tag}_w1{c}", [128, 4096], BF16))
               for c in range(8)]
        w2b = [E(nc.sbuf_tensor(f"{tag}_w2{f}", [128, D], BF16))
               for f in range(32)]
        qt_ = [E(nc.sbuf_tensor(f"{tag}_qt{i}", [128, 512], I32)) for i in range(2)]
        qf_ = [E(nc.sbuf_tensor(f"{tag}_qf{i}", [128, 512], F32)) for i in range(2)]
        srow = E(nc.sbuf_tensor(f"{tag}_srow", [1, 512], F32))
        brawt = E(nc.sbuf_tensor(f"{tag}_braw", [40, 128], F32))
        bcols = E(nc.sbuf_tensor(f"{tag}_bcols", [128, 40], F32))
        xn2t = [E(nc.sbuf_tensor(f"{tag}_x2{c}", [128, 512], BF16))
                for c in range(8)]
        xnt = [E(nc.sbuf_tensor(f"{tag}_x1{c}", [128, 512], BF16))
               for c in range(8)]
        ht = [E(nc.sbuf_tensor(f"{tag}_h{f}", [128, 512], BF16)) for f in range(32)]
        grt = [E(nc.sbuf_tensor(f"{tag}_gr{c}", [128, 512], BF16))
               for c in range(8)]
        xtl = qf_
        stl = [qt_[i][:].bitcast(F32) for i in range(2)]
        gtl = [E(nc.sbuf_tensor(f"{tag}_gtl{i}", [128, 512], BF16)) for i in range(2)]
        g2l = [E(nc.sbuf_tensor(f"{tag}_g2l{i}", [128, 512], F32)) for i in range(2)]
        otl = stl
        ps_a = [E(nc.psum_tensor(f"{tag}_psa{i}", [128, 512], F32)) for i in range(2)]
        ps_b = [E(nc.psum_tensor(f"{tag}_psb{i}", [128, 512], F32)) for i in range(2)]
        ps_s = E(nc.psum_tensor(f"{tag}_pss", [128, 512], F32))

        with nc.Block() as blk:
            @blk.sync
            def _(sy):
                f_bl.inc(sy.dma_start(
                    out=brawt[0:32, :],
                    in_=b1_ap.rearrange("(a b) -> a b", b=128)), 16)
                f_bl.inc(sy.dma_start(
                    out=brawt[32:40, :],
                    in_=b2_ap.rearrange("(a b) -> a b", b=128)), 16)

            @blk.tensor
            def _(t):
                t.wait_ge(f_bl.h, f_bl.n)
                f_bt.inc(t.transpose(ps_a[0][:, 0:40], brawt[:],
                                     C.ident_f[0:40, 0:40]), 1)

            @blk.vector
            def _(v):
                v.wait_ge(f_bt.h, f_bt.n)
                f_bt.inc(v.tensor_copy(out=bcols[:], in_=ps_a[0][:, 0:40]), 1)

        DQ = []
        for c in range(8):
            for fb in range(8):
                DQ.append((1, c, fb))
        for f in range(32):
            for ob in range(2):
                DQ.append((2, f, ob))
        ld0, sr0, cv0, bc0, dq0 = f_ld.n, f_sr.n, f_cv.n, f_bc.n, f_dq.n
        with nc.Block() as blk:
            @blk.sync
            def _(sy):
                for k, (w, a, b) in enumerate(DQ):
                    if k >= 2:
                        sy.wait_ge(f_cv.h, cv0 + k - 1)
                    if w == 1:
                        src = q1_ap[a * 128:(a + 1) * 128, b * 512:(b + 1) * 512]
                    else:
                        src = q2_ap[a * 128:(a + 1) * 128, b * 512:(b + 1) * 512]
                    f_ld.inc(sy.dma_start(out=qt_[k % 2][:], in_=src), 16)

            @blk.scalar
            def _(s):
                for k, (w, a, b) in enumerate(DQ):
                    s.wait_ge(f_ld.h, ld0 + (k + 1) * 16)
                    if k >= 2:
                        s.wait_ge(f_dq.h, dq0 + k - 1)
                    f_cv.inc(s.copy(out=qf_[k % 2][:], in_=qt_[k % 2][:]), 1)

            @blk.gpsimd
            def _(g):
                for k, (w, a, b) in enumerate(DQ):
                    if k >= 1:
                        g.wait_ge(f_bc.h, bc0 + k)
                    src = (s1_ap if w == 1 else s2_ap)[a:a + 1,
                                                       b * 512:(b + 1) * 512]
                    f_sr.inc(g.dma_start(out=srow[:], in_=src), 16)

            @blk.tensor
            def _(t):
                for k, (w, a, b) in enumerate(DQ):
                    t.wait_ge(f_sr.h, sr0 + (k + 1) * 16)
                    if k >= 1:
                        t.wait_ge(f_dq.h, dq0 + k)
                    f_bc.inc(t.matmul(ps_s[:], C.ones_f[0:1, :], srow[:],
                                      start=True, stop=True), 1)

            @blk.vector
            def _(v):
                for k, (w, a, b) in enumerate(DQ):
                    v.wait_ge(f_cv.h, cv0 + k + 1)
                    v.wait_ge(f_bc.h, bc0 + k + 1)
                    dst = (w1b[a][:, b * 512:(b + 1) * 512] if w == 1
                           else w2b[a][:, b * 512:(b + 1) * 512])
                    f_dq.inc(v.scalar_tensor_tensor(
                        out=dst, in0=qf_[k % 2][:], scalar=-128.0, in1=ps_s[:],
                        op0=ALU.add, op1=ALU.mult), 1)

        for ch in range(NTC):
            tb = ch * 512
            xl0, h10, hs0, p20, gr0 = f_xl.n, f_h1.n, f_hs.n, f_p2.n, f_gr.n
            cl0 = [f_cl[0].n, f_cl[1].n]
            cg0, cc0 = f_cg.n, f_cc.n
            so0 = [f_so[0].n, f_so[1].n]
            with nc.Block() as blk:
                @blk.sync
                def _(sy):
                    for c in range(8):
                        f_xl.inc(sy.dma_start(
                            out=xn2t[c][:],
                            in_=xn2_ap[c * 128:(c + 1) * 128, tb:tb + 512]), 16)
                        f_xl.inc(sy.dma_start(
                            out=xnt[c][:],
                            in_=xn_ap[c * 128:(c + 1) * 128, tb:tb + 512]), 16)

                @blk.tensor
                def _(t):
                    t.wait_ge(f_xl.h, xl0 + 256)
                    for f in range(32):
                        if f >= 2:
                            t.wait_ge(f_hs.h, hs0 + f - 1)
                        pb = ps_a[f % 2]
                        for c in range(8):
                            i_ = t.matmul(pb[:],
                                          w1b[c][:, f * 128:(f + 1) * 128],
                                          xn2t[c][:], start=(c == 0),
                                          stop=(c == 7))
                        f_h1.inc(i_, 1)
                    for o8 in range(8):
                        t.wait_ge(f_hs.h, hs0 + 32)
                        if o8 >= 2:
                            t.wait_ge(f_gr.h, gr0 + 2 * (o8 - 2) + 1)
                        pb = ps_b[o8 % 2]
                        for f in range(32):
                            i_ = t.matmul(pb[:],
                                          w2b[f][:, o8 * 128:(o8 + 1) * 128],
                                          ht[f][:], start=(f == 0),
                                          stop=(f == 31))
                        f_p2.inc(i_, 1)

                @blk.scalar
                def _(s):
                    for f in range(32):
                        s.wait_ge(f_h1.h, h10 + f + 1)
                        f_hs.inc(s.activation(out=ht[f][:], in_=ps_a[f % 2][:],
                                              func=AF.Silu,
                                              bias=bcols[:, f:f + 1]), 1)

                @blk.vector
                def _(v):
                    for o8 in range(8):
                        v.wait_ge(f_p2.h, p20 + o8 + 1)
                        if o8 >= 2:
                            v.wait_ge(f_gr.h, gr0 + 2 * (o8 - 2) + 2)
                        f_gr.inc(v.tensor_scalar(
                            out=g2l[o8 % 2][:], in0=ps_b[o8 % 2][:],
                            scalar1=bcols[:, 32 + o8:33 + o8], scalar2=0.5,
                            op0=ALU.add, op1=ALU.mult), 1)
                        v.wait_ge(f_gr.h, f_gr.n)
                        f_gr.inc(v.tensor_tensor(out=grt[o8][:],
                                                 in0=g2l[o8 % 2][:],
                                                 in1=xnt[o8][:], op=ALU.add), 1)

            with nc.Block() as blk:
                @blk.sync
                def _(sy):
                    for c in range(8):
                        sy.wait_ge(f_cc.h, cc0 + max(0, 2 * (c - 1)))
                        sy.wait_ge(f_so[c % 2].h, so0[c % 2] + (c // 2) * 16)
                        f_cl[c % 2].inc(sy.dma_start(
                            out=xtl[c % 2][:],
                            in_=x_ap[c * 128:(c + 1) * 128, tb:tb + 512]), 16)
                        f_cl[c % 2].inc(sy.dma_start(
                            out=stl[c % 2],
                            in_=s_ap[c * 128:(c + 1) * 128, tb:tb + 512]), 16)

                @blk.scalar
                def _(s):
                    for c in range(8):
                        s.wait_ge(f_cc.h, cc0 + max(0, 2 * (c - 2) + 1))
                        f_cg.inc(s.activation(out=gtl[c % 2][:], in_=grt[c][:],
                                              func=AF.Silu), 1)

                @blk.vector
                def _(v):
                    for c in range(8):
                        v.wait_ge(f_cl[c % 2].h, cl0[c % 2] + (c // 2 + 1) * 32)
                        v.wait_ge(f_cg.h, cg0 + c + 1)
                        v.wait_ge(f_so[c % 2].h, so0[c % 2] + (c // 2) * 16)
                        f_cc.inc(v.tensor_tensor(out=g2l[c % 2][:],
                                                 in0=stl[c % 2],
                                                 in1=gtl[c % 2][:],
                                                 op=ALU.mult), 1)
                        v.wait_ge(f_cc.h, f_cc.n)
                        f_cc.inc(v.tensor_tensor(out=otl[c % 2],
                                                 in0=xtl[c % 2][:],
                                                 in1=g2l[c % 2][:],
                                                 op=ALU.add), 1)

                @blk.gpsimd
                def _(g):
                    for c in range(8):
                        g.wait_ge(f_cc.h, cc0 + 2 * (c + 1))
                        f_so[c % 2].inc(g.dma_start(
                            out=xout_ap[c * 128:(c + 1) * 128, tb:tb + 512],
                            in_=otl[c % 2]), 16)

        with nc.Block() as blk:
            @blk.gpsimd
            def _(g):
                g.wait_ge(f_so[0].h, f_so[0].n)
                g.wait_ge(f_so[1].h, f_so[1].n)



def emit_attn3(C, tag, xn_ap, win_ap, convw_ap, convb_ap, wx_ap, wdt_ap,
               bdt_ap, alog_ap, dvec_ap, wout_ap, s_ap):
    nc = C.nc
    DI = 2048
    TH = 512
    NQ = T // TH
    sres_d = nc.dram_tensor(f"{tag}_sres", [DI, N], BF16, kind="Internal").ap()
    with contextlib.ExitStack() as st:
        E = st.enter_context
        z_ld = C.sem("z_ld")
        z_w = [C.sem("z_w0"), C.sem("z_w1")]
        z_wx = C.sem("z_wx")
        z_wg = C.sem("z_wg")
        z_bt = C.sem("z_bt")
        z_b2 = C.sem("z_b2")
        z_ae = C.sem("z_ae")
        z_mm = C.sem("z_mm")
        z_ev = C.sem("z_ev")
        z_rs = C.sem("z_rs")
        z_sr = C.sem("z_sr")
        z_cv = C.sem("z_cv")
        z_si = C.sem("z_si")
        z_xd = C.sem("z_xd")
        z_xe = C.sem("z_xe")
        d_mm = C.sem("d_mm")
        d_s = C.sem("d_s")
        d_v = C.sem("d_v")
        z_pb = C.sem("z_pb")
        z_pe = C.sem("z_pe")
        z_ss = C.sem("z_ss")
        z_sc = C.sem("z_sc")
        z_sp = C.sem("z_sp")
        z_sl = C.sem("z_sl")
        z_ga = C.sem("z_ga")
        z_om = C.sem("z_om")
        z_oe = C.sem("z_oe")
        z_so = [C.sem("z_so0"), C.sem("z_so1")]
        wx0_, ae0_ = z_wx.n, z_ae.n

        xnt = [E(nc.sbuf_tensor(f"{tag}_xn{c}", [128, TH], BF16)) for c in range(8)]
        ws = [E(nc.sbuf_tensor(f"{tag}_ws{i}", [128, D], BF16)) for i in range(2)]
        wos = [E(nc.sbuf_tensor(f"{tag}_wo{i}", [128, DI], BF16))
               for i in range(2)]
        xsp = [E(nc.sbuf_tensor(f"{tag}_xs{m}", [128, TH + 3], BF16))
               for m in range(16)]
        ut = [E(nc.sbuf_tensor(f"{tag}_u{m}", [128, TH], BF16)) for m in range(16)]
        dlt = [E(nc.sbuf_tensor(f"{tag}_dl{m}", [128, TH], BF16))
               for m in range(16)]
        dut = [E(nc.sbuf_tensor(f"{tag}_du{m}", [128, TH], BF16))
               for m in range(16)]
        Bbc = [E(nc.sbuf_tensor(f"{tag}_Bb{n}", [128, TH], BF16))
               for n in range(16)]
        Cbc = [E(nc.sbuf_tensor(f"{tag}_Cb{n}", [128, TH], BF16))
               for n in range(16)]
        wxt = E(nc.sbuf_tensor(f"{tag}_wxt", [128, 16 * 96], BF16))
        wdts = E(nc.sbuf_tensor(f"{tag}_wdts", [64, DI], BF16))
        dtb = E(nc.sbuf_tensor(f"{tag}_dtb", [64, TH], BF16))
        Bsb = E(nc.sbuf_tensor(f"{tag}_Bsb", [16, TH], BF16))
        Csb = E(nc.sbuf_tensor(f"{tag}_Csb", [16, TH], BF16))
        cw = [E(nc.sbuf_tensor(f"{tag}_cw{m}", [128, 4], F32)) for m in range(16)]
        Acl = [E(nc.sbuf_tensor(f"{tag}_Ac{m}", [128, 16], F32)) for m in range(16)]
        craw = E(nc.sbuf_tensor(f"{tag}_craw", [32, 128], F32))
        ccol = E(nc.sbuf_tensor(f"{tag}_ccol", [128, 32], F32))
        braw2 = E(nc.sbuf_tensor(f"{tag}_braw2", [16, 128], F32))
        bdtc = E(nc.sbuf_tensor(f"{tag}_bdtc", [128, 16], F32))
        selt = E(nc.sbuf_tensor(f"{tag}_sel", [16, DI], BF16))
        selo = E(nc.sbuf_tensor(f"{tag}_selo", [16, DI], BF16))
        ascr = E(nc.sbuf_tensor(f"{tag}_ascr", [128, TH], F32))
        bscr = E(nc.sbuf_tensor(f"{tag}_bscr", [128, TH], F32))
        hscr = E(nc.sbuf_tensor(f"{tag}_hscr", [128, TH], F32))
        cscr = E(nc.sbuf_tensor(f"{tag}_cscr", [128, TH], F32))
        csc2 = E(nc.sbuf_tensor(f"{tag}_csc2", [128, TH], F32))
        dscr = E(nc.sbuf_tensor(f"{tag}_dscr", [128, TH], F32))
        yacc = [E(nc.sbuf_tensor(f"{tag}_ya{i}", [128, TH], F32)) for i in range(2)]
        ysc = E(nc.sbuf_tensor(f"{tag}_ysc", [128, TH], F32))
        srt = E(nc.sbuf_tensor(f"{tag}_srt", [128, TH], BF16))
        hcar = [E(nc.sbuf_tensor(f"{tag}_hc{m}", [128, 16], F32))
                for m in range(16)]
        sst = [E(nc.sbuf_tensor(f"{tag}_sst{i}", [128, TH], F32))
               for i in range(2)]
        ps_a = [E(nc.psum_tensor(f"{tag}_psa{i}", [128, 512], F32)) for i in range(2)]
        ps_b = [E(nc.psum_tensor(f"{tag}_psb{i}", [128, 512], F32)) for i in range(2)]
        ps_dt = E(nc.psum_tensor(f"{tag}_psdt", [64, 512], F32))
        ps_B = E(nc.psum_tensor(f"{tag}_psB", [16, 512], F32))
        ps_C = E(nc.psum_tensor(f"{tag}_psC", [16, 512], F32))

        with nc.Block() as blk:
            @blk.sync
            def _(sy):
                z_wx.inc(sy.dma_start(
                    out=craw[0:16, :],
                    in_=convb_ap.rearrange("(a b) -> a b", b=128)), 16)
                z_wx.inc(sy.dma_start(
                    out=craw[16:32, :],
                    in_=dvec_ap.rearrange("(a b) -> a b", b=128)), 16)
                z_wx.inc(sy.dma_start(
                    out=braw2[:],
                    in_=bdt_ap.rearrange("(a b) -> a b", b=128)), 16)

            @blk.gpsimd
            def _(g):
                z_wg.inc(g.dma_start(out=wdts[:], in_=wdt_ap), 16)
                z_wg.inc(g.dma_start(
                    out=wxt[:].rearrange("p (c f) -> p c f", f=96),
                    in_=wx_ap.rearrange("(c p) f -> p c f", p=128)), 16)
                for m in range(16):
                    z_wg.inc(g.dma_start(
                        out=cw[m][:], in_=convw_ap[m * 128:(m + 1) * 128, :]), 16)
                    z_wg.inc(g.dma_start(
                        out=Acl[m][:], in_=alog_ap[m * 128:(m + 1) * 128, :]), 16)

            @blk.vector
            def _(v):
                v.memset(selo[:], 1.0)

            @blk.tensor
            def _(t):
                t.wait_ge(z_wx.h, wx0_ + 48)
                z_bt.inc(t.transpose(ps_a[0][:, 0:32], craw[:],
                                     C.ident_f[0:32, 0:32]), 1)
                z_bt.inc(t.transpose(ps_b[0][:, 0:16], braw2[:],
                                     C.ident_f[0:16, 0:16]), 1)

        with nc.Block() as blk:
            @blk.gpsimd
            def _(g):
                g.affine_select(out=selt[:], in_=selo[:],
                                pattern=[[-1, 16], [0, 128]],
                                compare_op=ALU.is_equal, fill=0.0, base=0,
                                channel_multiplier=1)

            @blk.vector
            def _(v):
                z_b2.inc(v.tensor_copy(out=ccol[:], in_=ps_a[0][:, 0:32]), 1)
                z_b2.inc(v.tensor_copy(out=bdtc[:], in_=ps_b[0][:, 0:16]), 1)

        with nc.Block() as blk:
            @blk.scalar
            def _(s):
                s.wait_ge(z_wg.h, z_wg.n)
                for m in range(16):
                    z_ae.inc(s.activation(out=hcar[m][:], in_=Acl[m][:],
                                          func=AF.Exp), 1)

            @blk.vector
            def _(v):
                for m in range(16):
                    v.wait_ge(z_ae.h, ae0_ + m + 1)
                    z_b2.inc(v.tensor_scalar_mul(out=Acl[m][:], in0=hcar[m][:],
                                                 scalar1=-1.0), 1)

        for s_ in range(SPC):
            for hf in range(NQ):
                t0 = s_ * T + hf * TH
                ld0, mm0, ev0, rs0 = z_ld.n, z_mm.n, z_ev.n, z_rs.n
                w_ready = []
                with nc.Block() as blk:
                    @blk.sync
                    def _(sy):
                        for c in range(8):
                            z_ld.inc(sy.dma_start(
                                out=xnt[c][:],
                                in_=xn_ap[c * 128:(c + 1) * 128, t0:t0 + TH]), 16)

                    @blk.gpsimd
                    def _(g):
                        for m in range(32):
                            if m >= 2:
                                g.wait_ge(z_mm.h, mm0 + m - 1)
                            src = win_ap[:, m * 128:(m + 1) * 128]
                            src = src.rearrange("(c p) f -> p c f", p=128)
                            dst = ws[m % 2][:].rearrange("p (c f) -> p c f", f=128)
                            w_ready.append(z_w[m % 2].inc(
                                g.dma_start(out=dst, in_=src), 16))
                        for m in range(16):
                            g.wait_ge(z_rs.h, rs0 + m + 1)
                            z_sr.inc(g.dma_start(
                                out=sres_d[m * 128:(m + 1) * 128, t0:t0 + TH],
                                in_=dut[m][:]), 16)

                    @blk.tensor
                    def _(t):
                        t.wait_ge(z_ld.h, ld0 + 128)
                        for m in range(32):
                            t.wait_ge(z_w[m % 2].h, w_ready[m])
                            if m >= 2:
                                if m - 2 < 16:
                                    t.wait_ge(z_ev.h, ev0 + m - 1)
                                else:
                                    t.wait_ge(z_rs.h, rs0 + m - 17)
                            pb = ps_a[m % 2]
                            for c in range(8):
                                i_ = t.matmul(
                                    pb[:], ws[m % 2][:, c * 128:(c + 1) * 128],
                                    xnt[c][:], start=(c == 0), stop=(c == 7))
                            z_mm.inc(i_, 1)

                    @blk.vector
                    def _(v):
                        for m in range(16):
                            v.wait_ge(z_mm.h, mm0 + m + 1)
                            z_ev.inc(v.tensor_copy(out=xsp[m][:, 3:3 + TH],
                                                   in_=ps_a[m % 2][:]), 1)

                    @blk.scalar
                    def _(s):
                        for m in range(16, 32):
                            s.wait_ge(z_mm.h, mm0 + m + 1)
                            z_rs.inc(s.activation(out=dut[m - 16][:],
                                                  in_=ps_a[m % 2][:],
                                                  func=AF.Silu), 1)

                cv0, si0 = z_cv.n, z_si.n
                P_CV = 6 if hf == 0 else 5
                with nc.Block() as blk:
                    @blk.vector
                    def _(v):
                        def ve(instr):
                            z_cv.inc(instr, 1)
                            v.wait_ge(z_cv.h, z_cv.n)
                        v.wait_ge(z_wg.h, z_wg.n)
                        for m in range(16):
                            if m >= 1:
                                v.wait_ge(z_si.h, si0 + m)
                            if hf == 0:
                                ve(v.memset(xsp[m][:, 0:3], 0.0))
                            ve(v.tensor_scalar(out=ascr[:],
                                               in0=xsp[m][:, 3:3 + TH],
                                               scalar1=cw[m][:, 3:4],
                                               scalar2=None, op0=ALU.mult))
                            ve(v.scalar_tensor_tensor(
                                out=cscr[:], in0=xsp[m][:, 2:2 + TH],
                                scalar=cw[m][:, 2:3], in1=ascr[:],
                                op0=ALU.mult, op1=ALU.add))
                            ve(v.scalar_tensor_tensor(
                                out=ascr[:], in0=xsp[m][:, 1:1 + TH],
                                scalar=cw[m][:, 1:2], in1=cscr[:],
                                op0=ALU.mult, op1=ALU.add))
                            ve(v.scalar_tensor_tensor(
                                out=cscr[:], in0=xsp[m][:, 0:TH],
                                scalar=cw[m][:, 0:1], in1=ascr[:],
                                op0=ALU.mult, op1=ALU.add))
                            ve(v.tensor_copy(out=xsp[m][:, 0:3],
                                             in_=xsp[m][:, TH:TH + 3]))

                    @blk.scalar
                    def _(s):
                        for m in range(16):
                            s.wait_ge(z_cv.h, cv0 + (m + 1) * P_CV)
                            z_si.inc(s.activation(out=ut[m][:], in_=cscr[:],
                                                  func=AF.Silu,
                                                  bias=ccol[:, m:m + 1]), 1)

                xd0, xe0 = z_xd.n, z_xe.n
                with nc.Block() as blk:
                    @blk.tensor
                    def _(t):
                        t.wait_ge(z_wg.h, z_wg.n)
                        for c in range(16):
                            t.matmul(ps_dt[:], wxt[:, c * 96:c * 96 + 64],
                                     ut[c][:], start=(c == 0), stop=(c == 15))
                        for c in range(16):
                            t.matmul(ps_B[:], wxt[:, c * 96 + 64:c * 96 + 80],
                                     ut[c][:], start=(c == 0), stop=(c == 15))
                        for c in range(16):
                            i_ = t.matmul(ps_C[:],
                                          wxt[:, c * 96 + 80:c * 96 + 96],
                                          ut[c][:], start=(c == 0), stop=(c == 15))
                        z_xd.inc(i_, 1)

                    @blk.vector
                    def _(v):
                        v.wait_ge(z_xd.h, xd0 + 1)
                        z_xe.inc(v.tensor_copy(out=dtb[:], in_=ps_dt[:]), 1)
                        z_xe.inc(v.tensor_copy(out=Bsb[:], in_=ps_B[:]), 1)
                        z_xe.inc(v.tensor_copy(out=Csb[:], in_=ps_C[:]), 1)

                dm0, ds0, dv0 = d_mm.n, d_s.n, d_v.n
                with nc.Block() as blk:
                    @blk.tensor
                    def _(t):
                        for m in range(16):
                            if m >= 2:
                                t.wait_ge(d_s.h, ds0 + 2 * (m - 2) + 1)
                            d_mm.inc(t.matmul(
                                ps_a[m % 2][:], wdts[:, m * 128:(m + 1) * 128],
                                dtb[:], start=True, stop=True), 1)

                    @blk.scalar
                    def _(s):
                        for m in range(16):
                            s.wait_ge(d_mm.h, dm0 + m + 1)
                            if m >= 1:
                                s.wait_ge(d_s.h, ds0 + 2 * (m - 1) + 2)
                            d_s.inc(s.activation(out=dscr[:], in_=ps_a[m % 2][:],
                                                 func=AF.Exp,
                                                 bias=bdtc[:, m:m + 1]), 1)
                            s.wait_ge(d_s.h, d_s.n)
                            d_s.inc(s.activation(out=dlt[m][:], in_=dscr[:],
                                                 func=AF.Ln, bias=1.0), 1)

                    @blk.vector
                    def _(v):
                        for m in range(16):
                            v.wait_ge(d_s.h, ds0 + 2 * m + 2)
                            v.wait_ge(z_sr.h, z_sr.n)
                            d_v.inc(v.tensor_tensor(out=dut[m][:], in0=dlt[m][:],
                                                    in1=ut[m][:], op=ALU.mult), 1)

                pb0, pe0 = z_pb.n, z_pe.n
                with nc.Block() as blk:
                    @blk.tensor
                    def _(t):
                        k = 0
                        for n in range(16):
                            for si in range(2):
                                srcs = (Bsb, Csb)[si]
                                if k >= 2:
                                    t.wait_ge(z_pe.h, pe0 + k - 1)
                                z_pb.inc(t.matmul(
                                    ps_a[k % 2][:],
                                    selt[:, n * 128:(n + 1) * 128],
                                    srcs[:], start=True, stop=True), 1)
                                k += 1

                    @blk.scalar
                    def _(s):
                        k = 0
                        for n in range(16):
                            for si in range(2):
                                dsts = (Bbc, Cbc)[si]
                                s.wait_ge(z_pb.h, pb0 + k + 1)
                                z_pe.inc(s.copy(out=dsts[n][:],
                                                in_=ps_a[k % 2][:]), 1)
                                k += 1

                sc0, sp0, ss0, ga0, sl0 = z_sc.n, z_sp.n, z_ss.n, z_ga.n, z_sl.n
                hsc = [hscr, dscr]
                csc = [cscr, csc2]
                with nc.Block() as blk:
                    @blk.sync
                    def _(sy):
                        sy.wait_ge(z_sr.h, z_sr.n)
                        for m in range(16):
                            sy.wait_ge(z_ga.h, ga0 + 2 * m)
                            z_sl.inc(sy.dma_start(
                                out=srt[:],
                                in_=sres_d[m * 128:(m + 1) * 128, t0:t0 + TH]),
                                16)

                    @blk.scalar
                    def _(s):
                        for m in range(16):
                            for n in range(16):
                                k = m * 16 + n
                                if k >= 1:
                                    s.wait_ge(z_sc.h, sc0 + 5 * (k - 1) + 2)
                                z_ss.inc(s.activation(
                                    out=ascr[:], in_=dlt[m][:], func=AF.Exp,
                                    scale=Acl[m][:, n:n + 1]), 1)

                    @blk.gpsimd
                    def _(g):
                        for m in range(16):
                            for n in range(16):
                                k = m * 16 + n
                                g.wait_ge(z_sc.h, sc0 + 5 * k + 2)
                                z_sp.inc(g.tensor_tensor(out=csc[k % 2][:],
                                                         in0=hsc[k % 2][:],
                                                         in1=Cbc[n][:],
                                                         op=ALU.mult), 1)

                    @blk.vector
                    def _(v):
                        def yacc_op(kk):
                            nn = kk % 16
                            v.wait_ge(z_sp.h, sp0 + kk + 1)
                            if nn == 0:
                                z_sc.inc(v.tensor_copy(out=yacc[1][:],
                                                       in_=csc[kk % 2][:]), 1)
                            else:
                                z_sc.inc(v.tensor_tensor(
                                    out=yacc[(nn + 1) % 2][:],
                                    in0=yacc[nn % 2][:], in1=csc[kk % 2][:],
                                    op=ALU.add), 1)
                            v.wait_ge(z_sc.h, z_sc.n)

                        def m_tail(mm):
                            v.wait_ge(z_sl.h, sl0 + (mm + 1) * 16)
                            z_ga.inc(v.scalar_tensor_tensor(
                                out=ysc[:], in0=ut[mm][:],
                                scalar=ccol[:, 16 + mm:17 + mm], in1=yacc[0][:],
                                op0=ALU.mult, op1=ALU.add), 1)
                            v.wait_ge(z_ga.h, z_ga.n)
                            z_ga.inc(v.tensor_tensor(out=ut[mm][:], in0=ysc[:],
                                                     in1=srt[:], op=ALU.mult), 1)

                        for m in range(16):
                            for n in range(16):
                                k = m * 16 + n
                                v.wait_ge(z_ss.h, ss0 + k + 1)
                                z_sc.inc(v.tensor_tensor(out=bscr[:],
                                                         in0=dut[m][:],
                                                         in1=Bbc[n][:],
                                                         op=ALU.mult), 1)
                                v.wait_ge(z_sc.h, z_sc.n)
                                if k >= 2:
                                    v.wait_ge(z_sp.h, sp0 + k - 1)
                                init = (0.0 if hf == 0 else hcar[m][:, n:n + 1])
                                z_sc.inc(v.tensor_tensor_scan(
                                    out=hsc[k % 2][:], data0=ascr[:],
                                    data1=bscr[:], initial=init,
                                    op0=ALU.mult, op1=ALU.add), 1)
                                v.wait_ge(z_sc.h, z_sc.n)
                                z_sc.inc(v.tensor_copy(
                                    out=hcar[m][:, n:n + 1],
                                    in_=hsc[k % 2][:, TH - 1:TH]), 1)
                                v.wait_ge(z_sc.h, z_sc.n)
                                if k >= 1:
                                    yacc_op(k - 1)
                                else:
                                    z_sc.inc(v.engine_nop(), 1)
                                    v.wait_ge(z_sc.h, z_sc.n)
                                z_sc.inc(v.engine_nop(), 1)
                                v.wait_ge(z_sc.h, z_sc.n)
                                if k >= 1 and (k - 1) % 16 == 15:
                                    m_tail((k - 1) // 16)
                        yacc_op(255)
                        m_tail(15)

                om0, oe0 = z_om.n, z_oe.n
                soO0 = [z_so[0].n, z_so[1].n]
                wo_ready = []
                with nc.Block() as blk:
                    @blk.gpsimd
                    def _(g):
                        def wodma(o8):
                            src = wout_ap[:, o8 * 128:(o8 + 1) * 128]
                            src = src.rearrange("(c p) f -> p c f", p=128)
                            dst = wos[o8 % 2][:].rearrange("p (c f) -> p c f",
                                                           f=128)
                            wo_ready.append(z_w[o8 % 2].inc(
                                g.dma_start(out=dst, in_=src), 16))
                        wodma(0)
                        wodma(1)
                        for o8 in range(8):
                            g.wait_ge(z_oe.h, oe0 + o8 + 1)
                            z_so[o8 % 2].inc(g.dma_start(
                                out=s_ap[o8 * 128:(o8 + 1) * 128, t0:t0 + TH],
                                in_=sst[o8 % 2][:]), 16)
                            if o8 + 2 < 8:
                                g.wait_ge(z_om.h, om0 + o8 + 1)
                                wodma(o8 + 2)

                    @blk.tensor
                    def _(t):
                        for o8 in range(8):
                            t.wait_ge(z_w[o8 % 2].h, wo_ready[o8])
                            if o8 >= 2:
                                t.wait_ge(z_oe.h, oe0 + o8 - 1)
                            pb = ps_b[o8 % 2]
                            for c in range(16):
                                i_ = t.matmul(
                                    pb[:], wos[o8 % 2][:, c * 128:(c + 1) * 128],
                                    ut[c][:], start=(c == 0), stop=(c == 15))
                            z_om.inc(i_, 1)

                    @blk.vector
                    def _(v):
                        for o8 in range(8):
                            v.wait_ge(z_om.h, om0 + o8 + 1)
                            v.wait_ge(z_so[o8 % 2].h,
                                      soO0[o8 % 2] + (o8 // 2) * 16)
                            z_oe.inc(v.tensor_copy(out=sst[o8 % 2][:],
                                                   in_=ps_b[o8 % 2][:]), 1)

        with nc.Block() as blk:
            @blk.gpsimd
            def _(g):
                g.wait_ge(z_so[0].h, z_so[0].n)
                g.wait_ge(z_so[1].h, z_so[1].n)
                g.wait_ge(z_sr.h, z_sr.n)



def emit_ffn1(C, tag, xn_ap, wc_ap, bc_ap, w1_ap, b1_ap, w2_ap, b2_ap,
              x_ap, s_ap, xout_ap):
    nc = C.nc
    with contextlib.ExitStack() as st:
        E = st.enter_context
        n_pre = C.sem("n_pre")
        n_prg = C.sem("n_prg")
        n_prw = C.sem("n_prw")
        n_tr = C.sem("n_tr")
        n_te = C.sem("n_te")
        n_io = C.sem("n_io")
        n_ld = C.sem("n_ld")
        n_lg = C.sem("n_lg")
        n_lv = C.sem("n_lv")
        n_w1 = [C.sem("n_w10"), C.sem("n_w11")]
        n_y1 = C.sem("n_y1")
        n_ys = C.sem("n_ys")
        n_px = C.sem("n_px")
        n_gv = C.sem("n_gv")
        n_rm = C.sem("n_rm")
        n_gr = C.sem("n_gr")
        n_cl = [C.sem("n_cl0"), C.sem("n_cl1")]
        n_cg = C.sem("n_cg")
        n_cc = C.sem("n_cc")
        n_so = [C.sem("n_so0"), C.sem("n_so1")]
        io0_, pre0_, prw0_ = n_io.n, n_pre.n, n_prw.n

        xnt = [E(nc.sbuf_tensor(f"{tag}_xn{c}", [128, 512], BF16))
               for c in range(8)]
        wct = E(nc.sbuf_tensor(f"{tag}_wct", [128, 1024], BF16))
        bccol = E(nc.sbuf_tensor(f"{tag}_bcc", [128, 1], F32))
        tierow = E(nc.sbuf_tensor(f"{tag}_tie", [1, 512], BF16))
        tiei = E(nc.sbuf_tensor(f"{tag}_tiei", [1, 512], I32))
        tie2 = E(nc.sbuf_tensor(f"{tag}_tie2", [1, 512], I32))
        tief = E(nc.sbuf_tensor(f"{tag}_tief", [1, 512], F32))
        W2T = [E(nc.sbuf_tensor(f"{tag}_w2t{i}", [128, D], BF16))
               for i in range(32)]
        w2stg = E(nc.sbuf_tensor(f"{tag}_w2s", [128, 256], BF16))
        b1raw = E(nc.sbuf_tensor(f"{tag}_b1r", [32, 128], F32))
        b1col = E(nc.sbuf_tensor(f"{tag}_b1c", [128, 32], F32))
        b2f = E(nc.sbuf_tensor(f"{tag}_b2f", [128, D], BF16))
        selp = [E(nc.sbuf_tensor(f"{tag}_sp{i}", [128, 128], BF16))
                for i in range(32)]
        selo = E(nc.sbuf_tensor(f"{tag}_spo", [128, 128], BF16))
        lgT = E(nc.sbuf_tensor(f"{tag}_lgT", [128, 512], F32))
        mxg = E(nc.sbuf_tensor(f"{tag}_mxg", [128, 16], F32))
        permT = E(nc.sbuf_tensor(f"{tag}_pT", [128, 512], BF16))
        w1s = [E(nc.sbuf_tensor(f"{tag}_w1{i}", [128, 1024], BF16))
               for i in range(2)]
        Y1 = [E(nc.sbuf_tensor(f"{tag}_Y{f}", [128, 512], BF16))
              for f in range(32)]
        Gt = [E(nc.sbuf_tensor(f"{tag}_G{f}", [128, 512], BF16))
              for f in range(32)]
        grt = [E(nc.sbuf_tensor(f"{tag}_gr{c}", [128, 512], BF16))
               for c in range(8)]
        xtl = [E(nc.sbuf_tensor(f"{tag}_xtl{i}", [128, 512], F32)) for i in range(2)]
        stl = [E(nc.sbuf_tensor(f"{tag}_stl{i}", [128, 512], F32)) for i in range(2)]
        gtl = [E(nc.sbuf_tensor(f"{tag}_gtl{i}", [128, 512], BF16)) for i in range(2)]
        g2l = [E(nc.sbuf_tensor(f"{tag}_g2l{i}", [128, 512], F32)) for i in range(2)]
        otl = [E(nc.sbuf_tensor(f"{tag}_otl{i}", [128, 512], F32)) for i in range(2)]
        ps_a = [E(nc.psum_tensor(f"{tag}_psa{i}", [128, 512], F32)) for i in range(2)]
        ps_b = [E(nc.psum_tensor(f"{tag}_psb{i}", [128, 512], F32)) for i in range(2)]

        with nc.Block() as blk:
            @blk.sync
            def _(sy):
                n_pre.inc(sy.dma_start(out=b1raw[:],
                                       in_=b1_ap.rearrange("a b f -> (a b f)")
                                       .rearrange("(a b) -> a b", b=128)), 16)
                n_pre.inc(sy.dma_start(out=bccol[:],
                                       in_=bc_ap.rearrange("a b -> (a b)")
                                       .unsqueeze(1)), 16)

            @blk.gpsimd
            def _(g):
                n_prw.inc(g.dma_start(
                    out=wct[:].rearrange("p (c e) -> p c e", e=128),
                    in_=wc_ap.rearrange("(c p) a b -> p c (a b)", p=128)), 16)
                n_prw.inc(g.dma_start(
                    out=b2f[:], in_=b2_ap.rearrange("a b d -> (a b) d")), 16)
                n_io.inc(g.iota(tiei[:], pattern=[[1, 512]], base=0,
                                channel_multiplier=0,
                                allow_small_or_imprecise_dtypes=True), 1)

            @blk.vector
            def _(v):
                v.memset(selo[:], 1.0)
                v.wait_ge(n_io.h, io0_ + 1)
                n_io.inc(v.tensor_single_scalar(out=tie2[:], in_=tiei[:],
                                                scalar=31, op=ALU.bitwise_and), 1)

            @blk.scalar
            def _(s):
                s.wait_ge(n_io.h, io0_ + 2)
                n_io.inc(s.copy(out=tief[:], in_=tie2[:]), 1)

        with nc.Block() as blk:
            @blk.vector
            def _(v):
                n_io.inc(v.tensor_scalar_mul(out=tierow[:], in0=tief[:],
                                             scalar1=1e-6 / 31.0), 1)

            @blk.gpsimd
            def _(g):
                for te in range(32):
                    g.affine_select(out=selp[te][:], in_=selo[:],
                                    pattern=[[-1, 4], [0, 32]],
                                    compare_op=ALU.is_equal, fill=0.0,
                                    base=-4 * te, channel_multiplier=1)

            @blk.tensor
            def _(t):
                t.wait_ge(n_pre.h, pre0_ + 32)
                n_tr.inc(t.transpose(ps_a[0][:, 0:32], b1raw[:],
                                     C.ident_f[0:32, 0:32]), 1)

        with nc.Block() as blk:
            @blk.vector
            def _(v):
                v.tensor_copy(out=b1col[:], in_=ps_a[0][:, 0:32])

        tr0, te0 = n_tr.n, n_te.n
        w2_ready = []
        with nc.Block() as blk:
            @blk.gpsimd
            def _(g):
                for es in range(128):
                    if es >= 1:
                        g.wait_ge(n_tr.h, tr0 + (es - 1) * 8 + 8)
                    src = w2_ap.rearrange("a b (c p) f -> (a b) p c f", p=128)
                    src = src[es]
                    dst = w2stg[:].rearrange("p (c f) -> p c f", f=32)
                    w2_ready.append(n_prg.inc(g.dma_start(out=dst, in_=src), 16))

            @blk.tensor
            def _(t):
                for es in range(128):
                    t.wait_ge(n_prg.h, w2_ready[es])
                    for c in range(8):
                        k = es * 8 + c
                        if k >= 2:
                            t.wait_ge(n_te.h, te0 + k - 1)
                        n_tr.inc(t.transpose(
                            ps_a[k % 2][:].bitcast(BF16)[0:32, 0:128],
                            w2stg[:, c * 32:(c + 1) * 32],
                            C.ident_b[:]), 1)

            @blk.vector
            def _(v):
                for es in range(128):
                    for c in range(8):
                        k = es * 8 + c
                        v.wait_ge(n_tr.h, tr0 + k + 1)
                        n_te.inc(v.tensor_copy(
                            out=W2T[es // 4][32 * (es % 4):32 * (es % 4) + 32,
                                             c * 128:(c + 1) * 128],
                            in_=ps_a[k % 2][:].bitcast(BF16)[0:32, 0:128]), 1)

        for ch in range(NTC):
            tb = ch * 512
            ld0 = n_ld.n
            lg0, lv0 = n_lg.n, n_lv.n
            y10, ys0 = n_y1.n, n_ys.n
            px0, gv0, rm0, gr0 = n_px.n, n_gv.n, n_rm.n, n_gr.n
            w1_ready = []
            with nc.Block() as blk:
                @blk.sync
                def _(sy):
                    for c in range(8):
                        n_ld.inc(sy.dma_start(
                            out=xnt[c][:],
                            in_=xn_ap[c * 128:(c + 1) * 128, tb:tb + 512]), 16)

                @blk.tensor
                def _(t):
                    t.wait_ge(n_ld.h, ld0 + 128)
                    t.wait_ge(n_prw.h, prw0_ + 32)
                    t.wait_ge(n_lv.h, lv0)
                    for c in range(8):
                        t.matmul(ps_b[0][:], wct[:, c * 128:(c + 1) * 128],
                                 xnt[c][:], start=(c == 0), stop=False)
                    t.wait_ge(n_io.h, io0_ + 4)
                    n_lg.inc(t.matmul(ps_b[0][:], C.ones_b[0:1, :], tierow[:],
                                      start=False, stop=True), 1)

                @blk.vector
                def _(v):
                    def ve(instr):
                        n_lv.inc(instr, 1)
                        v.wait_ge(n_lv.h, n_lv.n)
                    v.wait_ge(n_lg.h, lg0 + 1)
                    ve(v.tensor_scalar(out=lgT[:], in0=ps_b[0][:],
                                       scalar1=bccol[:], scalar2=None,
                                       op0=ALU.add))
                    ve(v.tensor_reduce(out=mxg[:],
                                       in_=lgT[:].rearrange("p (g w) -> p g w",
                                                            w=32),
                                       axis=AX.X, op=ALU.max))
                    ve(v.tensor_tensor(
                        out=permT[:].rearrange("p (g w) -> p g w", w=32),
                        in0=lgT[:].rearrange("p (g w) -> p g w", w=32),
                        in1=mxg[:].unsqueeze(2).broadcast_to([128, 16, 32]),
                        op=ALU.is_equal))

            with nc.Block() as blk:
                @blk.gpsimd
                def _(g):
                    for ft in range(32):
                        if ft >= 2:
                            g.wait_ge(n_y1.h, y10 + ft - 1)
                        src4 = w1_ap.rearrange("a b (c p) f -> p (a b) c f",
                                               p=128)
                        dst4 = w1s[ft % 2][:].rearrange("p (c e f) -> p e c f",
                                                        e=4, f=32)
                        for e4 in range(4):
                            n_w1[ft % 2].inc(g.dma_start(
                                out=dst4[:, e4], in_=src4[:, 4 * ft + e4]), 16)
                        w1_ready.append(n_w1[ft % 2].n)

                @blk.tensor
                def _(t):
                    for ft in range(32):
                        t.wait_ge(n_w1[ft % 2].h, w1_ready[ft])
                        if ft >= 2:
                            t.wait_ge(n_ys.h, ys0 + ft - 1)
                        pb = ps_a[ft % 2]
                        for c in range(8):
                            i_ = t.matmul(pb[:],
                                          w1s[ft % 2][:, c * 128:(c + 1) * 128],
                                          xnt[c][:], start=(c == 0),
                                          stop=(c == 7))
                        n_y1.inc(i_, 1)

                @blk.scalar
                def _(s):
                    for ft in range(32):
                        s.wait_ge(n_y1.h, y10 + ft + 1)
                        n_ys.inc(s.activation(out=Y1[ft][:], in_=ps_a[ft % 2][:],
                                              func=AF.Relu,
                                              bias=b1col[:, ft:ft + 1]), 1)

            with nc.Block() as blk:
                @blk.tensor
                def _(t):
                    for ft in range(32):
                        if ft >= 2:
                            t.wait_ge(n_gv.h, gv0 + ft - 1)
                        n_px.inc(t.matmul(ps_b[ft % 2][:], selp[ft][:],
                                          permT[:], start=True, stop=True), 1)

                @blk.vector
                def _(v):
                    for ft in range(32):
                        v.wait_ge(n_px.h, px0 + ft + 1)
                        n_gv.inc(v.tensor_tensor(out=Gt[ft][:], in0=Y1[ft][:],
                                                 in1=ps_b[ft % 2][:],
                                                 op=ALU.mult), 1)

            with nc.Block() as blk:
                @blk.tensor
                def _(t):
                    for o8 in range(8):
                        if o8 >= 2:
                            t.wait_ge(n_gr.h, gr0 + o8 - 1)
                        pb = ps_a[o8 % 2]
                        for ft in range(32):
                            t.matmul(pb[:], W2T[ft][:, o8 * 128:(o8 + 1) * 128],
                                     Gt[ft][:], start=(ft == 0), stop=False)
                        i_ = t.matmul(pb[:], b2f[:, o8 * 128:(o8 + 1) * 128],
                                      permT[:], start=False, stop=True)
                        n_rm.inc(i_, 1)

                @blk.vector
                def _(v):
                    for o8 in range(8):
                        v.wait_ge(n_rm.h, rm0 + o8 + 1)
                        n_gr.inc(v.tensor_copy(out=grt[o8][:],
                                               in_=ps_a[o8 % 2][:]), 1)

            cg0, cc0 = n_cg.n, n_cc.n
            cl0 = [n_cl[0].n, n_cl[1].n]
            so0 = [n_so[0].n, n_so[1].n]
            with nc.Block() as blk:
                @blk.sync
                def _(sy):
                    for c in range(8):
                        sy.wait_ge(n_cc.h, cc0 + max(0, 2 * (c - 1)))
                        n_cl[c % 2].inc(sy.dma_start(
                            out=xtl[c % 2][:],
                            in_=x_ap[c * 128:(c + 1) * 128, tb:tb + 512]), 16)
                        n_cl[c % 2].inc(sy.dma_start(
                            out=stl[c % 2][:],
                            in_=s_ap[c * 128:(c + 1) * 128, tb:tb + 512]), 16)

                @blk.scalar
                def _(s):
                    for c in range(8):
                        s.wait_ge(n_cc.h, cc0 + max(0, 2 * (c - 2) + 1))
                        n_cg.inc(s.activation(out=gtl[c % 2][:], in_=grt[c][:],
                                              func=AF.Silu), 1)

                @blk.vector
                def _(v):
                    for c in range(8):
                        v.wait_ge(n_cl[c % 2].h, cl0[c % 2] + (c // 2 + 1) * 32)
                        v.wait_ge(n_cg.h, cg0 + c + 1)
                        v.wait_ge(n_so[c % 2].h, so0[c % 2] + (c // 2) * 16)
                        n_cc.inc(v.tensor_tensor(out=g2l[c % 2][:],
                                                 in0=stl[c % 2][:],
                                                 in1=gtl[c % 2][:],
                                                 op=ALU.mult), 1)
                        v.wait_ge(n_cc.h, n_cc.n)
                        n_cc.inc(v.tensor_tensor(out=otl[c % 2][:],
                                                 in0=xtl[c % 2][:],
                                                 in1=g2l[c % 2][:],
                                                 op=ALU.add), 1)

                @blk.gpsimd
                def _(g):
                    for c in range(8):
                        g.wait_ge(n_cc.h, cc0 + 2 * (c + 1))
                        n_so[c % 2].inc(g.dma_start(
                            out=xout_ap[c * 128:(c + 1) * 128, tb:tb + 512],
                            in_=otl[c % 2][:]), 16)

        with nc.Block() as blk:
            @blk.gpsimd
            def _(g):
                g.wait_ge(n_so[0].h, n_so[0].n)
                g.wait_ge(n_so[1].h, n_so[1].n)



def emit_attn2(C, tag, xn_ap, keyw_ap, keyb_ap, vqw_ap, outw_ap, outb_ap, s_ap):
    nc = C.nc
    CH = 128
    NCH = T // CH
    with contextlib.ExitStack() as st:
        E = st.enter_context
        g_ld = C.sem("g_ld")
        g_w = [C.sem("g_w0"), C.sem("g_w1")]
        g_tr = C.sem("g_tr")
        g_te = [C.sem("g_te0"), C.sem("g_te1")]
        g_bl = C.sem("g_bl")
        g_bt = C.sem("g_bt")
        bl0_, bt0_ = g_bl.n, g_bt.n
        g_mm = C.sem("g_mm")
        g_ev = C.sem("g_ev")
        g_kv = C.sem("g_kv")
        g_ks = C.sem("g_ks")
        g_bd = C.sem("g_bd")
        g_rm = C.sem("g_rm")
        g_rv = C.sem("g_rv")
        g_rs = C.sem("g_rs")
        g_ms = C.sem("g_ms")
        e_mm = C.sem("g_emm")
        e_ev = C.sem("g_eev")
        e_so = [C.sem("g_es0"), C.sem("g_es1")]

        at = [E(nc.sbuf_tensor(f"{tag}_a{c}", [128, T], BF16)) for c in range(8)]
        vt = [E(nc.sbuf_tensor(f"{tag}_v{i}", [128, D], BF16)) for i in range(16)]
        Yf = [E(nc.sbuf_tensor(f"{tag}_Y{c}", [128, T], BF16)) for c in range(8)]
        wv = [E(nc.sbuf_tensor(f"{tag}_wv{c}", [128, D], BF16)) for c in range(8)]
        wqk = [E(nc.sbuf_tensor(f"{tag}_wqk{i}", [128, D], BF16)) for i in range(2)]
        wstg = [E(nc.sbuf_tensor(f"{tag}_ws{i}", [128, D], BF16)) for i in range(2)]
        braw = E(nc.sbuf_tensor(f"{tag}_braw", [16, 128], F32))
        bcols = E(nc.sbuf_tensor(f"{tag}_bcols", [128, 16], F32))
        qf = E(nc.sbuf_tensor(f"{tag}_qf", [128, T], BF16))
        kp = E(nc.sbuf_tensor(f"{tag}_kp", [128, T], BF16))
        Lc = E(nc.sbuf_tensor(f"{tag}_Lc", [128, T], F32))
        s1 = E(nc.sbuf_tensor(f"{tag}_s1", [128, T], F32))
        s2 = E(nc.sbuf_tensor(f"{tag}_s2", [128, T], F32))
        qlv = [E(nc.sbuf_tensor(f"{tag}_ql{i}", [128, T], BF16)) for i in range(4)]
        klv = [E(nc.sbuf_tensor(f"{tag}_kl{i}", [128, T], BF16)) for i in range(4)]
        qcs = E(nc.sbuf_tensor(f"{tag}_qcs", [128, CH], BF16))
        ecs = E(nc.sbuf_tensor(f"{tag}_ecs", [128, CH], F32))
        dcs = E(nc.sbuf_tensor(f"{tag}_dcs", [128, CH], F32))
        khf = E(nc.sbuf_tensor(f"{tag}_khf", [128, CH], BF16))
        khtm = E(nc.sbuf_tensor(f"{tag}_khtm", [128, 64], BF16))
        rsel = E(nc.sbuf_tensor(f"{tag}_rsel", [128, CH], BF16))
        zero_b = E(nc.sbuf_tensor(f"{tag}_zerob", [128, CH], BF16))
        mcau = E(nc.sbuf_tensor(f"{tag}_mcau", [128, CH], U8))
        mcf = E(nc.sbuf_tensor(f"{tag}_mcf", [128, CH], F32))
        pcc = E(nc.sbuf_tensor(f"{tag}_pcc", [64, 2], F32))
        Hst = [[E(nc.sbuf_tensor(f"{tag}_H{h}{i}", [64, 64], F32))
                for i in range(2)] for h in range(2)]
        Htmp = E(nc.sbuf_tensor(f"{tag}_Htmp", [64, 64], F32))
        s3 = E(nc.sbuf_tensor(f"{tag}_s3", [128, T], F32))
        Hbf2 = E(nc.sbuf_tensor(f"{tag}_Hb2", [128, 64], BF16))
        sst = [E(nc.sbuf_tensor(f"{tag}_sst{i}", [128, 512], F32))
               for i in range(2)]
        ps_p = [E(nc.psum_tensor(f"{tag}_psp{i}", [128, 512], F32))
                for i in range(2)]
        ps_r = [E(nc.psum_tensor(f"{tag}_psr{i}", [128, 512], F32))
                for i in range(2)]
        ps_y = [E(nc.psum_tensor(f"{tag}_psy{i}", [128, 512], F32))
                for i in range(2)]
        ps_h = E(nc.psum_tensor(f"{tag}_psh", [64, 512], F32))
        ps_t = E(nc.psum_tensor(f"{tag}_pst", [128, 512], F32))

        LEV = [16, 32, 64]

        with nc.Block() as blk:
            @blk.sync
            def _(sy):
                g_bl.inc(sy.dma_start(
                    out=braw[0:8, :],
                    in_=keyb_ap.rearrange("(a b) -> a b", b=128)), 16)
                g_bl.inc(sy.dma_start(
                    out=braw[8:16, :],
                    in_=outb_ap.rearrange("(a b) -> a b", b=128)), 16)

            @blk.vector
            def _(v):
                v.memset(zero_b[:], 0.0)
                v.memset(s1[:, 0:CH], 1.0)

            @blk.tensor
            def _(t):
                t.wait_ge(g_bl.h, bl0_ + 32)
                g_bt.inc(t.transpose(ps_p[0][:, 0:16], braw[:],
                                     C.ident_f[0:16, 0:16]), 1)

        with nc.Block() as blk:
            @blk.gpsimd
            def _(g):
                g.affine_select(out=mcf[:], in_=s1[:, 0:CH],
                                pattern=[[1, CH]], compare_op=ALU.is_ge,
                                fill=0.0, base=0, channel_multiplier=-1)

            @blk.vector
            def _(v):
                v.wait_ge(g_bt.h, bt0_ + 1)
                g_bt.inc(v.tensor_copy(out=bcols[:], in_=ps_p[0][:, 0:16]), 1)

        with nc.Block() as blk:
            @blk.gpsimd
            def _(g):
                g_ms.inc(g.tensor_copy(out=mcau[:], in_=mcf[:]), 1)

        tr0 = g_tr.n
        te0 = [g_te[0].n, g_te[1].n]
        wv_ready = []
        with nc.Block() as blk:
            @blk.gpsimd
            def _(g):
                for cb in range(8):
                    if cb >= 2:
                        g.wait_ge(g_tr.h, tr0 + (cb - 1) * 8)
                    wv_ready.append(g_w[cb % 2].inc(g.dma_start(
                        out=wstg[cb % 2][:],
                        in_=vqw_ap[cb * 128:(cb + 1) * 128, :]), 16))

            @blk.tensor
            def _(t):
                for cb in range(8):
                    t.wait_ge(g_w[cb % 2].h, wv_ready[cb])
                    for c in range(8):
                        k = cb * 8 + c
                        if k >= 2:
                            t.wait_ge(g_te[k % 2].h, te0[k % 2] + k // 2)
                        g_tr.inc(t.transpose(
                            ps_r[k % 2][:].bitcast(BF16)[:, 0:128],
                            wstg[cb % 2][:, c * 128:(c + 1) * 128],
                            C.ident_b[:]), 1)

            @blk.vector
            def _(v):
                for cb in range(8):
                    for c in range(8):
                        k = cb * 8 + c
                        v.wait_ge(g_tr.h, tr0 + k + 1)
                        g_te[k % 2].inc(v.tensor_copy(
                            out=wv[c][:, cb * 128:(cb + 1) * 128],
                            in_=ps_r[k % 2][:].bitcast(BF16)[:, 0:128]), 1)

        for s_ in range(SPC):
            t0 = s_ * T
            ld0, mm0, ev0 = g_ld.n, g_mm.n, g_ev.n
            with nc.Block() as blk:
                @blk.sync
                def _(sy):
                    for c in range(8):
                        g_ld.inc(sy.dma_start(
                            out=at[c][:],
                            in_=xn_ap[c * 128:(c + 1) * 128, t0:t0 + T]), 16)

                @blk.tensor
                def _(t):
                    t.wait_ge(g_ld.h, ld0 + 128)
                    for i16 in range(16):
                        for fh in range(2):
                            g_ = i16 * 2 + fh
                            if g_ >= 2:
                                t.wait_ge(g_ev.h, ev0 + g_ - 1)
                            pb = ps_p[g_ % 2]
                            for c in range(8):
                                i_ = t.matmul(
                                    pb[:], at[c][:, i16 * 128:(i16 + 1) * 128],
                                    wv[c][:, fh * 512:(fh + 1) * 512],
                                    start=(c == 0), stop=(c == 7))
                            g_mm.inc(i_, 1)

                @blk.vector
                def _(v):
                    for i16 in range(16):
                        for fh in range(2):
                            g_ = i16 * 2 + fh
                            v.wait_ge(g_mm.h, mm0 + g_ + 1)
                            g_ev.inc(v.tensor_copy(
                                out=vt[i16][:, fh * 512:(fh + 1) * 512],
                                in_=ps_p[g_ % 2][:]), 1)

            for hp in range(8):
                mmh0, evh0 = g_mm.n, g_ev.n
                trh0 = g_tr.n
                teh0 = [g_te[0].n, g_te[1].n]
                wr = []
                with nc.Block() as blk:
                    @blk.gpsimd
                    def _(g):
                        for j in range(2):
                            wap = keyw_ap if j == 0 else vqw_ap
                            rw = hp * 128 if j == 0 else 1024 + hp * 128
                            wr.append(g_w[j % 2].inc(g.dma_start(
                                out=wstg[j % 2][:],
                                in_=wap[rw:rw + 128, :]), 16))

                    @blk.tensor
                    def _(t):
                        for j in range(2):
                            t.wait_ge(g_w[j % 2].h, wr[j])
                            if j >= 1:
                                t.wait_ge(g_te[0].h, teh0[0] + 8)
                            for c in range(8):
                                k = j * 8 + c
                                if c >= 2:
                                    t.wait_ge(g_te[j % 2].h,
                                              teh0[j % 2] + (j // 2) * 8 + c - 1)
                                g_tr.inc(t.transpose(
                                    ps_r[c % 2][:].bitcast(BF16)[:, 0:128],
                                    wstg[j % 2][:, c * 128:(c + 1) * 128],
                                    C.ident_b[:]), 1)
                            t.wait_ge(g_te[j % 2].h, teh0[j % 2] + (j // 2 + 1) * 8)
                            for tch in range(4):
                                g_ = j * 4 + tch
                                if g_ >= 2:
                                    t.wait_ge(g_ev.h, evh0 + g_ - 1)
                                pb = ps_p[g_ % 2]
                                for c in range(8):
                                    i_ = t.matmul(
                                        pb[:],
                                        wqk[j % 2][:, c * 128:(c + 1) * 128],
                                        at[c][:, tch * 512:(tch + 1) * 512],
                                        start=(c == 0), stop=(c == 7))
                                g_mm.inc(i_, 1)

                    @blk.vector
                    def _(v):
                        for j in range(2):
                            for c in range(8):
                                k = j * 8 + c
                                v.wait_ge(g_tr.h, trh0 + k + 1)
                                g_te[j % 2].inc(v.tensor_copy(
                                    out=wqk[j % 2][:, c * 128:(c + 1) * 128],
                                    in_=ps_r[c % 2][:].bitcast(BF16)[:, 0:128]),
                                    1)
                            for tch in range(4):
                                g_ = j * 4 + tch
                                v.wait_ge(g_mm.h, mmh0 + g_ + 1)
                                if j == 0:
                                    g_ev.inc(v.tensor_scalar(
                                        out=s3[:, tch * 512:(tch + 1) * 512],
                                        in0=ps_p[g_ % 2][:],
                                        scalar1=bcols[:, hp:hp + 1],
                                        scalar2=None, op0=ALU.add), 1)
                                else:
                                    g_ev.inc(v.tensor_copy(
                                        out=qf[:, tch * 512:(tch + 1) * 512],
                                        in_=ps_p[g_ % 2][:]), 1)

                kv0, ks0 = g_kv.n, g_ks.n
                with nc.Block() as blk:
                    @blk.scalar
                    def _(s):
                        g_ks.inc(s.activation(out=kp[:], in_=s3[:],
                                              func=AF.Sigmoid, scale=-1.0), 1)
                        g_ks.inc(s.activation(out=s1[:], in_=s3[:],
                                              func=AF.Abs), 1)
                        s.wait_ge(g_ks.h, g_ks.n)
                        g_ks.inc(s.activation(out=s2[:], in_=s1[:], func=AF.Exp,
                                              scale=-1.0), 1)
                        s.wait_ge(g_ks.h, g_ks.n)
                        g_ks.inc(s.activation(out=s1[:], in_=s2[:], func=AF.Ln,
                                              bias=1.0), 1)

                    @blk.vector
                    def _(v):
                        def ve(instr):
                            g_kv.inc(instr, 1)
                            v.wait_ge(g_kv.h, g_kv.n)
                        v.wait_ge(g_ks.h, ks0 + 4)
                        ve(v.tensor_scalar_min(out=Lc[:], in0=s3[:],
                                               scalar1=0.0))
                        ve(v.tensor_tensor(out=s2[:], in0=Lc[:], in1=s1[:],
                                           op=ALU.subtract))
                        ve(v.tensor_tensor_scan(out=Lc[:], data0=s2[:],
                                                data1=s2[:], initial=0.0,
                                                op0=ALU.add, op1=ALU.bypass))

                bd0, ks1 = g_bd.n, g_ks.n
                with nc.Block() as blk:
                    @blk.vector
                    def _(v):
                        def ve(instr):
                            g_bd.inc(instr, 1)
                            v.wait_ge(g_bd.h, g_bd.n)
                        for li in range(4):
                            if li < 3:
                                ell = LEV[li]
                                nb = T // (2 * ell)
                                lmv = Lc[:, ell - 1::2 * ell]
                                lmv = lmv.unsqueeze(2).broadcast_to(
                                    [128, nb, 2 * ell])
                                d3 = s1[:].rearrange("p (b w) -> p b w",
                                                     w=2 * ell)
                                l3 = Lc[:].rearrange("p (b w) -> p b w",
                                                     w=2 * ell)
                            else:
                                nb = T // 16
                                lmv = Lc[:, 0::16]
                                lmv = lmv.unsqueeze(2).broadcast_to(
                                    [128, nb, 16])
                                d3 = s1[:].rearrange("p (b w) -> p b w", w=16)
                                l3 = Lc[:].rearrange("p (b w) -> p b w", w=16)
                            ve(v.tensor_tensor(out=d3, in0=l3, in1=lmv,
                                               op=ALU.subtract))
                            ve(v.tensor_scalar_min(out=s2[:], in0=s1[:],
                                                   scalar1=0.0))
                            v.wait_ge(g_ks.h, ks1 + 2 * li + 1)
                            ve(v.tensor_tensor(out=qlv[li][:], in0=qf[:],
                                               in1=s3[:], op=ALU.mult))
                            if li < 3:
                                ve(v.tensor_scalar_max(out=s2[:], in0=s1[:],
                                                       scalar1=0.0))
                            else:
                                ve(v.tensor_copy(out=s2[:], in_=s1[:]))
                            v.wait_ge(g_ks.h, ks1 + 2 * li + 2)
                            ve(v.tensor_tensor(out=klv[li][:], in0=kp[:],
                                               in1=s3[:], op=ALU.mult))

                    @blk.scalar
                    def _(s):
                        for li in range(4):
                            s.wait_ge(g_bd.h, bd0 + li * 5 + 2)
                            g_ks.inc(s.activation(out=s3[:], in_=s2[:],
                                                  func=AF.Exp), 1)
                            s.wait_ge(g_bd.h, bd0 + li * 5 + 4)
                            g_ks.inc(s.activation(out=s3[:], in_=s2[:],
                                                  func=AF.Exp, scale=-1.0), 1)

                with nc.Block() as blk:
                    @blk.vector
                    def _(v):
                        for h2 in range(2):
                            v.memset(Hst[h2][0][:], 0.0)
                        v.memset(Hbf2[:], 0.0)

                rm0, rv0, rs0 = g_rm.n, g_rv.n, g_rs.n
                with nc.Block() as blk:
                    @blk.scalar
                    def _(s):
                        for ch in range(NCH):
                            for h2 in range(2):
                                k = ch * 2 + h2
                                hb = h2 * 64
                                s.wait_ge(g_rv.h, rv0 + k * 11 + 1)
                                g_rs.inc(s.activation(out=ecs[hb:hb + 64, :],
                                                      in_=dcs[hb:hb + 64, :],
                                                      func=AF.Exp), 1)
                                s.wait_ge(g_rv.h, rv0 + k * 11 + 3)
                                g_rs.inc(s.activation(out=ecs[hb:hb + 64, :],
                                                      in_=dcs[hb:hb + 64, :],
                                                      func=AF.Exp, scale=-1.0),
                                         1)
                                s.wait_ge(g_rv.h, rv0 + k * 11 + 5)
                                g_rs.inc(s.activation(out=pcc[:, 1:2],
                                                      in_=pcc[:, 0:1],
                                                      func=AF.Exp), 1)

                    @blk.vector
                    def _(v):
                        for ch in range(NCH):
                            cs, ce = ch * CH, (ch + 1) * CH
                            for h2 in range(2):
                                k = ch * 2 + h2
                                hb = h2 * 64
                                v.wait_ge(g_rv.h, g_rv.n)
                                rc_ = max(cs - 1, 0)
                                g_rv.inc(v.tensor_scalar(
                                    out=dcs[hb:hb + 64, :],
                                    in0=Lc[hb:hb + 64, cs:ce],
                                    scalar1=Lc[hb:hb + 64, rc_:rc_ + 1],
                                    scalar2=None, op0=ALU.subtract), 1)
                                v.wait_ge(g_rs.h, rs0 + k * 3 + 1)
                                if k >= 1:
                                    v.wait_ge(g_rm.h, rm0 + (k - 1) * 11 + 1)
                                g_rv.inc(v.tensor_tensor(
                                    out=qcs[hb:hb + 64, :],
                                    in0=qf[hb:hb + 64, cs:ce],
                                    in1=ecs[hb:hb + 64, :], op=ALU.mult), 1)
                                v.wait_ge(g_rv.h, g_rv.n)
                                g_rv.inc(v.tensor_scalar(
                                    out=dcs[hb:hb + 64, :],
                                    in0=Lc[hb:hb + 64, cs:ce],
                                    scalar1=Lc[hb:hb + 64, ce - 1:ce],
                                    scalar2=None, op0=ALU.subtract), 1)
                                v.wait_ge(g_rs.h, rs0 + k * 3 + 2)
                                if k >= 1:
                                    v.wait_ge(g_rm.h, rm0 + (k - 1) * 11 + 10)
                                g_rv.inc(v.tensor_tensor(
                                    out=khf[hb:hb + 64, :],
                                    in0=kp[hb:hb + 64, cs:ce],
                                    in1=ecs[hb:hb + 64, :], op=ALU.mult), 1)
                                v.wait_ge(g_rv.h, g_rv.n)
                                g_rv.inc(v.tensor_tensor(
                                    out=pcc[:, 0:1],
                                    in0=Lc[hb:hb + 64, ce - 1:ce],
                                    in1=Lc[hb:hb + 64, rc_:rc_ + 1],
                                    op=ALU.subtract), 1)
                                v.wait_ge(g_rm.h, rm0 + k * 11 + 9)
                                v.wait_ge(g_rv.h, g_rv.n)
                                g_rv.inc(v.select(
                                    out=rsel[:], mask=mcau[:],
                                    on_true=ps_r[k % 2][:, 0:CH],
                                    on_false=zero_b[:], add_drain=True), 1)
                                v.wait_ge(g_rm.h, rm0 + k * 11 + 10)
                                g_rv.inc(v.tensor_copy(
                                    out=khtm[:],
                                    in_=ps_t[:].bitcast(BF16)[:, 0:64]), 1)
                                v.wait_ge(g_rm.h, rm0 + k * 11 + 11)
                                v.wait_ge(g_rs.h, rs0 + k * 3 + 3)
                                cur = ch % 2
                                g_rv.inc(v.tensor_scalar(
                                    out=Htmp[:],
                                    in0=Hst[h2][cur][:],
                                    scalar1=pcc[:, 1:2], scalar2=None,
                                    op0=ALU.mult), 1)
                                v.wait_ge(g_rv.h, g_rv.n)
                                g_rv.inc(v.tensor_tensor(
                                    out=Hst[h2][1 - cur][:],
                                    in0=Htmp[:],
                                    in1=ps_h[:, 0:64], op=ALU.add), 1)
                                v.wait_ge(g_rv.h, g_rv.n)
                                g_rv.inc(v.tensor_copy(
                                    out=Hbf2[hb:hb + 64, :],
                                    in_=Hst[h2][1 - cur][:]), 1)
                                v.wait_ge(g_rv.h, g_rv.n)
                                g_rv.inc(v.tensor_copy(
                                    out=Yf[hp][hb:hb + 64, cs:ce],
                                    in_=ps_y[k % 2][0:64, 0:CH]), 1)

                    @blk.tensor
                    def _(t):
                        for ch in range(NCH):
                            cs, ce = ch * CH, (ch + 1) * CH
                            for h2 in range(2):
                                k = ch * 2 + h2
                                hb = h2 * 64
                                hgl = (hp * 2 + h2) * 64
                                t.wait_ge(g_rv.h, rv0 + k * 11 + 2)
                                g_rm.inc(t.matmul(ps_y[k % 2][0:64, 0:CH],
                                                  Hbf2[hb:hb + 64, :],
                                                  qcs[hb:hb + 64, :],
                                                  start=True, stop=False), 1)
                                g_rm.inc(t.matmul(
                                    ps_r[k % 2][:, 0:CH],
                                    klv[3][hb:hb + 64, cs:ce],
                                    qlv[3][hb:hb + 64, cs:ce],
                                    start=True, stop=True), 1)
                                for kk in range(4):
                                    g_rm.inc(t.matmul(
                                        ps_r[k % 2][32 * kk:32 * kk + 16,
                                                    32 * kk + 16:32 * kk + 32],
                                        klv[0][hb:hb + 64,
                                               cs + 32 * kk:cs + 32 * kk + 16],
                                        qlv[0][hb:hb + 64, cs + 32 * kk + 16:
                                               cs + 32 * kk + 32],
                                        start=True, stop=True,
                                        tile_position=(hb, 32 * kk)), 1)
                                for kk in range(2):
                                    g_rm.inc(t.matmul(
                                        ps_r[k % 2][64 * kk:64 * kk + 32,
                                                    64 * kk + 32:64 * kk + 64],
                                        klv[1][hb:hb + 64,
                                               cs + 64 * kk:cs + 64 * kk + 32],
                                        qlv[1][hb:hb + 64, cs + 64 * kk + 32:
                                               cs + 64 * kk + 64],
                                        start=True, stop=True), 1)
                                g_rm.inc(t.matmul(
                                    ps_r[k % 2][0:64, 64:128],
                                    klv[2][hb:hb + 64, cs:cs + 64],
                                    qlv[2][hb:hb + 64, cs + 64:cs + 128],
                                    start=True, stop=True), 1)
                                t.wait_ge(g_rv.h, rv0 + k * 11 + 4)
                                g_rm.inc(t.transpose(
                                    ps_t[:].bitcast(BF16)[:, 0:64],
                                    khf[hb:hb + 64, :],
                                    C.ident_b[hb:hb + 64, hb:hb + 64]), 1)
                                t.wait_ge(g_rv.h, rv0 + k * 11 + 7)
                                t.matmul(ps_y[k % 2][0:64, 0:CH],
                                         vt[ch][:, hgl:hgl + 64],
                                         rsel[:], start=False, stop=True)
                                g_rm.inc(t.matmul(
                                    ps_h[:, 0:64], khtm[:],
                                    vt[ch][:, hgl:hgl + 64],
                                    start=True, stop=True), 1)

            mmE0, evE0 = e_mm.n, e_ev.n
            trE0 = g_tr.n
            teE0 = [g_te[0].n, g_te[1].n]
            soE0 = [e_so[0].n, e_so[1].n]
            wp_ready = []
            with nc.Block() as blk:
                @blk.gpsimd
                def _(g):
                    def wpdma(o8):
                        wp_ready.append(g_w[o8 % 2].inc(g.dma_start(
                            out=wstg[o8 % 2][:],
                            in_=outw_ap[o8 * 128:(o8 + 1) * 128, :]), 16))
                    wpdma(0)
                    wpdma(1)
                    for o8 in range(8):
                        for tch in range(4):
                            j = o8 * 4 + tch
                            g.wait_ge(e_ev.h, evE0 + j + 1)
                            e_so[j % 2].inc(g.dma_start(
                                out=s_ap[o8 * 128:(o8 + 1) * 128,
                                         t0 + tch * 512:t0 + (tch + 1) * 512],
                                in_=sst[j % 2][:]), 16)
                        if o8 + 2 < 8:
                            g.wait_ge(e_mm.h, mmE0 + (o8 + 1) * 4)
                            wpdma(o8 + 2)

                @blk.tensor
                def _(t):
                    for o8 in range(8):
                        t.wait_ge(g_w[o8 % 2].h, wp_ready[o8])
                        if o8 >= 1:
                            t.wait_ge(g_te[(o8 - 1) % 2].h,
                                      teE0[(o8 - 1) % 2] + ((o8 - 1) // 2 + 1) * 8)
                        for c in range(8):
                            kq = o8 * 8 + c
                            if c >= 2:
                                t.wait_ge(g_te[o8 % 2].h,
                                          teE0[o8 % 2] + (o8 // 2) * 8 + c - 1)
                            g_tr.inc(t.transpose(
                                ps_r[c % 2][:].bitcast(BF16)[:, 0:128],
                                wstg[o8 % 2][:, c * 128:(c + 1) * 128],
                                C.ident_b[:]), 1)
                        t.wait_ge(g_te[o8 % 2].h, teE0[o8 % 2] + (o8 // 2 + 1) * 8)
                        for tch in range(4):
                            j = o8 * 4 + tch
                            if j >= 2:
                                t.wait_ge(e_ev.h, evE0 + j - 1)
                            pb = ps_p[j % 2]
                            for c in range(8):
                                i_ = t.matmul(
                                    pb[:], wqk[o8 % 2][:, c * 128:(c + 1) * 128],
                                    Yf[c][:, tch * 512:(tch + 1) * 512],
                                    start=(c == 0), stop=(c == 7))
                            e_mm.inc(i_, 1)

                @blk.vector
                def _(v):
                    for o8 in range(8):
                        for c in range(8):
                            kq = o8 * 8 + c
                            v.wait_ge(g_tr.h, trE0 + kq + 1)
                            g_te[o8 % 2].inc(v.tensor_copy(
                                out=wqk[o8 % 2][:, c * 128:(c + 1) * 128],
                                in_=ps_r[c % 2][:].bitcast(BF16)[:, 0:128]), 1)
                        for tch in range(4):
                            j = o8 * 4 + tch
                            v.wait_ge(e_mm.h, mmE0 + j + 1)
                            v.wait_ge(e_so[j % 2].h, soE0[j % 2] + (j // 2) * 16)
                            e_ev.inc(v.tensor_scalar(
                                out=sst[j % 2][:], in0=ps_p[j % 2][:],
                                scalar1=bcols[:, 8 + o8:9 + o8], scalar2=None,
                                op0=ALU.add), 1)

        with nc.Block() as blk:
            @blk.gpsimd
            def _(g):
                g.wait_ge(e_so[0].h, e_so[0].n)
                g.wait_ge(e_so[1].h, e_so[1].n)



AF32 = np.float32
ABF = ml_dtypes.bfloat16
AFM = [D, N]

_WSPECS = [
    ("wa",   "L0_attn_W_attn", [D, 3 * D], 0),
    ("ba",   "L0_attn_b_attn", [3 * D], 0),
    ("wp",   "L0_attn_W_proj", [D, D], 0),
    ("bp",   "L0_attn_b_proj", [D], 0),
    ("m0w1", "L0_ffn_W1", [8, D, 4096], 0),
    ("m0b1", "L0_ffn_b1", [8, 4096], 0),
    ("m0w2", "L0_ffn_W2", [8, 4096, D], 0),
    ("m0b2", "L0_ffn_b2", [8, D], 0),
    ("wq1",  "L1_attn_W_Q", [D, D], 0),
    ("wk1",  "L1_attn_W_K", [D, D], 0),
    ("wv1",  "L1_attn_W_V", [D, D], 0),
    ("wo1",  "L1_attn_W_O", [D, D], 0),
    ("n1wc", "L1_ffn_Wc", [D, 32, 4], 0),
    ("n1bc", "L1_ffn_bc", [32, 4], 0),
    ("n1w1", "L1_ffn_W1", [32, 4, D, 32], 0),
    ("n1b1", "L1_ffn_b1", [32, 4, 32], 0),
    ("n1w2", "L1_ffn_W2", [32, 4, D, 32], 0),
    ("n1b2", "L1_ffn_b2", [32, 4, D], 0),
    ("kw2",  "L2_attn_key_w", [D, D], 0),
    ("kb2",  "L2_attn_key_b", [D], 0),
    ("vq2",  "L2_attn_vq_w", [2 * D, D], 0),
    ("ow2",  "L2_attn_out_w", [D, D], 0),
    ("ob2",  "L2_attn_out_b", [D], 0),
    ("lg2",  "L2_ffn_ln_gamma", [D], 0),
    ("lb2",  "L2_ffn_ln_beta", [D], 0),
    ("f2q1", "L2_ffn_fc1_q", [D, 4096], 1),
    ("f2s1", "L2_ffn_fc1_scales", [8, 4096], 0),
    ("f2b1", "L2_ffn_fc1_bias", [4096], 0),
    ("f2q2", "L2_ffn_fc2_q", [4096, D], 1),
    ("f2s2", "L2_ffn_fc2_scales", [32, D], 0),
    ("f2b2", "L2_ffn_fc2_bias", [D], 0),
    ("win3", "L3_attn_W_in", [D, 4096], 0),
    ("cw3",  "L3_attn_conv_w", [2048, 4], 0),
    ("cb3",  "L3_attn_conv_b", [2048], 0),
    ("wx3",  "L3_attn_W_x", [2048, 96], 0),
    ("wdt3", "L3_attn_W_dt", [64, 2048], 0),
    ("bdt3", "L3_attn_b_dt", [2048], 0),
    ("alog3", "L3_attn_A_log", [2048, 16], 0),
    ("dv3",  "L3_attn_D", [2048], 0),
    ("wout3", "L3_attn_W_out", [2048, D], 0),
    ("gw3",  "L3_ffn_gate_w", [D, 8], 0),
    ("gb3",  "L3_ffn_gate_b", [8], 0),
    ("m3w1", "L3_ffn_W1", [8, D, 4096], 0),
    ("m3b1", "L3_ffn_b1", [8, 4096], 0),
    ("m3w2", "L3_ffn_W2", [8, 4096, D], 0),
    ("m3b2", "L3_ffn_b2", [8, D], 0),
]


def build_fused():
    nc = bass.Bass("TRN2", target_bir_lowering=False)
    i = {"x": nc.dram_tensor("x", [N, D], F32, kind="ExternalInput").ap()}
    for bn, _k, sh, isint in _WSPECS:
        i[bn] = nc.dram_tensor(bn, sh, I32 if isint else F32,
                               kind="ExternalInput").ap()
    out = nc.dram_tensor("xo", [N, D], F32, kind="ExternalOutput").ap()
    xfm = [nc.dram_tensor(f"xfm{p}", AFM, F32, kind="Internal").ap()
           for p in range(2)]
    xn = nc.dram_tensor("xn_d", AFM, BF16, kind="Internal").ap()
    xn2 = nc.dram_tensor("xn2_d", AFM, BF16, kind="Internal").ap()
    sf = nc.dram_tensor("sf_d", AFM, F32, kind="Internal").ap()
    xr = nc.dram_tensor("xr_d", [1, N], F32, kind="Internal").ap()

    with contextlib.ExitStack() as stack:
        C = Ctx(nc, stack)
        common_setup(C)
        emit_pro(C, i["x"], xfm[0])
        cur = 0
        for li in range(4):
            a, b = xfm[cur], xfm[1 - cur]
            emit_ln(C, f"l{li}n", a, xn, xr if li == 0 else None)
            if li == 0:
                emit_attn0(C, "a0", xn, i["wa"][:, 0:1024],
                           i["wa"][:, 1024:2048], i["wa"][:, 2048:3072],
                           i["wp"], i["ba"], i["bp"], sf)
                emit_moe(C, "m0", xn, i["m0w1"], i["m0b1"], i["m0w2"],
                         i["m0b2"], a, sf, b, "relu", mk_routing_l0(xr))
            elif li == 1:
                emit_attn0(C, "a1", xn, i["wq1"], i["wk1"], i["wv1"],
                           i["wo1"], None, None, sf, wt_trans=True, rope=True)
                emit_ffn1(C, "n1", xn, i["n1wc"], i["n1bc"], i["n1w1"],
                          i["n1b1"], i["n1w2"], i["n1b2"], a, sf, b)
            elif li == 2:
                emit_attn2(C, "g2", xn, i["kw2"], i["kb2"], i["vq2"],
                           i["ow2"], i["ob2"], sf)
                emit_ln(C, "ln2b", xn, xn2, None, in_bf=True,
                        gamma_ap=i["lg2"], beta_ap=i["lb2"])
                emit_ffn2(C, "f2", xn, xn2, i["f2q1"], i["f2s1"], i["f2b1"],
                          i["f2q2"], i["f2s2"], i["f2b2"], a, sf, b)
            else:
                emit_attn3(C, "a3", xn, i["win3"], i["cw3"], i["cb3"],
                           i["wx3"], i["wdt3"], i["bdt3"], i["alog3"],
                           i["dv3"], i["wout3"], sf)
                emit_moe(C, "m3", xn, i["m3w1"], i["m3b1"], i["m3w2"],
                         i["m3b2"], a, sf, b, "silu",
                         mk_routing_l3(xn, i["gw3"], i["gb3"]))
            cur = 1 - cur
        emit_epi(C, xfm[cur], out)
    return nc


def kernel(**inp):
    x = np.ascontiguousarray(np.asarray(inp["L0_attn_x"], AF32))
    xs = [np.ascontiguousarray(x[2 * c:2 * c + 2].reshape(N, D))
          for c in range(NCORES)]
    wmap = {}
    for bn, k, _sh, isint in _WSPECS:
        wmap[bn] = np.ascontiguousarray(
            np.asarray(inp[k], np.int32 if isint else AF32))
    nc = build_fused()
    feeds = [dict(wmap, x=xs[c]) for c in range(NCORES)]
    res = run_bass_kernel_spmd(nc, feeds, core_ids=list(range(NCORES)))
    out = np.concatenate([np.asarray(rc["xo"]).reshape(SPC, T, D)
                          for rc in res.results], axis=0)
    return out


# version 14
# speedup vs baseline: 1.0219x; 1.0219x over previous
import contextlib
import numpy as np
import ml_dtypes
import concourse.bass as bass
import concourse.mybir as mybir
from concourse.bass_utils import run_bass_kernel_spmd

F32 = mybir.dt.float32
BF16 = mybir.dt.bfloat16
I32 = mybir.dt.int32
U8 = mybir.dt.uint8
AF = mybir.ActivationFunctionType
ALU = mybir.AluOpType
AX = mybir.AxisListType

NCORES = 8
B = 16
T = 2048
D = 1024
SPC = B // NCORES
N = SPC * T
NH = 16
HD = 64
TC = 512
NTC = N // TC
DCH = D // 128

EXP_SHIFT = 16.0



class Sem:
    def __init__(self, h):
        self.h = h
        self.n = 0

    def inc(self, instr, k=1):
        instr.then_inc(self.h, k)
        self.n += k
        return self.n


SEM_ALIAS = {
    "a_ev": "PS5", "a_lb": "PS1", "a_ld": "PS0", "a_mm": "PS4",
    "a_ro": "PS6", "a_te0": "PS14", "a_te1": "PS15", "a_tp": "PS2",
    "a_tr": "PS3", "a_w": "PG0", "a_w0": "PG1", "a_w1": "PG2", "d_ex": "PS8",
    "d_ly": "PS10", "d_mm": "PS12", "d_no": "PS11", "d_pr": "PS9",
    "d_s": "PS13", "d_sc": "PS7", "d_v": "PS14", "e_ev": "PS13",
    "e_mm": "PS12", "e_so0": "PG3", "e_so1": "PG4", "epi_ev": "PS2",
    "epi_ld": "PS0", "epi_so": "PG0", "epi_tr": "PS1", "f2_bc": "PS4",
    "f2_bl": "PS1", "f2_bt": "PS2", "f2_cc": "PS12", "f2_cg": "PS11",
    "f2_cl0": "PS13", "f2_cl1": "PS14", "f2_cv": "PS3", "f2_dq": "PS5",
    "f2_gr": "PS10", "f2_h1": "PS7", "f2_hs": "PS8", "f2_ld": "PS0",
    "f2_p2": "PS9", "f2_so0": "PG1", "f2_so1": "PG2", "f2_sr": "PG0",
    "f2_xl": "PS6", "g_bd": "PS8", "g_bl": "PS2", "g_bt": "PS3",
    "g_eev": "PS14", "g_emm": "PS13", "g_es0": "PG2", "g_es1": "PG3",
    "g_ev": "PS5", "g_init": "PS0", "g_ks": "PS7", "g_kv": "PS6",
    "g_ld": "PS0", "g_mm": "PS4", "g_ms": "PS12", "g_rm": "PS9",
    "g_rs": "PS11", "g_rv": "PS10", "g_te0": "PS15", "g_te1": "PS16",
    "g_tr": "PS1", "g_w0": "PG0", "g_w1": "PG1", "ln_ap": "PS6",
    "ln_bc": "PS4", "ln_cb": "PS1", "ln_gb": "PS8", "ln_gbt": "PS9",
    "ln_ld": "PS0", "ln_mm": "PS2", "ln_rs": "PS5", "ln_rv": "PS7",
    "ln_so": "PG0", "ln_st": "PS3", "m_ac": "PS9", "m_b": "PS1",
    "m_b2": "PG0", "m_bt": "PS2", "m_cg": "PS10", "m_cv": "PS11",
    "m_h1": "PS5", "m_hs": "PS6", "m_hv": "PS7", "m_ld": "PS0",
    "m_p2": "PS8", "m_so0": "PG2", "m_so1": "PG3", "m_w10": "PS12",
    "m_w11": "PS13", "m_w2": "PG1", "m_wb": "PS3", "m_wv": "PS4",
    "m_xl0": "PS14", "m_xl1": "PS15", "n_cc": "PS14", "n_cg": "PS13",
    "n_cl0": "PS15", "n_cl1": "PS16", "n_gr": "PS12", "n_gv": "PS10",
    "n_io": "PS3", "n_ld": "PS4", "n_lg": "PS5", "n_lv": "PS6",
    "n_pre": "PS0", "n_prg": "PG0", "n_prw": "PG1", "n_px": "PS9",
    "n_rm": "PS11", "n_so0": "PG4", "n_so1": "PG5", "n_te": "PS2",
    "n_tr": "PS1", "n_w10": "PG2", "n_w11": "PG3", "n_y1": "PS7",
    "n_ys": "PS8", "pro_ev": "PS2", "pro_ld": "PS0", "pro_so": "PG0",
    "pro_tr": "PS1", "q_c": "PS20", "q_gw": "PG5", "q_ld": "PS18",
    "q_mm": "PS19", "q_so": "PG6", "q_tr": "PS21", "r_c": "PS17",
    "r_ld": "PS16", "r_so": "PG4", "z_ae": "PS4", "z_b2": "PS3",
    "z_bt": "PS2", "z_cv": "PS8", "z_ev": "PS6", "z_ga": "PS21",
    "z_ld": "PS0", "z_mm": "PS5", "z_oe": "PS23", "z_om": "PS22",
    "z_pb": "PS15", "z_pe": "PS16", "z_rs": "PS7", "z_sc": "PS18",
    "z_si": "PS9", "z_sl": "PS20", "z_so0": "PG2", "z_so1": "PG3",
    "z_sp": "PS19", "z_sr": "PG1", "z_ss": "PS17", "z_w0": "PS24",
    "z_w1": "PS25", "z_wg": "PG0", "z_wx": "PS1", "z_xd": "PS10",
    "z_xe": "PS11",
}


class Ctx:
    def __init__(self, nc, stack):
        self.nc = nc
        self.stack = stack
        self._sems = {}

    def sem(self, name):
        name = SEM_ALIAS.get(name, name)
        if name not in self._sems:
            h = self.stack.enter_context(self.nc.semaphore(name))
            self._sems[name] = Sem(h)
        return self._sems[name]

    def sbuf(self, name, shape, dt):
        return self.stack.enter_context(self.nc.sbuf_tensor(name, shape, dt))


CONSTS = [1e-5, -EXP_SHIFT, 0.5, -128.0]


def common_setup(C):
    nc = C.nc
    C.ident_f = C.sbuf("g_ident_f", [128, 128], F32)
    C.ident_b = C.sbuf("g_ident_b", [128, 128], BF16)
    C.ones_f = C.sbuf("g_ones_f", [128, 128], F32)
    C.ones_b = C.sbuf("g_ones_b", [128, 128], BF16)
    C.ones512_b = C.sbuf("g_ones512b", [128, 512], BF16)
    C.cmask = [C.sbuf(f"g_cmask{i}", [128, 512], BF16) for i in range(4)]
    C.cvals = {}
    for v in CONSTS:
        t = C.sbuf(f"g_const_{len(C.cvals)}", [128, 1], F32)
        nc.const_aps.register(F32, v, t[:])
        C.cvals[v] = t
    ini = C.sem("g_init")
    with nc.Block() as blk:
        @blk.gpsimd
        def _(g):
            for v, t in C.cvals.items():
                ini.inc(g.memset(t[:], v), 1)
            ini.inc(g.memset(C.ones_f[:], 1.0), 1)
            ini.inc(g.memset(C.ones_b[:], 1.0), 1)
            ini.inc(g.memset(C.ones512_b[:], 1.0), 1)
            g.wait_ge(ini.h, ini.n)
            g.affine_select(out=C.ident_f[:], in_=C.ones_f[:], pattern=[[1, 128]],
                            compare_op=ALU.is_equal, fill=0.0, base=0,
                            channel_multiplier=-1)
            g.affine_select(out=C.ident_b[:], in_=C.ones_b[:], pattern=[[1, 128]],
                            compare_op=ALU.is_equal, fill=0.0, base=0,
                            channel_multiplier=-1)
            for o in range(4):
                g.affine_select(out=C.cmask[o][:], in_=C.ones512_b[:],
                                pattern=[[1, 512]], compare_op=ALU.is_ge,
                                fill=0.0, base=-128 * o, channel_multiplier=-1)



def emit_pro(C, x_ap, xfm_ap):
    nc = C.nc
    with contextlib.ExitStack() as st:
        E = st.enter_context
        ld = C.sem("pro_ld")
        tr = C.sem("pro_tr")
        ev = C.sem("pro_ev")
        so = C.sem("pro_so")
        ld0, tr0, ev0, so0 = ld.n, tr.n, ev.n, so.n
        xt = [E(nc.sbuf_tensor(f"pro_x{i}", [128, D], F32)) for i in range(8)]
        stg = [E(nc.sbuf_tensor(f"pro_s{c}", [128, 1024], F32)) for c in range(8)]
        pp = [E(nc.psum_tensor(f"pro_p{i}", [128, 512], F32)) for i in range(2)]
        NG = N // 1024

        with nc.Block() as blk:
            @blk.sync
            def _(sy):
                for g_ in range(NG):
                    if g_ > 0:
                        sy.wait_ge(so.h, so0 + g_ * 8 * 16)
                        sy.wait_ge(tr.h, tr0 + g_ * 64)
                    for i in range(8):
                        ld.inc(sy.dma_start(
                            out=xt[i][:],
                            in_=x_ap[g_ * 1024 + i * 128:g_ * 1024 + (i + 1) * 128, :]),
                            16)

            @blk.tensor
            def _(t):
                k = 0
                for g_ in range(NG):
                    t.wait_ge(ld.h, ld0 + (g_ + 1) * 128)
                    for i in range(8):
                        for c in range(8):
                            t.wait_ge(ev.h, ev0 + max(0, k - 1))
                            tr.inc(t.transpose(pp[k % 2][:, 0:128],
                                               xt[i][:, c * 128:(c + 1) * 128],
                                               C.ident_f[:]), 1)
                            k += 1

            @blk.vector
            def _(v):
                k = 0
                for g_ in range(NG):
                    for i in range(8):
                        for c in range(8):
                            v.wait_ge(tr.h, tr0 + k + 1)
                            ev.inc(v.tensor_copy(out=stg[c][:, i * 128:(i + 1) * 128],
                                                 in_=pp[k % 2][:, 0:128]), 1)
                            k += 1

            @blk.gpsimd
            def _(g):
                for g_ in range(NG):
                    g.wait_ge(ev.h, ev0 + (g_ + 1) * 64)
                    for c in range(8):
                        so.inc(g.dma_start(
                            out=xfm_ap[c * 128:(c + 1) * 128,
                                       g_ * 1024:(g_ + 1) * 1024],
                            in_=stg[c][:]), 16)
                g.wait_ge(so.h, so.n)



def emit_epi(C, xfm_ap, out_ap):
    nc = C.nc
    with contextlib.ExitStack() as st:
        E = st.enter_context
        ld = C.sem("epi_ld")
        tr = C.sem("epi_tr")
        ev = C.sem("epi_ev")
        so = C.sem("epi_so")
        ld0, tr0, ev0, so0 = ld.n, tr.n, ev.n, so.n
        xt = [E(nc.sbuf_tensor(f"epi_x{c}", [128, 1024], F32)) for c in range(8)]
        stg = [E(nc.sbuf_tensor(f"epi_s{i}", [128, 1024], F32)) for i in range(8)]
        pp = [E(nc.psum_tensor(f"epi_p{i}", [128, 512], F32)) for i in range(2)]
        NG = N // 1024

        with nc.Block() as blk:
            @blk.sync
            def _(sy):
                for g_ in range(NG):
                    if g_ > 0:
                        sy.wait_ge(so.h, so0 + g_ * 8 * 16)
                        sy.wait_ge(tr.h, tr0 + g_ * 64)
                    for c in range(8):
                        ld.inc(sy.dma_start(
                            out=xt[c][:],
                            in_=xfm_ap[c * 128:(c + 1) * 128,
                                       g_ * 1024:(g_ + 1) * 1024]), 16)

            @blk.tensor
            def _(t):
                k = 0
                for g_ in range(NG):
                    t.wait_ge(ld.h, ld0 + (g_ + 1) * 128)
                    for i in range(8):
                        for c in range(8):
                            t.wait_ge(ev.h, ev0 + max(0, k - 1))
                            tr.inc(t.transpose(pp[k % 2][:, 0:128],
                                               xt[c][:, i * 128:(i + 1) * 128],
                                               C.ident_f[:]), 1)
                            k += 1

            @blk.vector
            def _(v):
                k = 0
                for g_ in range(NG):
                    for i in range(8):
                        for c in range(8):
                            v.wait_ge(tr.h, tr0 + k + 1)
                            ev.inc(v.tensor_copy(out=stg[i][:, c * 128:(c + 1) * 128],
                                                 in_=pp[k % 2][:, 0:128]), 1)
                            k += 1

            @blk.gpsimd
            def _(g):
                for g_ in range(NG):
                    g.wait_ge(ev.h, ev0 + (g_ + 1) * 64)
                    for i in range(8):
                        so.inc(g.dma_start(
                            out=out_ap[g_ * 1024 + i * 128:
                                       g_ * 1024 + (i + 1) * 128, :],
                            in_=stg[i][:]), 16)
                g.wait_ge(so.h, so.n)



def emit_ln(C, tag, xfm_ap, xn_ap, xr_ap=None, in_bf=False,
            gamma_ap=None, beta_ap=None):
    nc = C.nc
    with contextlib.ExitStack() as st:
        E = st.enter_context
        ld = C.sem("ln_ld")
        cb = C.sem("ln_cb")
        mm = C.sem("ln_mm")
        stt = C.sem("ln_st")
        bc = C.sem("ln_bc")
        rs = C.sem("ln_rs")
        ap = C.sem("ln_ap")
        rv = C.sem("ln_rv")
        so = C.sem("ln_so")
        ld0, cb0, mm0, st0, bc0, rs0, ap0, rv0, so0 = (
            ld.n, cb.n, mm.n, stt.n, bc.n, rs.n, ap.n, rv.n, so.n)
        DTI = BF16 if in_bf else F32
        xt = [E(nc.sbuf_tensor(f"{tag}_x{c}", [128, TC], DTI)) for c in range(8)]
        sqf = [E(nc.sbuf_tensor(f"{tag}_sq{c}", [128, TC], DTI)) for c in range(8)]
        if gamma_ap is not None:
            gbr_ = E(nc.sbuf_tensor(f"{tag}_gbr", [16, 128], F32))
            gbc = E(nc.sbuf_tensor(f"{tag}_gbc", [128, 16], F32))
        xnb = [E(nc.sbuf_tensor(f"{tag}_xn{c}", [128, TC], BF16)) for c in range(8)]
        xf2 = [E(nc.sbuf_tensor(f"{tag}_x2{c}", [128, TC], F32)) for c in range(8)]
        x0f = E(nc.sbuf_tensor(f"{tag}_x0f", [2, TC], F32))
        srow = E(nc.sbuf_tensor(f"{tag}_stat", [1, 12 * TC], F32))
        SL = lambda i: srow[:, i * TC:(i + 1) * TC]
        rrow = E(nc.sbuf_tensor(f"{tag}_rr", [1, TC], F32))
        ps_sum = E(nc.psum_tensor(f"{tag}_psum", [1, TC], F32))
        ps_sq = E(nc.psum_tensor(f"{tag}_psq", [1, TC], F32))
        ps_rst = E(nc.psum_tensor(f"{tag}_prst", [128, TC], F32))
        ps_mrs = E(nc.psum_tensor(f"{tag}_pmrs", [128, TC], F32))
        ps_r = E(nc.psum_tensor(f"{tag}_pr", [1, TC], F32))
        if gamma_ap is not None:
            gb_l = C.sem("ln_gb")
            gb_t = C.sem("ln_gbt")
            with nc.Block() as blk:
                @blk.sync
                def _(sy):
                    gb_l.inc(sy.dma_start(
                        out=gbr_[0:8, :],
                        in_=gamma_ap.rearrange("(a b) -> a b", b=128)), 16)
                    gb_l.inc(sy.dma_start(
                        out=gbr_[8:16, :],
                        in_=beta_ap.rearrange("(a b) -> a b", b=128)), 16)

                @blk.tensor
                def _(t):
                    t.wait_ge(gb_l.h, gb_l.n)
                    gb_t.inc(t.transpose(ps_rst[:, 0:16], gbr_[:],
                                         C.ident_f[0:16, 0:16]), 1)

                @blk.vector
                def _(v):
                    v.wait_ge(gb_t.h, gb_t.n)
                    gb_t.inc(v.tensor_copy(out=gbc[:], in_=ps_rst[:, 0:16]), 1)
        P_AP = 17 if xr_ap is not None else 16
        if gamma_ap is not None:
            P_AP += 8
        P_ST = 12
        P_SO = 9 if xr_ap is not None else 8

        with nc.Block() as blk:
            @blk.sync
            def _(sy):
                for ch in range(NTC):
                    if ch > 0:
                        sy.wait_ge(ap.h, ap0 + ch * P_AP)
                        sy.wait_ge(mm.h, mm0 + ch)
                    for c in range(8):
                        ld.inc(sy.dma_start(out=xt[c][:],
                                            in_=xfm_ap[c * 128:(c + 1) * 128,
                                                       ch * TC:(ch + 1) * TC]), 16)

            @blk.scalar
            def _(s):
                for ch in range(NTC):
                    s.wait_ge(ld.h, ld0 + (ch + 1) * 128)
                    if ch > 0:
                        s.wait_ge(mm.h, mm0 + ch)
                    for c in range(8):
                        cb.inc(s.activation(out=sqf[c][:], in_=xt[c][:],
                                            func=AF.Square), 1)
                    s.wait_ge(stt.h, st0 + ch * P_ST + 4)
                    if ch > 0:
                        s.wait_ge(bc.h, bc0 + ch)
                    stt.inc(s.activation(out=SL(4), in_=SL(3), func=AF.Ln,
                                         bias=1e-5), 1)
                    s.wait_ge(stt.h, st0 + ch * P_ST + 5)
                    stt.inc(s.activation(out=SL(5), in_=SL(4),
                                         func=AF.Exp, scale=-0.5), 1)

            @blk.tensor
            def _(t):
                for ch in range(NTC):
                    if ch > 0:
                        t.wait_ge(stt.h, st0 + (ch - 1) * P_ST + 2)
                    t.wait_ge(ld.h, ld0 + (ch + 1) * 128)
                    for c in range(8):
                        t.wait_ge(cb.h, cb0 + ch * 8 + c + 1)
                        onesv = C.ones_b if in_bf else C.ones_f
                        t.matmul(ps_sum[:], onesv[:, 0:1], xt[c][:],
                                 start=(c == 0), stop=(c == 7))
                        i_ = t.matmul(ps_sq[:], onesv[:, 0:1], sqf[c][:],
                                      start=(c == 0), stop=(c == 7))
                    mm.inc(i_, 1)
                    t.wait_ge(stt.h, st0 + ch * P_ST + 12)
                    if ch > 0:
                        t.wait_ge(ap.h, ap0 + ch * P_AP)
                    t.matmul(ps_rst[:], C.ones_f[0:1, :], SL(11),
                             start=True, stop=True)
                    bc.inc(t.matmul(ps_mrs[:], C.ones_f[0:1, :], SL(6),
                                    start=True, stop=True), 1)
                    if xr_ap is not None:
                        t.wait_ge(ap.h, ap0 + ch * P_AP + 3)
                        if ch > 0:
                            t.wait_ge(rv.h, rv0 + ch)
                        rs.inc(t.matmul(ps_r[:], C.ones_f[0:2, 0:1], x0f[:],
                                        start=True, stop=True), 1)

            @blk.vector
            def _(v):
                for ch in range(NTC):
                    if ch > 0:
                        v.wait_ge(so.h, so0 + ch * P_SO * 16)
                    v.wait_ge(mm.h, mm0 + ch + 1)
                    if ch > 0:
                        v.wait_ge(stt.h, st0 + ch * P_ST)
                    stt.inc(v.tensor_scalar_mul(out=SL(0), in0=ps_sum[:],
                                                scalar1=1.0 / D), 1)
                    stt.inc(v.tensor_scalar_mul(out=SL(1), in0=ps_sq[:],
                                                scalar1=1.0 / D), 1)
                    v.wait_ge(stt.h, st0 + ch * P_ST + 2)
                    stt.inc(v.tensor_tensor(out=SL(2), in0=SL(0), in1=SL(0),
                                            op=ALU.mult), 1)
                    v.wait_ge(stt.h, st0 + ch * P_ST + 3)
                    stt.inc(v.tensor_tensor(out=SL(3), in0=SL(1), in1=SL(2),
                                            op=ALU.subtract), 1)
                    v.wait_ge(stt.h, st0 + ch * P_ST + 6)
                    stt.inc(v.tensor_scalar_add(out=SL(7), in0=SL(3),
                                                scalar1=1e-5), 1)
                    v.wait_ge(stt.h, st0 + ch * P_ST + 7)
                    stt.inc(v.tensor_tensor(out=SL(8), in0=SL(5), in1=SL(5),
                                            op=ALU.mult), 1)
                    v.wait_ge(stt.h, st0 + ch * P_ST + 8)
                    stt.inc(v.tensor_tensor(out=SL(9), in0=SL(7), in1=SL(8),
                                            op=ALU.mult), 1)
                    v.wait_ge(stt.h, st0 + ch * P_ST + 9)
                    stt.inc(v.tensor_scalar(out=SL(10), in0=SL(9), scalar1=-0.5,
                                            scalar2=1.5, op0=ALU.mult,
                                            op1=ALU.add), 1)
                    v.wait_ge(stt.h, st0 + ch * P_ST + 10)
                    stt.inc(v.tensor_tensor(out=SL(11), in0=SL(5), in1=SL(10),
                                            op=ALU.mult), 1)
                    v.wait_ge(stt.h, st0 + ch * P_ST + 11)
                    stt.inc(v.tensor_tensor(out=SL(6), in0=SL(0), in1=SL(11),
                                            op=ALU.mult), 1)
                    v.wait_ge(bc.h, bc0 + ch + 1)
                    for c in range(8):
                        ap.inc(v.tensor_tensor(out=xf2[c][:], in0=xt[c][:],
                                               in1=ps_rst[:], op=ALU.mult), 1)
                        if c == 0 and xr_ap is not None:
                            if ch > 0:
                                v.wait_ge(rs.h, rs0 + ch)
                            v.wait_ge(ap.h, ap0 + ch * P_AP + 1)
                            ap.inc(v.tensor_tensor(out=x0f[:], in0=xf2[0][0:2, :],
                                                   in1=ps_mrs[0:2, :],
                                                   op=ALU.subtract), 1)
                        v.wait_ge(ap.h, ap.n)
                        if gamma_ap is not None:
                            ap.inc(v.tensor_tensor(out=xt[c][:], in0=xf2[c][:],
                                                   in1=ps_mrs[:],
                                                   op=ALU.subtract), 1)
                            v.wait_ge(ap.h, ap.n)
                            ap.inc(v.tensor_scalar(out=xnb[c][:], in0=xt[c][:],
                                                   scalar1=gbc[:, c:c + 1],
                                                   scalar2=gbc[:, 8 + c:9 + c],
                                                   op0=ALU.mult, op1=ALU.add), 1)
                        else:
                            ap.inc(v.tensor_tensor(out=xnb[c][:], in0=xf2[c][:],
                                                   in1=ps_mrs[:],
                                                   op=ALU.subtract), 1)
                    if xr_ap is not None:
                        v.wait_ge(rs.h, rs0 + ch + 1)
                        rv.inc(v.tensor_copy(out=rrow[:], in_=ps_r[:]), 1)

            @blk.gpsimd
            def _(g):
                for ch in range(NTC):
                    g.wait_ge(ap.h, ap0 + (ch + 1) * P_AP)
                    for c in range(8):
                        so.inc(g.dma_start(out=xn_ap[c * 128:(c + 1) * 128,
                                                     ch * TC:(ch + 1) * TC],
                                           in_=xnb[c][:]), 16)
                    if xr_ap is not None:
                        g.wait_ge(rv.h, rv0 + ch + 1)
                        so.inc(g.dma_start(out=xr_ap[:, ch * TC:(ch + 1) * TC],
                                           in_=rrow[:]), 16)
                g.wait_ge(so.h, so.n)



def emit_attn0(C, tag, xn_ap, wq_ap, wk_ap, wv_ap, wo_ap, battn_ap, bproj_ap,
               s_ap, wt_trans=False, rope=False):
    nc = C.nc
    QB = 4
    with contextlib.ExitStack() as st:
        E = st.enter_context
        a_ld = C.sem("a_ld")
        a_lb = C.sem("a_lb")
        a_w = C.sem("a_w")
        a_wq = [C.sem("a_w0"), C.sem("a_w1")]
        a_tp = C.sem("a_tp")
        a_tr = C.sem("a_tr")
        a_te = [C.sem("a_te0"), C.sem("a_te1")]
        a_mm = C.sem("a_mm")
        a_ev = C.sem("a_ev")
        a_ro = C.sem("a_ro")
        d_sc = C.sem("d_sc")
        d_ex = C.sem("d_ex")
        d_pr = C.sem("d_pr")
        d_ly = C.sem("d_ly")
        d_no = C.sem("d_no")
        e_mm = C.sem("e_mm")
        e_ev = C.sem("e_ev")
        e_so = [C.sem("e_so0"), C.sem("e_so1")]
        lb0_, w0_, tp0_ = a_lb.n, a_w.n, a_tp.n

        at = [E(nc.sbuf_tensor(f"{tag}_a{c}", [128, T], BF16)) for c in range(8)]
        qt = [E(nc.sbuf_tensor(f"{tag}_q{c}", [128, T], BF16)) for c in range(8)]
        kt8 = [E(nc.sbuf_tensor(f"{tag}_k{c}", [128, T], BF16)) for c in range(8)]
        vt = [E(nc.sbuf_tensor(f"{tag}_v{i}", [128, D], BF16)) for i in range(16)]
        wv = [E(nc.sbuf_tensor(f"{tag}_wv{c}", [128, D], BF16)) for c in range(8)]
        wqk = [E(nc.sbuf_tensor(f"{tag}_wqk{i}", [128, D], BF16)) for i in range(2)]
        wp = [E(nc.sbuf_tensor(f"{tag}_wp{i}", [128, D], BF16)) for i in range(2)]
        if wt_trans:
            wstg = [E(nc.sbuf_tensor(f"{tag}_ws{i}", [128, D], BF16))
                    for i in range(2)]
        if battn_ap is not None or bproj_ap is not None:
            brawt = E(nc.sbuf_tensor(f"{tag}_braw", [32, 128], F32))
            bcols = E(nc.sbuf_tensor(f"{tag}_bcols", [128, 32], F32))
            vbrow = E(nc.sbuf_tensor(f"{tag}_vbrow", [1, D], BF16))
        if rope:
            ctab = E(nc.sbuf_tensor(f"{tag}_ctab", [128, T], BF16))
            stab = E(nc.sbuf_tensor(f"{tag}_stab", [128, T], BF16))
            rsc = [E(nc.sbuf_tensor(f"{tag}_rsc{i}", [32, T], BF16))
                   for i in range(4)]
        praw = [E(nc.sbuf_tensor(f"{tag}_pr{i}", [128, 512], BF16)) for i in range(2)]
        pmsk = [E(nc.sbuf_tensor(f"{tag}_pm{i}", [128, 512], BF16)) for i in range(2)]
        rsb = [E(nc.sbuf_tensor(f"{tag}_rsb{i}", [64, 512], F32)) for i in range(2)]
        sst = [E(nc.sbuf_tensor(f"{tag}_sst{i}", [128, 512], F32)) for i in range(2)]
        ps_p = [E(nc.psum_tensor(f"{tag}_psp{i}", [128, 512], F32)) for i in range(2)]
        ps_sc = [E(nc.psum_tensor(f"{tag}_pss{i}", [128, 512], F32)) for i in range(2)]
        ps_l = [E(nc.psum_tensor(f"{tag}_psl{i}", [64, 512], F32)) for i in range(2)]
        ps_y = [E(nc.psum_tensor(f"{tag}_psy{i}", [64, 512], F32)) for i in range(2)]

        if rope:
            W_ = 256
            with contextlib.ExitStack() as st2:
                E2 = st2.enter_context
                thc = E2(nc.sbuf_tensor(f"{tag}_thc", [64, 4], F32))
                sa_ = E2(nc.sbuf_tensor(f"{tag}_sa", [64, W_], F32))
                sb_ = E2(nc.sbuf_tensor(f"{tag}_sb", [64, W_], F32))
                sc_ = E2(nc.sbuf_tensor(f"{tag}_sc", [64, W_], F32))
                sd_ = E2(nc.sbuf_tensor(f"{tag}_sd", [64, W_], F32))
                si_ = E2(nc.sbuf_tensor(f"{tag}_si", [64, W_], I32))
                with nc.Block() as blk:
                    @blk.gpsimd
                    def _(g):
                        a_ro.inc(g.iota(thc[:, 0:1], pattern=[[1, 1]], base=0,
                                        channel_multiplier=1,
                                        allow_small_or_imprecise_dtypes=True), 1)

                    @blk.vector
                    def _(v):
                        def ve(instr):
                            a_ro.inc(instr, 1)
                            v.wait_ge(a_ro.h, a_ro.n)
                        v.wait_ge(a_ro.h, a_ro.n)
                        ve(v.tensor_single_scalar(out=thc[:, 1:2],
                                                  in_=thc[:, 0:1], scalar=32.0,
                                                  op=ALU.is_ge))
                        ve(v.tensor_scalar_mul(out=thc[:, 2:3], in0=thc[:, 1:2],
                                               scalar1=32.0))
                        ve(v.tensor_tensor(out=thc[:, 3:4], in0=thc[:, 0:1],
                                           in1=thc[:, 2:3], op=ALU.subtract))

                    @blk.scalar
                    def _(s):
                        import math
                        s.wait_ge(a_ro.h, a_ro.n)
                        a_ro.inc(s.activation(out=thc[:, 0:1], in_=thc[:, 3:4],
                                              func=AF.Exp,
                                              scale=-math.log(10000.0) / 32.0), 1)
                for tch in range(T // W_):
                    import math
                    with nc.Block() as blk:
                        @blk.gpsimd
                        def _(g):
                            a_ro.inc(g.iota(sa_[:], pattern=[[1, W_]],
                                            base=tch * W_, channel_multiplier=0,
                                            allow_small_or_imprecise_dtypes=True),
                                     1)

                        @blk.vector
                        def _(v):
                            def ve(instr):
                                a_ro.inc(instr, 1)
                                v.wait_ge(a_ro.h, a_ro.n)
                            v.wait_ge(a_ro.h, a_ro.n)
                            ve(v.tensor_scalar(out=sb_[:], in0=sa_[:],
                                               scalar1=thc[:, 0:1], scalar2=None,
                                               op0=ALU.mult))
                            ve(v.tensor_scalar_mul(out=sa_[:], in0=sb_[:],
                                                   scalar1=1.0 / (2 * math.pi)))

                    with nc.Block() as blk:
                        @blk.scalar
                        def _(s):
                            a_ro.inc(s.copy(out=si_[:], in_=sa_[:]), 1)
                            s.wait_ge(a_ro.h, a_ro.n)
                            a_ro.inc(s.copy(out=sb_[:], in_=si_[:]), 1)

                    with nc.Block() as blk:
                        @blk.vector
                        def _(v):
                            def ve(instr):
                                a_ro.inc(instr, 1)
                                v.wait_ge(a_ro.h, a_ro.n)
                            ve(v.tensor_tensor(out=sc_[:], in0=sa_[:],
                                               in1=sb_[:], op=ALU.subtract))
                            ve(v.tensor_single_scalar(out=sb_[:], in_=sc_[:],
                                                      scalar=0.5, op=ALU.is_ge))
                            ve(v.tensor_tensor(out=sd_[:], in0=sc_[:],
                                               in1=sb_[:], op=ALU.subtract))

                    with nc.Block() as blk:
                        @blk.scalar
                        def _(s):
                            import math
                            a_ro.inc(s.activation(
                                out=stab[0:64, tch * W_:(tch + 1) * W_],
                                in_=sd_[:], func=AF.Sin, scale=2 * math.pi), 1)

                    with nc.Block() as blk:
                        @blk.vector
                        def _(v):
                            a_ro.inc(v.tensor_scalar_add(out=sb_[:], in0=sa_[:],
                                                         scalar1=0.25), 1)

                    with nc.Block() as blk:
                        @blk.scalar
                        def _(s):
                            a_ro.inc(s.copy(out=si_[:], in_=sb_[:]), 1)
                            s.wait_ge(a_ro.h, a_ro.n)
                            a_ro.inc(s.copy(out=sc_[:], in_=si_[:]), 1)

                    with nc.Block() as blk:
                        @blk.vector
                        def _(v):
                            def ve(instr):
                                a_ro.inc(instr, 1)
                                v.wait_ge(a_ro.h, a_ro.n)
                            ve(v.tensor_tensor(out=sd_[:], in0=sb_[:],
                                               in1=sc_[:], op=ALU.subtract))
                            ve(v.tensor_single_scalar(out=sb_[:], in_=sd_[:],
                                                      scalar=0.5, op=ALU.is_ge))
                            ve(v.tensor_tensor(out=sa_[:], in0=sd_[:],
                                               in1=sb_[:], op=ALU.subtract))

                    with nc.Block() as blk:
                        @blk.scalar
                        def _(s):
                            import math
                            a_ro.inc(s.activation(
                                out=ctab[0:64, tch * W_:(tch + 1) * W_],
                                in_=sa_[:], func=AF.Sin, scale=2 * math.pi), 1)

        if rope:
            with nc.Block() as blk:
                @blk.vector
                def _(v):
                    def ve(instr):
                        a_ro.inc(instr, 1)
                        v.wait_ge(a_ro.h, a_ro.n)
                    ve(v.tensor_copy(out=ctab[64:128, :], in_=ctab[0:64, :]))
                    ve(v.tensor_copy(out=stab[64:96, :], in_=stab[32:64, :]))
                    ve(v.tensor_copy(out=stab[96:128, :], in_=stab[0:32, :]))
                    ve(v.tensor_copy(out=stab[32:64, :], in_=stab[0:32, :]))
                    ve(v.tensor_copy(out=stab[0:32, :], in_=stab[64:96, :]))

        CT = []
        for h in range(NH):
            for qb in range(QB):
                idx = h * QB + qb
                for kt in range(4 * (qb + 1)):
                    CT.append((idx, h, qb, kt, kt >= qb * 4))

        def wsrc(j):
            p_, f8 = divmod(j, 8)
            wap = wq_ap if p_ == 0 else wk_ap
            if wt_trans:
                return wap[f8 * 128:(f8 + 1) * 128, :]
            src = wap[:, f8 * 128:(f8 + 1) * 128]
            return src.rearrange("(c p) f -> p c f", p=128)

        for s_ in range(SPC):
            t0 = s_ * T
            ld0, mm0, ev0 = a_ld.n, a_mm.n, a_ev.n
            sc0, ex0, pr0, ly0, no0 = d_sc.n, d_ex.n, d_pr.n, d_ly.n, d_no.n

            sc_at, pr_at = {}, {}
            scn, prn = sc0, pr0
            for (idx, h, qb, kt, diag) in CT:
                scn += 1
                sc_at[(idx, kt)] = scn
                prn += 1
                pr_at[(idx, kt)] = prn
            wq_ready = []

            if wt_trans and s_ == 0:
                trW0 = a_tr.n
                teW0 = [a_te[0].n, a_te[1].n]
                wv_ready = []
                with nc.Block() as blk:
                    @blk.gpsimd
                    def _(g):
                        for cb in range(8):
                            if cb >= 2:
                                g.wait_ge(a_tr.h, trW0 + (cb - 1) * 8)
                            wv_ready.append(a_wq[cb % 2].inc(g.dma_start(
                                out=wstg[cb % 2][:],
                                in_=wv_ap[cb * 128:(cb + 1) * 128, :]), 16))

                    @blk.tensor
                    def _(t):
                        for cb in range(8):
                            t.wait_ge(a_wq[cb % 2].h, wv_ready[cb])
                            for c in range(8):
                                k = cb * 8 + c
                                if k >= 2:
                                    t.wait_ge(a_te[k % 2].h,
                                              teW0[k % 2] + k // 2)
                                a_tr.inc(t.transpose(
                                    ps_sc[k % 2][:].bitcast(BF16)[:, 0:128],
                                    wstg[cb % 2][:, c * 128:(c + 1) * 128],
                                    C.ident_b[:]), 1)

                    @blk.vector
                    def _(v):
                        for cb in range(8):
                            for c in range(8):
                                k = cb * 8 + c
                                v.wait_ge(a_tr.h, trW0 + k + 1)
                                a_te[k % 2].inc(v.tensor_copy(
                                    out=wv[c][:, cb * 128:(cb + 1) * 128],
                                    in_=ps_sc[k % 2][:].bitcast(BF16)[:, 0:128]),
                                    1)

            tr0 = a_tr.n
            te0 = [a_te[0].n, a_te[1].n]

            with nc.Block() as blk:
                @blk.sync
                def _(sy):
                    for c in range(8):
                        a_ld.inc(sy.dma_start(
                            out=at[c][:],
                            in_=xn_ap[c * 128:(c + 1) * 128, t0:t0 + T]), 16)
                    if s_ == 0 and battn_ap is not None:
                        a_lb.inc(sy.dma_start(
                            out=brawt[0:24, :],
                            in_=battn_ap.rearrange("(a b) -> a b", b=128)), 16)
                        a_lb.inc(sy.dma_start(
                            out=brawt[24:32, :],
                            in_=bproj_ap.rearrange("(a b) -> a b", b=128)), 16)

                @blk.gpsimd
                def _(g):
                    if s_ == 0 and not wt_trans:
                        for c in range(8):
                            a_w.inc(g.dma_start(
                                out=wv[c][:],
                                in_=wv_ap[c * 128:(c + 1) * 128, :]), 16)
                        if battn_ap is not None:
                            a_w.inc(g.dma_start(
                                out=vbrow[:],
                                in_=battn_ap[2048:3072].unsqueeze(0)), 16)
                    for j in range(16):
                        if j >= 2:
                            g.wait_ge(a_mm.h, mm0 + (j - 1) * 4)
                        if wt_trans:
                            dst = wstg[j % 2][:]
                        else:
                            dst = wqk[j % 2][:].rearrange("p (c f) -> p c f", f=128)
                        wq_ready.append(a_wq[j % 2].inc(
                            g.dma_start(out=dst, in_=wsrc(j)), 16))

                @blk.tensor
                def _(t):
                    if s_ == 0 and battn_ap is not None:
                        t.wait_ge(a_lb.h, lb0_ + 32)
                        a_tp.inc(t.transpose(ps_p[0][:, 0:32], brawt[:],
                                             C.ident_f[0:32, 0:32]), 1)
                        t.wait_ge(a_tp.h, a_tp.n + 1)
                    t.wait_ge(a_ld.h, ld0 + 8 * 16)
                    for j in range(16):
                        t.wait_ge(a_wq[j % 2].h, wq_ready[j])
                        if wt_trans:
                            if j >= 1:
                                t.wait_ge(a_te[(j - 1) % 2].h,
                                          te0[(j - 1) % 2] + ((j - 1) // 2 + 1) * 8)
                            for c in range(8):
                                if c >= 2:
                                    t.wait_ge(a_te[j % 2].h,
                                              te0[j % 2] + (j // 2) * 8 + c - 1)
                                a_tr.inc(t.transpose(
                                    ps_sc[c % 2][:].bitcast(BF16)[:, 0:128],
                                    wstg[j % 2][:, c * 128:(c + 1) * 128],
                                    C.ident_b[:]), 1)
                            t.wait_ge(a_te[j % 2].h, te0[j % 2] + (j // 2 + 1) * 8)
                        for tch in range(4):
                            g_ = j * 4 + tch
                            if g_ >= 2:
                                t.wait_ge(a_ev.h, ev0 + g_ - 1)
                            pb = ps_p[g_ % 2]
                            for c in range(8):
                                i_ = t.matmul(pb[:],
                                              wqk[j % 2][:, c * 128:(c + 1) * 128],
                                              at[c][:, tch * 512:(tch + 1) * 512],
                                              start=(c == 0), stop=(c == 7))
                            a_mm.inc(i_, 1)
                    if s_ == 0 and not wt_trans:
                        t.wait_ge(a_w.h, w0_ + (144 if battn_ap is not None else 128))
                    for i16 in range(16):
                        for fh in range(2):
                            g_ = 64 + i16 * 2 + fh
                            t.wait_ge(a_ev.h, ev0 + g_ - 1)
                            pb = ps_p[g_ % 2]
                            for c in range(8):
                                last = (c == 7 and battn_ap is None)
                                i_ = t.matmul(pb[:],
                                              at[c][:, i16 * 128:(i16 + 1) * 128],
                                              wv[c][:, fh * 512:(fh + 1) * 512],
                                              start=(c == 0), stop=last)
                            if battn_ap is not None:
                                i_ = t.matmul(pb[:], C.ones_b[0:1, :],
                                              vbrow[:, fh * 512:(fh + 1) * 512],
                                              start=False, stop=True)
                            a_mm.inc(i_, 1)

                @blk.vector
                def _(v):
                    if s_ == 0 and battn_ap is not None:
                        v.wait_ge(a_tp.h, tp0_ + 1)
                        a_tp.inc(v.tensor_copy(out=bcols[:], in_=ps_p[0][:, 0:32]),
                                 1)
                    for j in range(16):
                        if wt_trans:
                            for c in range(8):
                                v.wait_ge(a_tr.h, tr0 + j * 8 + c + 1)
                                a_te[j % 2].inc(v.tensor_copy(
                                    out=wqk[j % 2][:, c * 128:(c + 1) * 128],
                                    in_=ps_sc[c % 2][:].bitcast(BF16)[:, 0:128]),
                                    1)
                        p_, f8 = divmod(j, 8)
                        dst = (qt if p_ == 0 else kt8)[f8]
                        for tch in range(4):
                            g_ = j * 4 + tch
                            v.wait_ge(a_mm.h, mm0 + g_ + 1)
                            if battn_ap is not None:
                                bcol = bcols[:, j:j + 1]
                                a_ev.inc(v.tensor_scalar(
                                    out=dst[:, tch * 512:(tch + 1) * 512],
                                    in0=ps_p[g_ % 2][:],
                                    scalar1=bcol, scalar2=None, op0=ALU.add), 1)
                            else:
                                a_ev.inc(v.tensor_copy(
                                    out=dst[:, tch * 512:(tch + 1) * 512],
                                    in_=ps_p[g_ % 2][:]), 1)
                    for i16 in range(16):
                        for fh in range(2):
                            g_ = 64 + i16 * 2 + fh
                            v.wait_ge(a_mm.h, mm0 + g_ + 1)
                            a_ev.inc(v.tensor_copy(
                                out=vt[i16][:, fh * 512:(fh + 1) * 512],
                                in_=ps_p[g_ % 2][:]), 1)

            if rope:
                with nc.Block() as blk:
                    @blk.vector
                    def _(v):
                        def ve(instr):
                            a_ro.inc(instr, 1)
                            v.wait_ge(a_ro.h, a_ro.n)
                        for tiles in (qt, kt8):
                            for c in range(8):
                                for hb in (0, 64):
                                    lo = tiles[c][hb:hb + 32, :]
                                    hi = tiles[c][hb + 32:hb + 64, :]
                                    ve(v.tensor_tensor(out=rsc[0][:], in0=lo,
                                                       in1=ctab[hb:hb + 32, :],
                                                       op=ALU.mult))
                                    ve(v.tensor_tensor(out=rsc[1][:], in0=lo,
                                                       in1=stab[hb:hb + 32, :],
                                                       op=ALU.mult))
                                    ve(v.tensor_tensor(
                                        out=rsc[2][:], in0=hi,
                                        in1=stab[hb + 32:hb + 64, :],
                                        op=ALU.mult))
                                    ve(v.tensor_tensor(
                                        out=rsc[3][:], in0=hi,
                                        in1=ctab[hb + 32:hb + 64, :],
                                        op=ALU.mult))
                                    ve(v.tensor_tensor(out=lo, in0=rsc[0][:],
                                                       in1=rsc[2][:],
                                                       op=ALU.subtract))
                                    ve(v.tensor_tensor(out=hi, in0=rsc[3][:],
                                                       in1=rsc[1][:],
                                                       op=ALU.add))

            with nc.Block() as blk:
                @blk.tensor
                def _(t):
                    for h in range(NH):
                        hb = (h % 2) * 64
                        kfm = kt8[h // 2]
                        qfm = qt[h // 2]
                        for qb in range(QB):
                            idx = h * QB + qb
                            nkt = 4 * (qb + 1)
                            if idx >= 2:
                                t.wait_ge(d_no.h, no0 + 2 * (idx - 1))
                            for kt in range(nkt):
                                if kt >= 2:
                                    t.wait_ge(d_pr.h, pr_at[(idx, kt - 2)])
                                d_sc.inc(t.matmul(
                                    ps_sc[kt % 2][:],
                                    kfm[hb:hb + 64, kt * 128:(kt + 1) * 128],
                                    qfm[hb:hb + 64, qb * 512:(qb + 1) * 512],
                                    start=True, stop=True), 1)
                                if kt >= 1:
                                    kk = kt - 1
                                    t.wait_ge(d_pr.h, pr_at[(idx, kk)])
                                    pt_ = (pmsk if kk >= qb * 4 else praw)[kk % 2]
                                    t.matmul(ps_l[idx % 2][:], C.ones_b[:, 0:64],
                                             pt_[:], start=(kk == 0), stop=False)
                                    t.matmul(ps_y[idx % 2][:],
                                             vt[kk][:, h * 64:(h + 1) * 64],
                                             pt_[:], start=(kk == 0), stop=False)
                            kk = nkt - 1
                            t.wait_ge(d_pr.h, pr_at[(idx, kk)])
                            pt_ = (pmsk if kk >= qb * 4 else praw)[kk % 2]
                            t.matmul(ps_l[idx % 2][:], C.ones_b[:, 0:64], pt_[:],
                                     start=(kk == 0), stop=True)
                            d_ly.inc(t.matmul(ps_y[idx % 2][:],
                                              vt[kk][:, h * 64:(h + 1) * 64],
                                              pt_[:], start=(kk == 0), stop=True),
                                     1)

                @blk.scalar
                def _(s):
                    for (idx, h, qb, kt, diag) in CT:
                        s.wait_ge(d_sc.h, sc_at[(idx, kt)])
                        tgt = d_ex if diag else d_pr
                        tgt.inc(s.activation(out=praw[kt % 2][:],
                                             in_=ps_sc[kt % 2][:], func=AF.Exp,
                                             scale=0.125, bias=-EXP_SHIFT), 1)

                @blk.vector
                def _(v):
                    exn = ex0
                    for (idx, h, qb, kt, diag) in CT:
                        if diag:
                            exn += 1
                            v.wait_ge(d_ex.h, exn)
                            d_pr.inc(v.tensor_tensor(out=pmsk[kt % 2][:],
                                                     in0=praw[kt % 2][:],
                                                     in1=C.cmask[kt - qb * 4][:],
                                                     op=ALU.mult), 1)
                        if kt == 4 * (qb + 1) - 1:
                            hb = (h % 2) * 64
                            v.wait_ge(d_ly.h, ly0 + idx + 1)
                            d_no.inc(v.reciprocal(out=rsb[idx % 2][:],
                                                  in_=ps_l[idx % 2][:]), 1)
                            v.wait_ge(d_no.h, d_no.n)
                            d_no.inc(v.tensor_tensor(
                                out=at[h // 2][hb:hb + 64,
                                               qb * 512:(qb + 1) * 512],
                                in0=ps_y[idx % 2][:], in1=rsb[idx % 2][:],
                                op=ALU.mult), 1)

            mmE0, evE0 = e_mm.n, e_ev.n
            soE0 = [e_so[0].n, e_so[1].n]
            trE0 = a_tr.n
            teE0 = [a_te[0].n, a_te[1].n]
            wp_ready = []
            with nc.Block() as blk:
                @blk.gpsimd
                def _(g):
                    def wpdma(o8):
                        if wt_trans:
                            dstw = wstg[o8 % 2][:]
                            src = wo_ap[o8 * 128:(o8 + 1) * 128, :]
                        else:
                            src = wo_ap[:, o8 * 128:(o8 + 1) * 128]
                            src = src.rearrange("(c p) f -> p c f", p=128)
                            dstw = wp[o8 % 2][:].rearrange("p (c f) -> p c f",
                                                           f=128)
                        wp_ready.append(a_wq[o8 % 2].inc(
                            g.dma_start(out=dstw, in_=src), 16))
                    wpdma(0)
                    wpdma(1)
                    for o8 in range(8):
                        for tch in range(4):
                            j = o8 * 4 + tch
                            g.wait_ge(e_ev.h, evE0 + j + 1)
                            e_so[j % 2].inc(g.dma_start(
                                out=s_ap[o8 * 128:(o8 + 1) * 128,
                                         t0 + tch * 512:t0 + (tch + 1) * 512],
                                in_=sst[j % 2][:]), 16)
                        if o8 + 2 < 8:
                            g.wait_ge(e_mm.h, mmE0 + (o8 + 1) * 4)
                            wpdma(o8 + 2)

                @blk.tensor
                def _(t):
                    for o8 in range(8):
                        t.wait_ge(a_wq[o8 % 2].h, wp_ready[o8])
                        if wt_trans:
                            if o8 >= 1:
                                t.wait_ge(a_te[(o8 - 1) % 2].h,
                                          teE0[(o8 - 1) % 2] +
                                          ((o8 - 1) // 2 + 1) * 8)
                            for c in range(8):
                                if c >= 2:
                                    t.wait_ge(a_te[o8 % 2].h,
                                              teE0[o8 % 2] + (o8 // 2) * 8 + c - 1)
                                a_tr.inc(t.transpose(
                                    ps_sc[c % 2][:].bitcast(BF16)[:, 0:128],
                                    wstg[o8 % 2][:, c * 128:(c + 1) * 128],
                                    C.ident_b[:]), 1)
                            t.wait_ge(a_te[o8 % 2].h,
                                      teE0[o8 % 2] + (o8 // 2 + 1) * 8)
                        for tch in range(4):
                            j = o8 * 4 + tch
                            if j >= 2:
                                t.wait_ge(e_ev.h, evE0 + j - 1)
                            pb = ps_p[j % 2]
                            for c in range(8):
                                i_ = t.matmul(pb[:],
                                              wp[o8 % 2][:, c * 128:(c + 1) * 128],
                                              at[c][:, tch * 512:(tch + 1) * 512],
                                              start=(c == 0), stop=(c == 7))
                            e_mm.inc(i_, 1)

                @blk.vector
                def _(v):
                    for o8 in range(8):
                        if wt_trans:
                            for c in range(8):
                                v.wait_ge(a_tr.h, trE0 + o8 * 8 + c + 1)
                                a_te[o8 % 2].inc(v.tensor_copy(
                                    out=wp[o8 % 2][:, c * 128:(c + 1) * 128],
                                    in_=ps_sc[c % 2][:].bitcast(BF16)[:, 0:128]),
                                    1)
                        for tch in range(4):
                            j = o8 * 4 + tch
                            v.wait_ge(e_mm.h, mmE0 + j + 1)
                            if j >= 2:
                                v.wait_ge(e_so[j % 2].h, soE0[j % 2] +
                                          (j // 2) * 16)
                            if bproj_ap is not None:
                                bcol = bcols[:, 24 + o8:24 + o8 + 1]
                                e_ev.inc(v.tensor_scalar(out=sst[j % 2][:],
                                                         in0=ps_p[j % 2][:],
                                                         scalar1=bcol,
                                                         scalar2=None,
                                                         op0=ALU.add), 1)
                            else:
                                e_ev.inc(v.tensor_copy(out=sst[j % 2][:],
                                                       in_=ps_p[j % 2][:]), 1)

            with nc.Block() as blk:
                @blk.gpsimd
                def _(g):
                    g.wait_ge(e_so[0].h, e_so[0].n)
                    g.wait_ge(e_so[1].h, e_so[1].n)



def emit_moe(C, tag, xn_ap, w1_ap, b1_ap, w2_ap, b2_ap,
             x_ap, s_ap, xout_ap, act, emit_routing):
    nc = C.nc
    TCH = 2048
    NCH2 = N // TCH
    NT = TCH // 512
    with contextlib.ExitStack() as st:
        E = st.enter_context
        m_ld = C.sem("m_ld")
        m_b = C.sem("m_b")
        m_b2 = C.sem("m_b2")
        m_bt = C.sem("m_bt")
        m_w1 = [C.sem("m_w10"), C.sem("m_w11")]
        m_w2 = C.sem("m_w2")
        m_wb = C.sem("m_wb")
        m_wv = C.sem("m_wv")
        m_h1 = C.sem("m_h1")
        m_hs = C.sem("m_hs")
        m_hv = C.sem("m_hv")
        m_p2 = C.sem("m_p2")
        m_ac = C.sem("m_ac")
        m_xl = [C.sem("m_xl0"), C.sem("m_xl1")]
        m_cg = C.sem("m_cg")
        m_cv = C.sem("m_cv")
        m_so = [C.sem("m_so0"), C.sem("m_so1")]

        w8d = nc.dram_tensor(f"{tag}_w8d", [8, N], BF16, kind="Internal").ap()
        emit_routing(C, w8d)

        xnres = [E(nc.sbuf_tensor(f"{tag}_xn{c}", [128, TCH], BF16))
                 for c in range(8)]
        Ht = [E(nc.sbuf_tensor(f"{tag}_H{f}", [128, TCH], BF16)) for f in range(16)]
        accb = [E(nc.sbuf_tensor(f"{tag}_ac{c}", [128, TCH], BF16))
                for c in range(8)]
        w2res = [E(nc.sbuf_tensor(f"{tag}_w2{f}", [128, D], BF16))
                 for f in range(16)]
        w1s = [E(nc.sbuf_tensor(f"{tag}_w1{i}", [128, D], BF16)) for i in range(2)]
        b1raw = [E(nc.sbuf_tensor(f"{tag}_b1r{i}", [128, 128], F32))
                 for i in range(2)]
        b1cols = [E(nc.sbuf_tensor(f"{tag}_b1c{i}", [128, 128], F32))
                  for i in range(2)]
        b2rowe = E(nc.sbuf_tensor(f"{tag}_b2row", [1, 1024], BF16))
        w8row = E(nc.sbuf_tensor(f"{tag}_w8row", [1, TCH], BF16))
        wbc = E(nc.sbuf_tensor(f"{tag}_wbc", [128, TCH], BF16))
        hs = [E(nc.sbuf_tensor(f"{tag}_hs{i}", [128, 512], F32)) for i in range(2)]
        xtl = [E(nc.sbuf_tensor(f"{tag}_xtl{i}", [128, 512], F32)) for i in range(2)]
        stl = [E(nc.sbuf_tensor(f"{tag}_stl{i}", [128, 512], F32)) for i in range(2)]
        gtl = [E(nc.sbuf_tensor(f"{tag}_gtl{i}", [128, 512], BF16)) for i in range(2)]
        g2l = [E(nc.sbuf_tensor(f"{tag}_g2l{i}", [128, 512], F32)) for i in range(2)]
        otl = [E(nc.sbuf_tensor(f"{tag}_otl{i}", [128, 512], F32)) for i in range(2)]
        ps_a = [E(nc.psum_tensor(f"{tag}_psa{i}", [128, 512], F32)) for i in range(2)]
        ps_b = [E(nc.psum_tensor(f"{tag}_psb{i}", [128, 512], F32)) for i in range(2)]
        ps_w = E(nc.psum_tensor(f"{tag}_psw", [128, 512], F32))

        AFACT = AF.Relu if act == "relu" else AF.Silu

        with nc.Block() as blk:
            @blk.sync
            def _(sy):
                for i in range(2):
                    m_b.inc(sy.dma_start(
                        out=b1raw[i][:],
                        in_=b1_ap.rearrange("e (f p) -> (e f) p", p=128)
                        [i * 128:(i + 1) * 128, :]), 16)

            @blk.tensor
            def _(t):
                t.wait_ge(m_b.h, m_b.n)
                for i in range(2):
                    m_bt.inc(t.transpose(ps_a[i][:, 0:128], b1raw[i][:],
                                         C.ident_f[:]), 1)

            @blk.vector
            def _(v):
                bt0 = m_bt.n - 2
                for i in range(2):
                    v.wait_ge(m_bt.h, bt0 + i + 1)
                    m_b.inc(v.tensor_copy(out=b1cols[i][:], in_=ps_a[i][:, 0:128]),
                            1)

        for ch2 in range(NCH2):
            tb = ch2 * TCH
            ld0 = m_ld.n
            with nc.Block() as blk:
                @blk.sync
                def _(sy):
                    for c in range(8):
                        m_ld.inc(sy.dma_start(
                            out=xnres[c][:],
                            in_=xn_ap[c * 128:(c + 1) * 128, tb:tb + TCH]), 16)

            for e in range(8):
                wb0, wv0, b0_ = m_wb.n, m_wv.n, m_b.n
                with nc.Block() as blk:
                    @blk.sync
                    def _(sy):
                        m_b.inc(sy.dma_start(out=w8row[:],
                                             in_=w8d[e:e + 1, tb:tb + TCH]), 16)

                    @blk.gpsimd
                    def _(g):
                        m_b2.inc(g.dma_start(out=b2rowe[:],
                                             in_=b2_ap[e:e + 1, :]), 16)

                    @blk.tensor
                    def _(t):
                        t.wait_ge(m_b.h, b0_ + 16)
                        t.wait_ge(m_b2.h, m_b2.n)
                        for q in range(NT):
                            if q >= 1:
                                t.wait_ge(m_wv.h, wv0 + q)
                            m_wb.inc(t.matmul(
                                ps_w[:], C.ones_b[0:1, :],
                                w8row[:, q * 512:(q + 1) * 512],
                                start=True, stop=True), 1)

                    @blk.vector
                    def _(v):
                        for q in range(NT):
                            v.wait_ge(m_wb.h, wb0 + q + 1)
                            m_wv.inc(v.tensor_copy(
                                out=wbc[:, q * 512:(q + 1) * 512], in_=ps_w[:]), 1)

                for half in range(2):
                    fb = half * 2048
                    h10, hs0, hv0 = m_h1.n, m_hs.n, m_hv.n
                    p20, ac0, w20 = m_p2.n, m_ac.n, m_w2.n
                    w1_ready = []
                    with nc.Block() as blk:
                        @blk.gpsimd
                        def _(g):
                            for f in range(16):
                                m_w2.inc(g.dma_start(
                                    out=w2res[f][:],
                                    in_=w2_ap[e, fb + f * 128:fb + (f + 1) * 128,
                                              :]), 16)
                            for f in range(16):
                                if f >= 2:
                                    g.wait_ge(m_h1.h, h10 + (f - 1) * NT)
                                src = w1_ap[e, :, fb + f * 128:fb + (f + 1) * 128]
                                src = src.rearrange("(c p) f -> p c f", p=128)
                                dstw = w1s[f % 2][:].rearrange("p (c f) -> p c f",
                                                               f=128)
                                w1_ready.append(m_w1[f % 2].inc(
                                    g.dma_start(out=dstw, in_=src), 16))

                        @blk.tensor
                        def _(t):
                            t.wait_ge(m_ld.h, ld0 + 128)
                            for f in range(16):
                                t.wait_ge(m_w1[f % 2].h, w1_ready[f])
                                for q in range(NT):
                                    j = f * NT + q
                                    if j >= 2:
                                        t.wait_ge(m_hs.h, hs0 + j - 1)
                                    pb = ps_a[j % 2]
                                    for c in range(8):
                                        i_ = t.matmul(
                                            pb[:],
                                            w1s[f % 2][:, c * 128:(c + 1) * 128],
                                            xnres[c][:, q * 512:(q + 1) * 512],
                                            start=(c == 0), stop=(c == 7))
                                    m_h1.inc(i_, 1)
                            t.wait_ge(m_w2.h, w20 + 256)
                            t.wait_ge(m_hv.h, hv0 + 64)
                            for o8 in range(8):
                                for q in range(NT):
                                    j = o8 * NT + q
                                    if j >= 2:
                                        t.wait_ge(m_ac.h, ac0 + j - 1)
                                    pb = ps_b[j % 2]
                                    for f in range(16):
                                        t.matmul(pb[:],
                                                 w2res[f][:,
                                                          o8 * 128:(o8 + 1) * 128],
                                                 Ht[f][:, q * 512:(q + 1) * 512],
                                                 start=(f == 0), stop=False)
                                    i_ = t.matmul(
                                        pb[:],
                                        b2rowe[0:1, o8 * 128:(o8 + 1) * 128],
                                        w8row[:, q * 512:(q + 1) * 512],
                                        start=False, stop=True)
                                    m_p2.inc(i_, 1)

                        @blk.scalar
                        def _(s):
                            for f in range(16):
                                idx = e * 32 + fb // 128 + f
                                bcol = b1cols[idx // 128][:,
                                                          idx % 128:idx % 128 + 1]
                                for q in range(NT):
                                    j = f * NT + q
                                    s.wait_ge(m_h1.h, h10 + j + 1)
                                    if j >= 2:
                                        s.wait_ge(m_hv.h, hv0 + j - 1)
                                    m_hs.inc(s.activation(out=hs[j % 2][:],
                                                          in_=ps_a[j % 2][:],
                                                          func=AFACT,
                                                          bias=bcol), 1)

                        @blk.vector
                        def _(v):
                            for f in range(16):
                                for q in range(NT):
                                    j = f * NT + q
                                    v.wait_ge(m_hs.h, hs0 + j + 1)
                                    m_hv.inc(v.tensor_tensor(
                                        out=Ht[f][:, q * 512:(q + 1) * 512],
                                        in0=hs[j % 2][:],
                                        in1=wbc[:, q * 512:(q + 1) * 512],
                                        op=ALU.mult), 1)
                            for o8 in range(8):
                                for q in range(NT):
                                    j = o8 * NT + q
                                    v.wait_ge(m_p2.h, p20 + j + 1)
                                    if e == 0 and half == 0:
                                        m_ac.inc(v.tensor_copy(
                                            out=accb[o8][:, q * 512:(q + 1) * 512],
                                            in_=ps_b[j % 2][:]), 1)
                                    else:
                                        m_ac.inc(v.tensor_tensor(
                                            out=accb[o8][:, q * 512:(q + 1) * 512],
                                            in0=accb[o8][:, q * 512:(q + 1) * 512],
                                            in1=ps_b[j % 2][:], op=ALU.add), 1)

            cg0, cv0 = m_cg.n, m_cv.n
            xl0 = [m_xl[0].n, m_xl[1].n]
            so0 = [m_so[0].n, m_so[1].n]
            with nc.Block() as blk:
                @blk.sync
                def _(sy):
                    for c in range(8):
                        for q in range(NT):
                            j = c * NT + q
                            sy.wait_ge(m_cv.h, cv0 + max(0, 2 * (j - 1)))
                            m_xl[j % 2].inc(sy.dma_start(
                                out=xtl[j % 2][:],
                                in_=x_ap[c * 128:(c + 1) * 128,
                                         tb + q * 512:tb + (q + 1) * 512]), 16)
                            m_xl[j % 2].inc(sy.dma_start(
                                out=stl[j % 2][:],
                                in_=s_ap[c * 128:(c + 1) * 128,
                                         tb + q * 512:tb + (q + 1) * 512]), 16)

                @blk.scalar
                def _(s):
                    for c in range(8):
                        for q in range(NT):
                            j = c * NT + q
                            s.wait_ge(m_cv.h, cv0 + max(0, 2 * (j - 2) + 1))
                            m_cg.inc(s.activation(
                                out=gtl[j % 2][:],
                                in_=accb[c][:, q * 512:(q + 1) * 512],
                                func=AF.Silu), 1)

                @blk.vector
                def _(v):
                    for c in range(8):
                        for q in range(NT):
                            j = c * NT + q
                            v.wait_ge(m_xl[j % 2].h, xl0[j % 2] + (j // 2 + 1) * 32)
                            v.wait_ge(m_cg.h, cg0 + j + 1)
                            v.wait_ge(m_so[j % 2].h, so0[j % 2] + (j // 2) * 16)
                            m_cv.inc(v.tensor_tensor(out=g2l[j % 2][:],
                                                     in0=stl[j % 2][:],
                                                     in1=gtl[j % 2][:],
                                                     op=ALU.mult), 1)
                            v.wait_ge(m_cv.h, m_cv.n)
                            m_cv.inc(v.tensor_tensor(out=otl[j % 2][:],
                                                     in0=xtl[j % 2][:],
                                                     in1=g2l[j % 2][:],
                                                     op=ALU.add), 1)

                @blk.gpsimd
                def _(g):
                    for c in range(8):
                        for q in range(NT):
                            j = c * NT + q
                            g.wait_ge(m_cv.h, cv0 + 2 * (j + 1))
                            m_so[j % 2].inc(g.dma_start(
                                out=xout_ap[c * 128:(c + 1) * 128,
                                            tb + q * 512:tb + (q + 1) * 512],
                                in_=otl[j % 2][:]), 16)

        with nc.Block() as blk:
            @blk.gpsimd
            def _(g):
                g.wait_ge(m_so[0].h, m_so[0].n)
                g.wait_ge(m_so[1].h, m_so[1].n)



def mk_routing_l0(xr_ap):
    def emit(C, w8d_ap):
        nc = C.nc
        r_ld = C.sem("r_ld")
        r_c = C.sem("r_c")
        r_so = C.sem("r_so")
        with contextlib.ExitStack() as st:
            E = st.enter_context
            W = 512
            rr = E(nc.sbuf_tensor("r0_rr", [1, W], F32))
            ri = E(nc.sbuf_tensor("r0_ri", [1, W], I32))
            rif = E(nc.sbuf_tensor("r0_rif", [1, W], F32))
            gt = E(nc.sbuf_tensor("r0_gt", [1, W], F32))
            lt = E(nc.sbuf_tensor("r0_lt", [1, W], F32))
            pos = E(nc.sbuf_tensor("r0_pos", [1, W], F32))
            aa = E(nc.sbuf_tensor("r0_aa", [1, W], F32))
            bb = E(nc.sbuf_tensor("r0_bb", [1, W], F32))
            t1 = E(nc.sbuf_tensor("r0_t1", [1, W], F32))
            t2 = E(nc.sbuf_tensor("r0_t2", [1, W], F32))
            trf = E(nc.sbuf_tensor("r0_trf", [1, W], F32))
            hi = E(nc.sbuf_tensor("r0_hi", [1, W], I32))
            hm = E(nc.sbuf_tensor("r0_hm", [1, W], I32))
            hf = E(nc.sbuf_tensor("r0_hf", [1, W], F32))
            m1 = E(nc.sbuf_tensor("r0_m1", [1, W], F32))
            m2 = E(nc.sbuf_tensor("r0_m2", [1, W], F32))
            m3 = E(nc.sbuf_tensor("r0_m3", [1, W], F32))
            we = [E(nc.sbuf_tensor(f"r0_we{e}", [1, W], F32)) for e in range(8)]
            for ch in range(N // W):
                with nc.Block() as blk:
                    @blk.sync
                    def _(sy):
                        r_ld.inc(sy.dma_start(out=rr[:],
                                              in_=xr_ap[:, ch * W:(ch + 1) * W]),
                                 16)

                    @blk.scalar
                    def _(s):
                        s.wait_ge(r_ld.h, r_ld.n)
                        r_c.inc(s.copy(out=ri[:], in_=rr[:]), 1)
                        s.wait_ge(r_c.h, r_c.n)
                        r_c.inc(s.copy(out=rif[:], in_=ri[:]), 1)
                        s.wait_ge(r_c.h, r_c.n)

                with nc.Block() as blk:
                    @blk.vector
                    def _(v):
                        def ve(instr):
                            r_c.inc(instr, 1)
                            v.wait_ge(r_c.h, r_c.n)
                        ve(v.tensor_tensor(out=gt[:], in0=rif[:], in1=rr[:],
                                           op=ALU.is_gt))
                        ve(v.tensor_tensor(out=lt[:], in0=rif[:], in1=rr[:],
                                           op=ALU.is_lt))
                        ve(v.tensor_single_scalar(out=pos[:], in_=rr[:],
                                                  scalar=0.0, op=ALU.is_ge))
                        ve(v.tensor_tensor(out=aa[:], in0=gt[:], in1=pos[:],
                                           op=ALU.mult))
                        ve(v.tensor_tensor(out=bb[:], in0=lt[:], in1=pos[:],
                                           op=ALU.mult))
                        ve(v.tensor_tensor(out=t1[:], in0=rif[:], in1=aa[:],
                                           op=ALU.subtract))
                        ve(v.tensor_tensor(out=t2[:], in0=t1[:], in1=lt[:],
                                           op=ALU.add))
                        ve(v.tensor_tensor(out=trf[:], in0=t2[:], in1=bb[:],
                                           op=ALU.subtract))

                with nc.Block() as blk:
                    @blk.scalar
                    def _(s):
                        r_c.inc(s.copy(out=hi[:], in_=trf[:]), 1)
                        s.wait_ge(r_c.h, r_c.n)

                with nc.Block() as blk:
                    @blk.vector
                    def _(v):
                        r_c.inc(v.tensor_single_scalar(out=hm[:], in_=hi[:],
                                                       scalar=7,
                                                       op=ALU.bitwise_and), 1)
                        v.wait_ge(r_c.h, r_c.n)

                with nc.Block() as blk:
                    @blk.scalar
                    def _(s):
                        r_c.inc(s.copy(out=hf[:], in_=hm[:]), 1)
                        s.wait_ge(r_c.h, r_c.n)

                with nc.Block() as blk:
                    @blk.vector
                    def _(v):
                        def ve(instr):
                            r_c.inc(instr, 1)
                            v.wait_ge(r_c.h, r_c.n)
                        for e in range(8):
                            ve(v.tensor_single_scalar(out=m1[:], in_=hf[:],
                                                      scalar=float(e),
                                                      op=ALU.is_equal))
                            ve(v.tensor_single_scalar(out=m2[:], in_=hf[:],
                                                      scalar=float((e + 7) % 8),
                                                      op=ALU.is_equal))
                            ve(v.tensor_tensor(out=m3[:], in0=m1[:], in1=m2[:],
                                               op=ALU.add))
                            ve(v.tensor_scalar_mul(out=we[e][:], in0=m3[:],
                                                   scalar1=0.5))

                with nc.Block() as blk:
                    @blk.gpsimd
                    def _(g):
                        for e in range(8):
                            r_so.inc(g.dma_start(
                                out=w8d_ap[e:e + 1, ch * W:(ch + 1) * W],
                                in_=we[e][:]), 16)
                        g.wait_ge(r_so.h, r_so.n)
    return emit



def mk_routing_l3(xn_ap, gw_ap, gb_ap):
    def emit(C, w8d_ap):
        nc = C.nc
        q_ld = C.sem("q_ld")
        q_gw = C.sem("q_gw")
        q_mm = C.sem("q_mm")
        q_c = C.sem("q_c")
        q_tr = C.sem("q_tr")
        q_so = C.sem("q_so")
        gw0_ = q_gw.n
        with contextlib.ExitStack() as st:
            E = st.enter_context
            W = 512
            xnt = [E(nc.sbuf_tensor(f"r3_xn{c}", [128, W], BF16)) for c in range(8)]
            gwt = E(nc.sbuf_tensor("r3_gw", [128, 64], BF16))
            gbr = E(nc.sbuf_tensor("r3_gb", [1, 8], BF16))
            lg = E(nc.sbuf_tensor("r3_lg", [128, 8], F32))
            mx1 = E(nc.sbuf_tensor("r3_mx1", [128, 1], F32))
            mx2 = E(nc.sbuf_tensor("r3_mx2", [128, 1], F32))
            mk1 = E(nc.sbuf_tensor("r3_mk1", [128, 8], F32))
            mk1f = E(nc.sbuf_tensor("r3_mk1f", [128, 8], F32))
            mk2 = E(nc.sbuf_tensor("r3_mk2", [128, 8], F32))
            mk2f = E(nc.sbuf_tensor("r3_mk2f", [128, 8], F32))
            cs = E(nc.sbuf_tensor("r3_cs", [128, 8], F32))
            cse = E(nc.sbuf_tensor("r3_cse", [128, 8], F32))
            lg2 = E(nc.sbuf_tensor("r3_lg2", [128, 8], F32))
            dd = E(nc.sbuf_tensor("r3_dd", [128, 1], F32))
            wa = E(nc.sbuf_tensor("r3_wa", [128, 1], F32))
            wb_ = E(nc.sbuf_tensor("r3_wb", [128, 1], F32))
            wt1 = E(nc.sbuf_tensor("r3_wt1", [128, 8], F32))
            wt2 = E(nc.sbuf_tensor("r3_wt2", [128, 8], F32))
            wtok = E(nc.sbuf_tensor("r3_wtok", [128, 8], BF16))
            zz = E(nc.sbuf_tensor("r3_zz", [128, 8], F32))
            wrow = E(nc.sbuf_tensor("r3_wrow", [8, W], F32))
            ps_l = E(nc.psum_tensor("r3_psl", [128, 512], F32))
            ps_t = E(nc.psum_tensor("r3_pst", [8, 512], BF16))

            with nc.Block() as blk:
                @blk.gpsimd
                def _(g):
                    src = gw_ap.rearrange("(c p) e -> p c e", p=128)
                    q_gw.inc(g.dma_start(
                        out=gwt[:].rearrange("p (c e) -> p c e", e=8), in_=src), 16)
                    q_gw.inc(g.dma_start(out=gbr[:], in_=gb_ap.unsqueeze(0)), 16)

                @blk.vector
                def _(v):
                    v.memset(zz[:], 0.0)

            for ch in range(N // W):
                ld0, mm0, c0, tr0 = q_ld.n, q_mm.n, q_c.n, q_tr.n
                with nc.Block() as blk:
                    @blk.sync
                    def _(sy):
                        for c in range(8):
                            q_ld.inc(sy.dma_start(
                                out=xnt[c][:],
                                in_=xn_ap[c * 128:(c + 1) * 128,
                                          ch * W:(ch + 1) * W]), 16)

                for tt in range(4):
                    with nc.Block() as blk:
                        @blk.tensor
                        def _(t):
                            t.wait_ge(q_ld.h, ld0 + 128)
                            t.wait_ge(q_gw.h, gw0_ + 32)
                            for c in range(8):
                                t.matmul(ps_l[:, 0:8], xnt[c][:, tt * 128:
                                                              (tt + 1) * 128],
                                         gwt[:, c * 8:(c + 1) * 8],
                                         start=(c == 0), stop=False)
                            q_mm.inc(t.matmul(ps_l[:, 0:8], C.ones_b[0:1, :],
                                              gbr[:], start=False, stop=True), 1)

                        @blk.vector
                        def _(v):
                            def ve(instr):
                                q_c.inc(instr, 1)
                                v.wait_ge(q_c.h, q_c.n)
                            v.wait_ge(q_mm.h, q_mm.n)
                            ve(v.tensor_copy(out=lg[:], in_=ps_l[:, 0:8]))
                            ve(v.reduce_max(out=mx1[:], in_=lg[:], axis=AX.X))
                            ve(v.tensor_scalar(out=mk1[:], in0=lg[:],
                                               scalar1=mx1[:], scalar2=None,
                                               op0=ALU.is_equal))
                            ve(v.tensor_tensor_scan(out=cs[:], data0=mk1[:],
                                                    data1=zz[:], initial=0.0,
                                                    op0=ALU.add, op1=ALU.add))
                            ve(v.tensor_single_scalar(out=cse[:], in_=cs[:],
                                                      scalar=1.0,
                                                      op=ALU.is_equal))
                            ve(v.tensor_tensor(out=mk1f[:], in0=mk1[:],
                                               in1=cse[:], op=ALU.mult))
                            ve(v.scalar_tensor_tensor(out=lg2[:], in0=mk1f[:],
                                                      scalar=-1e30, in1=lg[:],
                                                      op0=ALU.mult, op1=ALU.add))
                            ve(v.reduce_max(out=mx2[:], in_=lg2[:], axis=AX.X))
                            ve(v.tensor_scalar(out=mk2[:], in0=lg2[:],
                                               scalar1=mx2[:], scalar2=None,
                                               op0=ALU.is_equal))
                            ve(v.tensor_tensor_scan(out=cs[:], data0=mk2[:],
                                                    data1=zz[:], initial=0.0,
                                                    op0=ALU.add, op1=ALU.add))
                            ve(v.tensor_single_scalar(out=cse[:], in_=cs[:],
                                                      scalar=1.0,
                                                      op=ALU.is_equal))
                            ve(v.tensor_tensor(out=mk2f[:], in0=mk2[:],
                                               in1=cse[:], op=ALU.mult))
                            ve(v.tensor_tensor(out=dd[:], in0=mx2[:], in1=mx1[:],
                                               op=ALU.subtract))

                    with nc.Block() as blk:
                        @blk.scalar
                        def _(s):
                            q_c.inc(s.activation(out=wa[:], in_=dd[:],
                                                 func=AF.Sigmoid, scale=-1.0), 1)
                            s.wait_ge(q_c.h, q_c.n)
                            q_c.inc(s.activation(out=wb_[:], in_=dd[:],
                                                 func=AF.Sigmoid), 1)

                    with nc.Block() as blk:
                        @blk.vector
                        def _(v):
                            def ve(instr):
                                q_c.inc(instr, 1)
                                v.wait_ge(q_c.h, q_c.n)
                            ve(v.tensor_scalar(out=wt1[:], in0=mk1f[:],
                                               scalar1=wa[:], scalar2=None,
                                               op0=ALU.mult))
                            ve(v.tensor_scalar(out=wt2[:], in0=mk2f[:],
                                               scalar1=wb_[:], scalar2=None,
                                               op0=ALU.mult))
                            ve(v.tensor_tensor(out=wtok[:], in0=wt1[:],
                                               in1=wt2[:], op=ALU.add))

                    with nc.Block() as blk:
                        @blk.tensor
                        def _(t):
                            q_tr.inc(t.transpose(ps_t[:, 0:128], wtok[:],
                                                 C.ident_b[:]), 1)

                        @blk.vector
                        def _(v):
                            v.wait_ge(q_tr.h, q_tr.n)
                            q_tr.inc(v.tensor_copy(
                                out=wrow[:, tt * 128:(tt + 1) * 128],
                                in_=ps_t[0:8, 0:128]), 1)

                with nc.Block() as blk:
                    @blk.gpsimd
                    def _(g):
                        q_so.inc(g.dma_start(
                            out=w8d_ap[:, ch * W:(ch + 1) * W], in_=wrow[:]), 16)
                        g.wait_ge(q_so.h, q_so.n)
    return emit



def emit_ffn2(C, tag, xn_ap, xn2_ap, q1_ap, s1_ap, b1_ap, q2_ap, s2_ap, b2_ap,
              x_ap, s_ap, xout_ap):
    nc = C.nc
    with contextlib.ExitStack() as st:
        E = st.enter_context
        f_ld = C.sem("f2_ld")
        f_sr = C.sem("f2_sr")
        f_bl = C.sem("f2_bl")
        f_bt = C.sem("f2_bt")
        f_cv = C.sem("f2_cv")
        f_bc = C.sem("f2_bc")
        f_dq = C.sem("f2_dq")
        f_xl = C.sem("f2_xl")
        f_h1 = C.sem("f2_h1")
        f_hs = C.sem("f2_hs")
        f_p2 = C.sem("f2_p2")
        f_gr = C.sem("f2_gr")
        f_cl = [C.sem("f2_cl0"), C.sem("f2_cl1")]
        f_cg = C.sem("f2_cg")
        f_cc = C.sem("f2_cc")
        f_so = [C.sem("f2_so0"), C.sem("f2_so1")]

        w1b = [E(nc.sbuf_tensor(f"{tag}_w1{c}", [128, 4096], BF16))
               for c in range(8)]
        w2b = [E(nc.sbuf_tensor(f"{tag}_w2{f}", [128, D], BF16))
               for f in range(32)]
        qt_ = [E(nc.sbuf_tensor(f"{tag}_qt{i}", [128, 512], I32)) for i in range(2)]
        qf_ = [E(nc.sbuf_tensor(f"{tag}_qf{i}", [128, 512], F32)) for i in range(2)]
        srow = E(nc.sbuf_tensor(f"{tag}_srow", [1, 512], F32))
        brawt = E(nc.sbuf_tensor(f"{tag}_braw", [40, 128], F32))
        bcols = E(nc.sbuf_tensor(f"{tag}_bcols", [128, 40], F32))
        xn2t = [E(nc.sbuf_tensor(f"{tag}_x2{c}", [128, 512], BF16))
                for c in range(8)]
        xnt = [E(nc.sbuf_tensor(f"{tag}_x1{c}", [128, 512], BF16))
               for c in range(8)]
        ht = [E(nc.sbuf_tensor(f"{tag}_h{f}", [128, 512], BF16)) for f in range(32)]
        grt = [E(nc.sbuf_tensor(f"{tag}_gr{c}", [128, 512], BF16))
               for c in range(8)]
        xtl = qf_
        stl = [qt_[i][:].bitcast(F32) for i in range(2)]
        gtl = [E(nc.sbuf_tensor(f"{tag}_gtl{i}", [128, 512], BF16)) for i in range(2)]
        g2l = [E(nc.sbuf_tensor(f"{tag}_g2l{i}", [128, 512], F32)) for i in range(2)]
        otl = stl
        ps_a = [E(nc.psum_tensor(f"{tag}_psa{i}", [128, 512], F32)) for i in range(2)]
        ps_b = [E(nc.psum_tensor(f"{tag}_psb{i}", [128, 512], F32)) for i in range(2)]
        ps_s = E(nc.psum_tensor(f"{tag}_pss", [128, 512], F32))

        with nc.Block() as blk:
            @blk.sync
            def _(sy):
                f_bl.inc(sy.dma_start(
                    out=brawt[0:32, :],
                    in_=b1_ap.rearrange("(a b) -> a b", b=128)), 16)
                f_bl.inc(sy.dma_start(
                    out=brawt[32:40, :],
                    in_=b2_ap.rearrange("(a b) -> a b", b=128)), 16)

            @blk.tensor
            def _(t):
                t.wait_ge(f_bl.h, f_bl.n)
                f_bt.inc(t.transpose(ps_a[0][:, 0:40], brawt[:],
                                     C.ident_f[0:40, 0:40]), 1)

            @blk.vector
            def _(v):
                v.wait_ge(f_bt.h, f_bt.n)
                f_bt.inc(v.tensor_copy(out=bcols[:], in_=ps_a[0][:, 0:40]), 1)

        DQ = []
        for c in range(8):
            for fb in range(8):
                DQ.append((1, c, fb))
        for f in range(32):
            for ob in range(2):
                DQ.append((2, f, ob))
        ld0, sr0, cv0, bc0, dq0 = f_ld.n, f_sr.n, f_cv.n, f_bc.n, f_dq.n
        with nc.Block() as blk:
            @blk.sync
            def _(sy):
                for k, (w, a, b) in enumerate(DQ):
                    if k >= 2:
                        sy.wait_ge(f_cv.h, cv0 + k - 1)
                    if w == 1:
                        src = q1_ap[a * 128:(a + 1) * 128, b * 512:(b + 1) * 512]
                    else:
                        src = q2_ap[a * 128:(a + 1) * 128, b * 512:(b + 1) * 512]
                    f_ld.inc(sy.dma_start(out=qt_[k % 2][:], in_=src), 16)

            @blk.scalar
            def _(s):
                for k, (w, a, b) in enumerate(DQ):
                    s.wait_ge(f_ld.h, ld0 + (k + 1) * 16)
                    if k >= 2:
                        s.wait_ge(f_dq.h, dq0 + k - 1)
                    f_cv.inc(s.copy(out=qf_[k % 2][:], in_=qt_[k % 2][:]), 1)

            @blk.gpsimd
            def _(g):
                for k, (w, a, b) in enumerate(DQ):
                    if k >= 1:
                        g.wait_ge(f_bc.h, bc0 + k)
                    src = (s1_ap if w == 1 else s2_ap)[a:a + 1,
                                                       b * 512:(b + 1) * 512]
                    f_sr.inc(g.dma_start(out=srow[:], in_=src), 16)

            @blk.tensor
            def _(t):
                for k, (w, a, b) in enumerate(DQ):
                    t.wait_ge(f_sr.h, sr0 + (k + 1) * 16)
                    if k >= 1:
                        t.wait_ge(f_dq.h, dq0 + k)
                    f_bc.inc(t.matmul(ps_s[:], C.ones_f[0:1, :], srow[:],
                                      start=True, stop=True), 1)

            @blk.vector
            def _(v):
                for k, (w, a, b) in enumerate(DQ):
                    v.wait_ge(f_cv.h, cv0 + k + 1)
                    v.wait_ge(f_bc.h, bc0 + k + 1)
                    dst = (w1b[a][:, b * 512:(b + 1) * 512] if w == 1
                           else w2b[a][:, b * 512:(b + 1) * 512])
                    f_dq.inc(v.scalar_tensor_tensor(
                        out=dst, in0=qf_[k % 2][:], scalar=-128.0, in1=ps_s[:],
                        op0=ALU.add, op1=ALU.mult), 1)

        for ch in range(NTC):
            tb = ch * 512
            xl0, h10, hs0, p20, gr0 = f_xl.n, f_h1.n, f_hs.n, f_p2.n, f_gr.n
            cl0 = [f_cl[0].n, f_cl[1].n]
            cg0, cc0 = f_cg.n, f_cc.n
            so0 = [f_so[0].n, f_so[1].n]
            with nc.Block() as blk:
                @blk.sync
                def _(sy):
                    for c in range(8):
                        f_xl.inc(sy.dma_start(
                            out=xn2t[c][:],
                            in_=xn2_ap[c * 128:(c + 1) * 128, tb:tb + 512]), 16)
                        f_xl.inc(sy.dma_start(
                            out=xnt[c][:],
                            in_=xn_ap[c * 128:(c + 1) * 128, tb:tb + 512]), 16)

                @blk.tensor
                def _(t):
                    t.wait_ge(f_xl.h, xl0 + 256)
                    for f in range(32):
                        if f >= 2:
                            t.wait_ge(f_hs.h, hs0 + f - 1)
                        pb = ps_a[f % 2]
                        for c in range(8):
                            i_ = t.matmul(pb[:],
                                          w1b[c][:, f * 128:(f + 1) * 128],
                                          xn2t[c][:], start=(c == 0),
                                          stop=(c == 7))
                        f_h1.inc(i_, 1)
                    for o8 in range(8):
                        t.wait_ge(f_hs.h, hs0 + 32)
                        if o8 >= 2:
                            t.wait_ge(f_gr.h, gr0 + 2 * (o8 - 2) + 1)
                        pb = ps_b[o8 % 2]
                        for f in range(32):
                            i_ = t.matmul(pb[:],
                                          w2b[f][:, o8 * 128:(o8 + 1) * 128],
                                          ht[f][:], start=(f == 0),
                                          stop=(f == 31))
                        f_p2.inc(i_, 1)

                @blk.scalar
                def _(s):
                    for f in range(32):
                        s.wait_ge(f_h1.h, h10 + f + 1)
                        f_hs.inc(s.activation(out=ht[f][:], in_=ps_a[f % 2][:],
                                              func=AF.Silu,
                                              bias=bcols[:, f:f + 1]), 1)

                @blk.vector
                def _(v):
                    for o8 in range(8):
                        v.wait_ge(f_p2.h, p20 + o8 + 1)
                        if o8 >= 2:
                            v.wait_ge(f_gr.h, gr0 + 2 * (o8 - 2) + 2)
                        f_gr.inc(v.tensor_scalar(
                            out=g2l[o8 % 2][:], in0=ps_b[o8 % 2][:],
                            scalar1=bcols[:, 32 + o8:33 + o8], scalar2=0.5,
                            op0=ALU.add, op1=ALU.mult), 1)
                        v.wait_ge(f_gr.h, f_gr.n)
                        f_gr.inc(v.tensor_tensor(out=grt[o8][:],
                                                 in0=g2l[o8 % 2][:],
                                                 in1=xnt[o8][:], op=ALU.add), 1)

            with nc.Block() as blk:
                @blk.sync
                def _(sy):
                    for c in range(8):
                        sy.wait_ge(f_cc.h, cc0 + max(0, 2 * (c - 1)))
                        sy.wait_ge(f_so[c % 2].h, so0[c % 2] + (c // 2) * 16)
                        f_cl[c % 2].inc(sy.dma_start(
                            out=xtl[c % 2][:],
                            in_=x_ap[c * 128:(c + 1) * 128, tb:tb + 512]), 16)
                        f_cl[c % 2].inc(sy.dma_start(
                            out=stl[c % 2],
                            in_=s_ap[c * 128:(c + 1) * 128, tb:tb + 512]), 16)

                @blk.scalar
                def _(s):
                    for c in range(8):
                        s.wait_ge(f_cc.h, cc0 + max(0, 2 * (c - 2) + 1))
                        f_cg.inc(s.activation(out=gtl[c % 2][:], in_=grt[c][:],
                                              func=AF.Silu), 1)

                @blk.vector
                def _(v):
                    for c in range(8):
                        v.wait_ge(f_cl[c % 2].h, cl0[c % 2] + (c // 2 + 1) * 32)
                        v.wait_ge(f_cg.h, cg0 + c + 1)
                        v.wait_ge(f_so[c % 2].h, so0[c % 2] + (c // 2) * 16)
                        f_cc.inc(v.tensor_tensor(out=g2l[c % 2][:],
                                                 in0=stl[c % 2],
                                                 in1=gtl[c % 2][:],
                                                 op=ALU.mult), 1)
                        v.wait_ge(f_cc.h, f_cc.n)
                        f_cc.inc(v.tensor_tensor(out=otl[c % 2],
                                                 in0=xtl[c % 2][:],
                                                 in1=g2l[c % 2][:],
                                                 op=ALU.add), 1)

                @blk.gpsimd
                def _(g):
                    for c in range(8):
                        g.wait_ge(f_cc.h, cc0 + 2 * (c + 1))
                        f_so[c % 2].inc(g.dma_start(
                            out=xout_ap[c * 128:(c + 1) * 128, tb:tb + 512],
                            in_=otl[c % 2]), 16)

        with nc.Block() as blk:
            @blk.gpsimd
            def _(g):
                g.wait_ge(f_so[0].h, f_so[0].n)
                g.wait_ge(f_so[1].h, f_so[1].n)



def emit_attn3(C, tag, xn_ap, win_ap, convw_ap, convb_ap, wx_ap, wdt_ap,
               bdt_ap, alog_ap, dvec_ap, wout_ap, s_ap):
    nc = C.nc
    DI = 2048
    TH = 512
    NQ = T // TH
    sres_d = nc.dram_tensor(f"{tag}_sres", [DI, N], BF16, kind="Internal").ap()
    with contextlib.ExitStack() as st:
        E = st.enter_context
        z_ld = C.sem("z_ld")
        z_w = [C.sem("z_w0"), C.sem("z_w1")]
        z_wx = C.sem("z_wx")
        z_wg = C.sem("z_wg")
        z_bt = C.sem("z_bt")
        z_b2 = C.sem("z_b2")
        z_ae = C.sem("z_ae")
        z_mm = C.sem("z_mm")
        z_ev = C.sem("z_ev")
        z_rs = C.sem("z_rs")
        z_sr = C.sem("z_sr")
        z_cv = C.sem("z_cv")
        z_si = C.sem("z_si")
        z_xd = C.sem("z_xd")
        z_xe = C.sem("z_xe")
        d_mm = C.sem("d_mm")
        d_s = C.sem("d_s")
        d_v = C.sem("d_v")
        z_pb = C.sem("z_pb")
        z_pe = C.sem("z_pe")
        z_ss = C.sem("z_ss")
        z_sc = C.sem("z_sc")
        z_sp = C.sem("z_sp")
        z_sl = C.sem("z_sl")
        z_ga = C.sem("z_ga")
        z_om = C.sem("z_om")
        z_oe = C.sem("z_oe")
        z_so = [C.sem("z_so0"), C.sem("z_so1")]
        wx0_, ae0_ = z_wx.n, z_ae.n

        xnt = [E(nc.sbuf_tensor(f"{tag}_xn{c}", [128, TH], BF16)) for c in range(8)]
        ws = [E(nc.sbuf_tensor(f"{tag}_ws{i}", [128, D], BF16)) for i in range(2)]
        wos = [E(nc.sbuf_tensor(f"{tag}_wo{i}", [128, DI], BF16))
               for i in range(2)]
        xsp = [E(nc.sbuf_tensor(f"{tag}_xs{m}", [128, TH + 3], BF16))
               for m in range(16)]
        ut = [E(nc.sbuf_tensor(f"{tag}_u{m}", [128, TH], BF16)) for m in range(16)]
        dlt = [E(nc.sbuf_tensor(f"{tag}_dl{m}", [128, TH], BF16))
               for m in range(16)]
        dut = [E(nc.sbuf_tensor(f"{tag}_du{m}", [128, TH], BF16))
               for m in range(16)]
        Bbc = [E(nc.sbuf_tensor(f"{tag}_Bb{n}", [128, TH], BF16))
               for n in range(16)]
        Cbc = [E(nc.sbuf_tensor(f"{tag}_Cb{n}", [128, TH], BF16))
               for n in range(16)]
        wxt = E(nc.sbuf_tensor(f"{tag}_wxt", [128, 16 * 96], BF16))
        wdts = E(nc.sbuf_tensor(f"{tag}_wdts", [64, DI], BF16))
        dtb = E(nc.sbuf_tensor(f"{tag}_dtb", [64, TH], BF16))
        Bsb = E(nc.sbuf_tensor(f"{tag}_Bsb", [16, TH], BF16))
        Csb = E(nc.sbuf_tensor(f"{tag}_Csb", [16, TH], BF16))
        cw = [E(nc.sbuf_tensor(f"{tag}_cw{m}", [128, 4], F32)) for m in range(16)]
        Acl = [E(nc.sbuf_tensor(f"{tag}_Ac{m}", [128, 16], F32)) for m in range(16)]
        craw = E(nc.sbuf_tensor(f"{tag}_craw", [32, 128], F32))
        ccol = E(nc.sbuf_tensor(f"{tag}_ccol", [128, 32], F32))
        braw2 = E(nc.sbuf_tensor(f"{tag}_braw2", [16, 128], F32))
        bdtc = E(nc.sbuf_tensor(f"{tag}_bdtc", [128, 16], F32))
        selt = E(nc.sbuf_tensor(f"{tag}_sel", [16, DI], BF16))
        selo = E(nc.sbuf_tensor(f"{tag}_selo", [16, DI], BF16))
        ascr = E(nc.sbuf_tensor(f"{tag}_ascr", [128, TH], F32))
        bscr = E(nc.sbuf_tensor(f"{tag}_bscr", [128, TH], F32))
        hscr = E(nc.sbuf_tensor(f"{tag}_hscr", [128, TH], F32))
        cscr = E(nc.sbuf_tensor(f"{tag}_cscr", [128, TH], F32))
        csc2 = E(nc.sbuf_tensor(f"{tag}_csc2", [128, TH], F32))
        dscr = E(nc.sbuf_tensor(f"{tag}_dscr", [128, TH], F32))
        yacc = [E(nc.sbuf_tensor(f"{tag}_ya{i}", [128, TH], F32)) for i in range(2)]
        ysc = E(nc.sbuf_tensor(f"{tag}_ysc", [128, TH], F32))
        srt = E(nc.sbuf_tensor(f"{tag}_srt", [128, TH], BF16))
        hcar = [E(nc.sbuf_tensor(f"{tag}_hc{m}", [128, 16], F32))
                for m in range(16)]
        sst = [E(nc.sbuf_tensor(f"{tag}_sst{i}", [128, TH], F32))
               for i in range(2)]
        ps_a = [E(nc.psum_tensor(f"{tag}_psa{i}", [128, 512], F32)) for i in range(2)]
        ps_b = [E(nc.psum_tensor(f"{tag}_psb{i}", [128, 512], F32)) for i in range(2)]
        ps_dt = E(nc.psum_tensor(f"{tag}_psdt", [64, 512], F32))
        ps_B = E(nc.psum_tensor(f"{tag}_psB", [16, 512], F32))
        ps_C = E(nc.psum_tensor(f"{tag}_psC", [16, 512], F32))

        with nc.Block() as blk:
            @blk.sync
            def _(sy):
                z_wx.inc(sy.dma_start(
                    out=craw[0:16, :],
                    in_=convb_ap.rearrange("(a b) -> a b", b=128)), 16)
                z_wx.inc(sy.dma_start(
                    out=craw[16:32, :],
                    in_=dvec_ap.rearrange("(a b) -> a b", b=128)), 16)
                z_wx.inc(sy.dma_start(
                    out=braw2[:],
                    in_=bdt_ap.rearrange("(a b) -> a b", b=128)), 16)

            @blk.gpsimd
            def _(g):
                z_wg.inc(g.dma_start(out=wdts[:], in_=wdt_ap), 16)
                z_wg.inc(g.dma_start(
                    out=wxt[:].rearrange("p (c f) -> p c f", f=96),
                    in_=wx_ap.rearrange("(c p) f -> p c f", p=128)), 16)
                for m in range(16):
                    z_wg.inc(g.dma_start(
                        out=cw[m][:], in_=convw_ap[m * 128:(m + 1) * 128, :]), 16)
                    z_wg.inc(g.dma_start(
                        out=Acl[m][:], in_=alog_ap[m * 128:(m + 1) * 128, :]), 16)

            @blk.vector
            def _(v):
                v.memset(selo[:], 1.0)

            @blk.tensor
            def _(t):
                t.wait_ge(z_wx.h, wx0_ + 48)
                z_bt.inc(t.transpose(ps_a[0][:, 0:32], craw[:],
                                     C.ident_f[0:32, 0:32]), 1)
                z_bt.inc(t.transpose(ps_b[0][:, 0:16], braw2[:],
                                     C.ident_f[0:16, 0:16]), 1)

        with nc.Block() as blk:
            @blk.gpsimd
            def _(g):
                g.affine_select(out=selt[:], in_=selo[:],
                                pattern=[[-1, 16], [0, 128]],
                                compare_op=ALU.is_equal, fill=0.0, base=0,
                                channel_multiplier=1)

            @blk.vector
            def _(v):
                z_b2.inc(v.tensor_copy(out=ccol[:], in_=ps_a[0][:, 0:32]), 1)
                z_b2.inc(v.tensor_copy(out=bdtc[:], in_=ps_b[0][:, 0:16]), 1)

        with nc.Block() as blk:
            @blk.scalar
            def _(s):
                s.wait_ge(z_wg.h, z_wg.n)
                for m in range(16):
                    z_ae.inc(s.activation(out=hcar[m][:], in_=Acl[m][:],
                                          func=AF.Exp), 1)

            @blk.vector
            def _(v):
                for m in range(16):
                    v.wait_ge(z_ae.h, ae0_ + m + 1)
                    z_b2.inc(v.tensor_scalar_mul(out=Acl[m][:], in0=hcar[m][:],
                                                 scalar1=-1.0), 1)

        for s_ in range(SPC):
            for hf in range(NQ):
                t0 = s_ * T + hf * TH
                ld0, mm0, ev0, rs0 = z_ld.n, z_mm.n, z_ev.n, z_rs.n
                w_ready = []
                with nc.Block() as blk:
                    @blk.sync
                    def _(sy):
                        for c in range(8):
                            z_ld.inc(sy.dma_start(
                                out=xnt[c][:],
                                in_=xn_ap[c * 128:(c + 1) * 128, t0:t0 + TH]), 16)

                    @blk.gpsimd
                    def _(g):
                        for m in range(32):
                            if m >= 2:
                                g.wait_ge(z_mm.h, mm0 + m - 1)
                            src = win_ap[:, m * 128:(m + 1) * 128]
                            src = src.rearrange("(c p) f -> p c f", p=128)
                            dst = ws[m % 2][:].rearrange("p (c f) -> p c f", f=128)
                            w_ready.append(z_w[m % 2].inc(
                                g.dma_start(out=dst, in_=src), 16))
                        for m in range(16):
                            g.wait_ge(z_rs.h, rs0 + m + 1)
                            z_sr.inc(g.dma_start(
                                out=sres_d[m * 128:(m + 1) * 128, t0:t0 + TH],
                                in_=dut[m][:]), 16)

                    @blk.tensor
                    def _(t):
                        t.wait_ge(z_ld.h, ld0 + 128)
                        for m in range(32):
                            t.wait_ge(z_w[m % 2].h, w_ready[m])
                            if m >= 2:
                                if m - 2 < 16:
                                    t.wait_ge(z_ev.h, ev0 + m - 1)
                                else:
                                    t.wait_ge(z_rs.h, rs0 + m - 17)
                            pb = ps_a[m % 2]
                            for c in range(8):
                                i_ = t.matmul(
                                    pb[:], ws[m % 2][:, c * 128:(c + 1) * 128],
                                    xnt[c][:], start=(c == 0), stop=(c == 7))
                            z_mm.inc(i_, 1)

                    @blk.vector
                    def _(v):
                        for m in range(16):
                            v.wait_ge(z_mm.h, mm0 + m + 1)
                            z_ev.inc(v.tensor_copy(out=xsp[m][:, 3:3 + TH],
                                                   in_=ps_a[m % 2][:]), 1)

                    @blk.scalar
                    def _(s):
                        for m in range(16, 32):
                            s.wait_ge(z_mm.h, mm0 + m + 1)
                            z_rs.inc(s.activation(out=dut[m - 16][:],
                                                  in_=ps_a[m % 2][:],
                                                  func=AF.Silu), 1)

                cv0, si0 = z_cv.n, z_si.n
                P_CV = 6 if hf == 0 else 5
                with nc.Block() as blk:
                    @blk.vector
                    def _(v):
                        def ve(instr):
                            z_cv.inc(instr, 1)
                            v.wait_ge(z_cv.h, z_cv.n)
                        v.wait_ge(z_wg.h, z_wg.n)
                        for m in range(16):
                            if m >= 1:
                                v.wait_ge(z_si.h, si0 + m)
                            if hf == 0:
                                ve(v.memset(xsp[m][:, 0:3], 0.0))
                            ve(v.tensor_scalar(out=ascr[:],
                                               in0=xsp[m][:, 3:3 + TH],
                                               scalar1=cw[m][:, 3:4],
                                               scalar2=None, op0=ALU.mult))
                            ve(v.scalar_tensor_tensor(
                                out=cscr[:], in0=xsp[m][:, 2:2 + TH],
                                scalar=cw[m][:, 2:3], in1=ascr[:],
                                op0=ALU.mult, op1=ALU.add))
                            ve(v.scalar_tensor_tensor(
                                out=ascr[:], in0=xsp[m][:, 1:1 + TH],
                                scalar=cw[m][:, 1:2], in1=cscr[:],
                                op0=ALU.mult, op1=ALU.add))
                            ve(v.scalar_tensor_tensor(
                                out=cscr[:], in0=xsp[m][:, 0:TH],
                                scalar=cw[m][:, 0:1], in1=ascr[:],
                                op0=ALU.mult, op1=ALU.add))
                            ve(v.tensor_copy(out=xsp[m][:, 0:3],
                                             in_=xsp[m][:, TH:TH + 3]))

                    @blk.scalar
                    def _(s):
                        for m in range(16):
                            s.wait_ge(z_cv.h, cv0 + (m + 1) * P_CV)
                            z_si.inc(s.activation(out=ut[m][:], in_=cscr[:],
                                                  func=AF.Silu,
                                                  bias=ccol[:, m:m + 1]), 1)

                xd0, xe0 = z_xd.n, z_xe.n
                with nc.Block() as blk:
                    @blk.tensor
                    def _(t):
                        t.wait_ge(z_wg.h, z_wg.n)
                        for c in range(16):
                            t.matmul(ps_dt[:], wxt[:, c * 96:c * 96 + 64],
                                     ut[c][:], start=(c == 0), stop=(c == 15))
                        for c in range(16):
                            t.matmul(ps_B[:], wxt[:, c * 96 + 64:c * 96 + 80],
                                     ut[c][:], start=(c == 0), stop=(c == 15))
                        for c in range(16):
                            i_ = t.matmul(ps_C[:],
                                          wxt[:, c * 96 + 80:c * 96 + 96],
                                          ut[c][:], start=(c == 0), stop=(c == 15))
                        z_xd.inc(i_, 1)

                    @blk.vector
                    def _(v):
                        v.wait_ge(z_xd.h, xd0 + 1)
                        z_xe.inc(v.tensor_copy(out=dtb[:], in_=ps_dt[:]), 1)
                        z_xe.inc(v.tensor_copy(out=Bsb[:], in_=ps_B[:]), 1)
                        z_xe.inc(v.tensor_copy(out=Csb[:], in_=ps_C[:]), 1)

                dm0, ds0, dv0 = d_mm.n, d_s.n, d_v.n
                with nc.Block() as blk:
                    @blk.tensor
                    def _(t):
                        for m in range(16):
                            if m >= 2:
                                t.wait_ge(d_s.h, ds0 + 2 * (m - 2) + 1)
                            d_mm.inc(t.matmul(
                                ps_a[m % 2][:], wdts[:, m * 128:(m + 1) * 128],
                                dtb[:], start=True, stop=True), 1)

                    @blk.scalar
                    def _(s):
                        for m in range(16):
                            s.wait_ge(d_mm.h, dm0 + m + 1)
                            if m >= 1:
                                s.wait_ge(d_s.h, ds0 + 2 * (m - 1) + 2)
                            d_s.inc(s.activation(out=dscr[:], in_=ps_a[m % 2][:],
                                                 func=AF.Exp,
                                                 bias=bdtc[:, m:m + 1]), 1)
                            s.wait_ge(d_s.h, d_s.n)
                            d_s.inc(s.activation(out=dlt[m][:], in_=dscr[:],
                                                 func=AF.Ln, bias=1.0), 1)

                    @blk.vector
                    def _(v):
                        for m in range(16):
                            v.wait_ge(d_s.h, ds0 + 2 * m + 2)
                            v.wait_ge(z_sr.h, z_sr.n)
                            d_v.inc(v.tensor_tensor(out=dut[m][:], in0=dlt[m][:],
                                                    in1=ut[m][:], op=ALU.mult), 1)

                pb0, pe0 = z_pb.n, z_pe.n
                with nc.Block() as blk:
                    @blk.tensor
                    def _(t):
                        k = 0
                        for n in range(16):
                            for si in range(2):
                                srcs = (Bsb, Csb)[si]
                                if k >= 2:
                                    t.wait_ge(z_pe.h, pe0 + k - 1)
                                z_pb.inc(t.matmul(
                                    ps_a[k % 2][:],
                                    selt[:, n * 128:(n + 1) * 128],
                                    srcs[:], start=True, stop=True), 1)
                                k += 1

                    @blk.scalar
                    def _(s):
                        k = 0
                        for n in range(16):
                            for si in range(2):
                                dsts = (Bbc, Cbc)[si]
                                s.wait_ge(z_pb.h, pb0 + k + 1)
                                z_pe.inc(s.copy(out=dsts[n][:],
                                                in_=ps_a[k % 2][:]), 1)
                                k += 1

                sc0, sp0, ss0, ga0, sl0 = z_sc.n, z_sp.n, z_ss.n, z_ga.n, z_sl.n
                hsc = [hscr, dscr]
                csc = [cscr, csc2]
                with nc.Block() as blk:
                    @blk.sync
                    def _(sy):
                        sy.wait_ge(z_sr.h, z_sr.n)
                        for m in range(16):
                            sy.wait_ge(z_ga.h, ga0 + 2 * m)
                            z_sl.inc(sy.dma_start(
                                out=srt[:],
                                in_=sres_d[m * 128:(m + 1) * 128, t0:t0 + TH]),
                                16)

                    @blk.scalar
                    def _(s):
                        for m in range(16):
                            for n in range(16):
                                k = m * 16 + n
                                if k >= 1:
                                    s.wait_ge(z_sc.h, sc0 + 3 * (k - 1) + 2)
                                z_ss.inc(s.activation(
                                    out=ascr[:], in_=dlt[m][:], func=AF.Exp,
                                    scale=Acl[m][:, n:n + 1]), 1)
                                if k >= 1:
                                    km = k - 1
                                    z_ss.inc(s.copy(
                                        out=hcar[km // 16][:, km % 16:km % 16 + 1],
                                        in_=hsc[km % 2][:, TH - 1:TH]), 1)
                        s.wait_ge(z_sc.h, sc0 + 3 * 255 + 2)
                        z_ss.inc(s.copy(out=hcar[15][:, 15:16],
                                        in_=hsc[255 % 2][:, TH - 1:TH]), 1)

                    @blk.gpsimd
                    def _(g):
                        for m in range(16):
                            for n in range(16):
                                k = m * 16 + n
                                g.wait_ge(z_sc.h, sc0 + 3 * k + 2)
                                z_sp.inc(g.tensor_tensor(out=csc[k % 2][:],
                                                         in0=hsc[k % 2][:],
                                                         in1=Cbc[n][:],
                                                         op=ALU.mult), 1)

                    @blk.vector
                    def _(v):
                        def yacc_op(kk):
                            nn = kk % 16
                            v.wait_ge(z_sp.h, sp0 + kk + 1)
                            if nn == 0:
                                z_sc.inc(v.tensor_copy(out=yacc[1][:],
                                                       in_=csc[kk % 2][:]), 1)
                            else:
                                z_sc.inc(v.tensor_tensor(
                                    out=yacc[(nn + 1) % 2][:],
                                    in0=yacc[nn % 2][:], in1=csc[kk % 2][:],
                                    op=ALU.add), 1)
                            v.wait_ge(z_sc.h, z_sc.n)

                        def m_tail(mm):
                            v.wait_ge(z_sl.h, sl0 + (mm + 1) * 16)
                            z_ga.inc(v.scalar_tensor_tensor(
                                out=ysc[:], in0=ut[mm][:],
                                scalar=ccol[:, 16 + mm:17 + mm], in1=yacc[0][:],
                                op0=ALU.mult, op1=ALU.add), 1)
                            v.wait_ge(z_ga.h, z_ga.n)
                            z_ga.inc(v.tensor_tensor(out=ut[mm][:], in0=ysc[:],
                                                     in1=srt[:], op=ALU.mult), 1)

                        for m in range(16):
                            for n in range(16):
                                k = m * 16 + n
                                v.wait_ge(z_ss.h, ss0 + (2 * k if k >= 1 else 1))
                                z_sc.inc(v.tensor_tensor(out=bscr[:],
                                                         in0=dut[m][:],
                                                         in1=Bbc[n][:],
                                                         op=ALU.mult), 1)
                                v.wait_ge(z_sc.h, z_sc.n)
                                if k >= 2:
                                    v.wait_ge(z_sp.h, sp0 + k - 1)
                                init = (0.0 if hf == 0 else hcar[m][:, n:n + 1])
                                z_sc.inc(v.tensor_tensor_scan(
                                    out=hsc[k % 2][:], data0=ascr[:],
                                    data1=bscr[:], initial=init,
                                    op0=ALU.mult, op1=ALU.add), 1)
                                v.wait_ge(z_sc.h, z_sc.n)
                                if k >= 1:
                                    yacc_op(k - 1)
                                else:
                                    z_sc.inc(v.engine_nop(), 1)
                                    v.wait_ge(z_sc.h, z_sc.n)
                                if k >= 1 and (k - 1) % 16 == 15:
                                    m_tail((k - 1) // 16)
                        yacc_op(255)
                        m_tail(15)

                om0, oe0 = z_om.n, z_oe.n
                soO0 = [z_so[0].n, z_so[1].n]
                wo_ready = []
                with nc.Block() as blk:
                    @blk.gpsimd
                    def _(g):
                        def wodma(o8):
                            src = wout_ap[:, o8 * 128:(o8 + 1) * 128]
                            src = src.rearrange("(c p) f -> p c f", p=128)
                            dst = wos[o8 % 2][:].rearrange("p (c f) -> p c f",
                                                           f=128)
                            wo_ready.append(z_w[o8 % 2].inc(
                                g.dma_start(out=dst, in_=src), 16))
                        wodma(0)
                        wodma(1)
                        for o8 in range(8):
                            g.wait_ge(z_oe.h, oe0 + o8 + 1)
                            z_so[o8 % 2].inc(g.dma_start(
                                out=s_ap[o8 * 128:(o8 + 1) * 128, t0:t0 + TH],
                                in_=sst[o8 % 2][:]), 16)
                            if o8 + 2 < 8:
                                g.wait_ge(z_om.h, om0 + o8 + 1)
                                wodma(o8 + 2)

                    @blk.tensor
                    def _(t):
                        for o8 in range(8):
                            t.wait_ge(z_w[o8 % 2].h, wo_ready[o8])
                            if o8 >= 2:
                                t.wait_ge(z_oe.h, oe0 + o8 - 1)
                            pb = ps_b[o8 % 2]
                            for c in range(16):
                                i_ = t.matmul(
                                    pb[:], wos[o8 % 2][:, c * 128:(c + 1) * 128],
                                    ut[c][:], start=(c == 0), stop=(c == 15))
                            z_om.inc(i_, 1)

                    @blk.vector
                    def _(v):
                        for o8 in range(8):
                            v.wait_ge(z_om.h, om0 + o8 + 1)
                            v.wait_ge(z_so[o8 % 2].h,
                                      soO0[o8 % 2] + (o8 // 2) * 16)
                            z_oe.inc(v.tensor_copy(out=sst[o8 % 2][:],
                                                   in_=ps_b[o8 % 2][:]), 1)

        with nc.Block() as blk:
            @blk.gpsimd
            def _(g):
                g.wait_ge(z_so[0].h, z_so[0].n)
                g.wait_ge(z_so[1].h, z_so[1].n)
                g.wait_ge(z_sr.h, z_sr.n)



def emit_ffn1(C, tag, xn_ap, wc_ap, bc_ap, w1_ap, b1_ap, w2_ap, b2_ap,
              x_ap, s_ap, xout_ap):
    nc = C.nc
    with contextlib.ExitStack() as st:
        E = st.enter_context
        n_pre = C.sem("n_pre")
        n_prg = C.sem("n_prg")
        n_prw = C.sem("n_prw")
        n_tr = C.sem("n_tr")
        n_te = C.sem("n_te")
        n_io = C.sem("n_io")
        n_ld = C.sem("n_ld")
        n_lg = C.sem("n_lg")
        n_lv = C.sem("n_lv")
        n_w1 = [C.sem("n_w10"), C.sem("n_w11")]
        n_y1 = C.sem("n_y1")
        n_ys = C.sem("n_ys")
        n_px = C.sem("n_px")
        n_gv = C.sem("n_gv")
        n_rm = C.sem("n_rm")
        n_gr = C.sem("n_gr")
        n_cl = [C.sem("n_cl0"), C.sem("n_cl1")]
        n_cg = C.sem("n_cg")
        n_cc = C.sem("n_cc")
        n_so = [C.sem("n_so0"), C.sem("n_so1")]
        io0_, pre0_, prw0_ = n_io.n, n_pre.n, n_prw.n

        xnt = [E(nc.sbuf_tensor(f"{tag}_xn{c}", [128, 512], BF16))
               for c in range(8)]
        wct = E(nc.sbuf_tensor(f"{tag}_wct", [128, 1024], BF16))
        bccol = E(nc.sbuf_tensor(f"{tag}_bcc", [128, 1], F32))
        tierow = E(nc.sbuf_tensor(f"{tag}_tie", [1, 512], BF16))
        tiei = E(nc.sbuf_tensor(f"{tag}_tiei", [1, 512], I32))
        tie2 = E(nc.sbuf_tensor(f"{tag}_tie2", [1, 512], I32))
        tief = E(nc.sbuf_tensor(f"{tag}_tief", [1, 512], F32))
        W2T = [E(nc.sbuf_tensor(f"{tag}_w2t{i}", [128, D], BF16))
               for i in range(32)]
        w2stg = E(nc.sbuf_tensor(f"{tag}_w2s", [128, 256], BF16))
        b1raw = E(nc.sbuf_tensor(f"{tag}_b1r", [32, 128], F32))
        b1col = E(nc.sbuf_tensor(f"{tag}_b1c", [128, 32], F32))
        b2f = E(nc.sbuf_tensor(f"{tag}_b2f", [128, D], BF16))
        selp = [E(nc.sbuf_tensor(f"{tag}_sp{i}", [128, 128], BF16))
                for i in range(32)]
        selo = E(nc.sbuf_tensor(f"{tag}_spo", [128, 128], BF16))
        lgT = E(nc.sbuf_tensor(f"{tag}_lgT", [128, 512], F32))
        mxg = E(nc.sbuf_tensor(f"{tag}_mxg", [128, 16], F32))
        permT = E(nc.sbuf_tensor(f"{tag}_pT", [128, 512], BF16))
        w1s = [E(nc.sbuf_tensor(f"{tag}_w1{i}", [128, 1024], BF16))
               for i in range(2)]
        Y1 = [E(nc.sbuf_tensor(f"{tag}_Y{f}", [128, 512], BF16))
              for f in range(32)]
        Gt = [E(nc.sbuf_tensor(f"{tag}_G{f}", [128, 512], BF16))
              for f in range(32)]
        grt = [E(nc.sbuf_tensor(f"{tag}_gr{c}", [128, 512], BF16))
               for c in range(8)]
        xtl = [E(nc.sbuf_tensor(f"{tag}_xtl{i}", [128, 512], F32)) for i in range(2)]
        stl = [E(nc.sbuf_tensor(f"{tag}_stl{i}", [128, 512], F32)) for i in range(2)]
        gtl = [E(nc.sbuf_tensor(f"{tag}_gtl{i}", [128, 512], BF16)) for i in range(2)]
        g2l = [E(nc.sbuf_tensor(f"{tag}_g2l{i}", [128, 512], F32)) for i in range(2)]
        otl = [E(nc.sbuf_tensor(f"{tag}_otl{i}", [128, 512], F32)) for i in range(2)]
        ps_a = [E(nc.psum_tensor(f"{tag}_psa{i}", [128, 512], F32)) for i in range(2)]
        ps_b = [E(nc.psum_tensor(f"{tag}_psb{i}", [128, 512], F32)) for i in range(2)]

        with nc.Block() as blk:
            @blk.sync
            def _(sy):
                n_pre.inc(sy.dma_start(out=b1raw[:],
                                       in_=b1_ap.rearrange("a b f -> (a b f)")
                                       .rearrange("(a b) -> a b", b=128)), 16)
                n_pre.inc(sy.dma_start(out=bccol[:],
                                       in_=bc_ap.rearrange("a b -> (a b)")
                                       .unsqueeze(1)), 16)

            @blk.gpsimd
            def _(g):
                n_prw.inc(g.dma_start(
                    out=wct[:].rearrange("p (c e) -> p c e", e=128),
                    in_=wc_ap.rearrange("(c p) a b -> p c (a b)", p=128)), 16)
                n_prw.inc(g.dma_start(
                    out=b2f[:], in_=b2_ap.rearrange("a b d -> (a b) d")), 16)
                n_io.inc(g.iota(tiei[:], pattern=[[1, 512]], base=0,
                                channel_multiplier=0,
                                allow_small_or_imprecise_dtypes=True), 1)

            @blk.vector
            def _(v):
                v.memset(selo[:], 1.0)
                v.wait_ge(n_io.h, io0_ + 1)
                n_io.inc(v.tensor_single_scalar(out=tie2[:], in_=tiei[:],
                                                scalar=31, op=ALU.bitwise_and), 1)

            @blk.scalar
            def _(s):
                s.wait_ge(n_io.h, io0_ + 2)
                n_io.inc(s.copy(out=tief[:], in_=tie2[:]), 1)

        with nc.Block() as blk:
            @blk.vector
            def _(v):
                n_io.inc(v.tensor_scalar_mul(out=tierow[:], in0=tief[:],
                                             scalar1=1e-6 / 31.0), 1)

            @blk.gpsimd
            def _(g):
                for te in range(32):
                    g.affine_select(out=selp[te][:], in_=selo[:],
                                    pattern=[[-1, 4], [0, 32]],
                                    compare_op=ALU.is_equal, fill=0.0,
                                    base=-4 * te, channel_multiplier=1)

            @blk.tensor
            def _(t):
                t.wait_ge(n_pre.h, pre0_ + 32)
                n_tr.inc(t.transpose(ps_a[0][:, 0:32], b1raw[:],
                                     C.ident_f[0:32, 0:32]), 1)

        with nc.Block() as blk:
            @blk.vector
            def _(v):
                v.tensor_copy(out=b1col[:], in_=ps_a[0][:, 0:32])

        tr0, te0 = n_tr.n, n_te.n
        w2_ready = []
        with nc.Block() as blk:
            @blk.gpsimd
            def _(g):
                for es in range(128):
                    if es >= 1:
                        g.wait_ge(n_tr.h, tr0 + (es - 1) * 8 + 8)
                    src = w2_ap.rearrange("a b (c p) f -> (a b) p c f", p=128)
                    src = src[es]
                    dst = w2stg[:].rearrange("p (c f) -> p c f", f=32)
                    w2_ready.append(n_prg.inc(g.dma_start(out=dst, in_=src), 16))

            @blk.tensor
            def _(t):
                for es in range(128):
                    t.wait_ge(n_prg.h, w2_ready[es])
                    for c in range(8):
                        k = es * 8 + c
                        if k >= 2:
                            t.wait_ge(n_te.h, te0 + k - 1)
                        n_tr.inc(t.transpose(
                            ps_a[k % 2][:].bitcast(BF16)[0:32, 0:128],
                            w2stg[:, c * 32:(c + 1) * 32],
                            C.ident_b[:]), 1)

            @blk.vector
            def _(v):
                for es in range(128):
                    for c in range(8):
                        k = es * 8 + c
                        v.wait_ge(n_tr.h, tr0 + k + 1)
                        n_te.inc(v.tensor_copy(
                            out=W2T[es // 4][32 * (es % 4):32 * (es % 4) + 32,
                                             c * 128:(c + 1) * 128],
                            in_=ps_a[k % 2][:].bitcast(BF16)[0:32, 0:128]), 1)

        for ch in range(NTC):
            tb = ch * 512
            ld0 = n_ld.n
            lg0, lv0 = n_lg.n, n_lv.n
            y10, ys0 = n_y1.n, n_ys.n
            px0, gv0, rm0, gr0 = n_px.n, n_gv.n, n_rm.n, n_gr.n
            w1_ready = []
            with nc.Block() as blk:
                @blk.sync
                def _(sy):
                    for c in range(8):
                        n_ld.inc(sy.dma_start(
                            out=xnt[c][:],
                            in_=xn_ap[c * 128:(c + 1) * 128, tb:tb + 512]), 16)

                @blk.tensor
                def _(t):
                    t.wait_ge(n_ld.h, ld0 + 128)
                    t.wait_ge(n_prw.h, prw0_ + 32)
                    t.wait_ge(n_lv.h, lv0)
                    for c in range(8):
                        t.matmul(ps_b[0][:], wct[:, c * 128:(c + 1) * 128],
                                 xnt[c][:], start=(c == 0), stop=False)
                    t.wait_ge(n_io.h, io0_ + 4)
                    n_lg.inc(t.matmul(ps_b[0][:], C.ones_b[0:1, :], tierow[:],
                                      start=False, stop=True), 1)

                @blk.vector
                def _(v):
                    def ve(instr):
                        n_lv.inc(instr, 1)
                        v.wait_ge(n_lv.h, n_lv.n)
                    v.wait_ge(n_lg.h, lg0 + 1)
                    ve(v.tensor_scalar(out=lgT[:], in0=ps_b[0][:],
                                       scalar1=bccol[:], scalar2=None,
                                       op0=ALU.add))
                    ve(v.tensor_reduce(out=mxg[:],
                                       in_=lgT[:].rearrange("p (g w) -> p g w",
                                                            w=32),
                                       axis=AX.X, op=ALU.max))
                    ve(v.tensor_tensor(
                        out=permT[:].rearrange("p (g w) -> p g w", w=32),
                        in0=lgT[:].rearrange("p (g w) -> p g w", w=32),
                        in1=mxg[:].unsqueeze(2).broadcast_to([128, 16, 32]),
                        op=ALU.is_equal))

            with nc.Block() as blk:
                @blk.gpsimd
                def _(g):
                    for ft in range(32):
                        if ft >= 2:
                            g.wait_ge(n_y1.h, y10 + ft - 1)
                        src4 = w1_ap.rearrange("a b (c p) f -> p (a b) c f",
                                               p=128)
                        dst4 = w1s[ft % 2][:].rearrange("p (c e f) -> p e c f",
                                                        e=4, f=32)
                        for e4 in range(4):
                            n_w1[ft % 2].inc(g.dma_start(
                                out=dst4[:, e4], in_=src4[:, 4 * ft + e4]), 16)
                        w1_ready.append(n_w1[ft % 2].n)

                @blk.tensor
                def _(t):
                    for ft in range(32):
                        t.wait_ge(n_w1[ft % 2].h, w1_ready[ft])
                        if ft >= 2:
                            t.wait_ge(n_ys.h, ys0 + ft - 1)
                        pb = ps_a[ft % 2]
                        for c in range(8):
                            i_ = t.matmul(pb[:],
                                          w1s[ft % 2][:, c * 128:(c + 1) * 128],
                                          xnt[c][:], start=(c == 0),
                                          stop=(c == 7))
                        n_y1.inc(i_, 1)

                @blk.scalar
                def _(s):
                    for ft in range(32):
                        s.wait_ge(n_y1.h, y10 + ft + 1)
                        n_ys.inc(s.activation(out=Y1[ft][:], in_=ps_a[ft % 2][:],
                                              func=AF.Relu,
                                              bias=b1col[:, ft:ft + 1]), 1)

            with nc.Block() as blk:
                @blk.tensor
                def _(t):
                    for ft in range(32):
                        if ft >= 2:
                            t.wait_ge(n_gv.h, gv0 + ft - 1)
                        n_px.inc(t.matmul(ps_b[ft % 2][:], selp[ft][:],
                                          permT[:], start=True, stop=True), 1)

                @blk.vector
                def _(v):
                    for ft in range(32):
                        v.wait_ge(n_px.h, px0 + ft + 1)
                        n_gv.inc(v.tensor_tensor(out=Gt[ft][:], in0=Y1[ft][:],
                                                 in1=ps_b[ft % 2][:],
                                                 op=ALU.mult), 1)

            with nc.Block() as blk:
                @blk.tensor
                def _(t):
                    for o8 in range(8):
                        if o8 >= 2:
                            t.wait_ge(n_gr.h, gr0 + o8 - 1)
                        pb = ps_a[o8 % 2]
                        for ft in range(32):
                            t.matmul(pb[:], W2T[ft][:, o8 * 128:(o8 + 1) * 128],
                                     Gt[ft][:], start=(ft == 0), stop=False)
                        i_ = t.matmul(pb[:], b2f[:, o8 * 128:(o8 + 1) * 128],
                                      permT[:], start=False, stop=True)
                        n_rm.inc(i_, 1)

                @blk.vector
                def _(v):
                    for o8 in range(8):
                        v.wait_ge(n_rm.h, rm0 + o8 + 1)
                        n_gr.inc(v.tensor_copy(out=grt[o8][:],
                                               in_=ps_a[o8 % 2][:]), 1)

            cg0, cc0 = n_cg.n, n_cc.n
            cl0 = [n_cl[0].n, n_cl[1].n]
            so0 = [n_so[0].n, n_so[1].n]
            with nc.Block() as blk:
                @blk.sync
                def _(sy):
                    for c in range(8):
                        sy.wait_ge(n_cc.h, cc0 + max(0, 2 * (c - 1)))
                        n_cl[c % 2].inc(sy.dma_start(
                            out=xtl[c % 2][:],
                            in_=x_ap[c * 128:(c + 1) * 128, tb:tb + 512]), 16)
                        n_cl[c % 2].inc(sy.dma_start(
                            out=stl[c % 2][:],
                            in_=s_ap[c * 128:(c + 1) * 128, tb:tb + 512]), 16)

                @blk.scalar
                def _(s):
                    for c in range(8):
                        s.wait_ge(n_cc.h, cc0 + max(0, 2 * (c - 2) + 1))
                        n_cg.inc(s.activation(out=gtl[c % 2][:], in_=grt[c][:],
                                              func=AF.Silu), 1)

                @blk.vector
                def _(v):
                    for c in range(8):
                        v.wait_ge(n_cl[c % 2].h, cl0[c % 2] + (c // 2 + 1) * 32)
                        v.wait_ge(n_cg.h, cg0 + c + 1)
                        v.wait_ge(n_so[c % 2].h, so0[c % 2] + (c // 2) * 16)
                        n_cc.inc(v.tensor_tensor(out=g2l[c % 2][:],
                                                 in0=stl[c % 2][:],
                                                 in1=gtl[c % 2][:],
                                                 op=ALU.mult), 1)
                        v.wait_ge(n_cc.h, n_cc.n)
                        n_cc.inc(v.tensor_tensor(out=otl[c % 2][:],
                                                 in0=xtl[c % 2][:],
                                                 in1=g2l[c % 2][:],
                                                 op=ALU.add), 1)

                @blk.gpsimd
                def _(g):
                    for c in range(8):
                        g.wait_ge(n_cc.h, cc0 + 2 * (c + 1))
                        n_so[c % 2].inc(g.dma_start(
                            out=xout_ap[c * 128:(c + 1) * 128, tb:tb + 512],
                            in_=otl[c % 2][:]), 16)

        with nc.Block() as blk:
            @blk.gpsimd
            def _(g):
                g.wait_ge(n_so[0].h, n_so[0].n)
                g.wait_ge(n_so[1].h, n_so[1].n)



def emit_attn2(C, tag, xn_ap, keyw_ap, keyb_ap, vqw_ap, outw_ap, outb_ap, s_ap):
    nc = C.nc
    CH = 128
    NCH = T // CH
    with contextlib.ExitStack() as st:
        E = st.enter_context
        g_ld = C.sem("g_ld")
        g_w = [C.sem("g_w0"), C.sem("g_w1")]
        g_tr = C.sem("g_tr")
        g_te = [C.sem("g_te0"), C.sem("g_te1")]
        g_bl = C.sem("g_bl")
        g_bt = C.sem("g_bt")
        bl0_, bt0_ = g_bl.n, g_bt.n
        g_mm = C.sem("g_mm")
        g_ev = C.sem("g_ev")
        g_kv = C.sem("g_kv")
        g_ks = C.sem("g_ks")
        g_bd = C.sem("g_bd")
        g_rm = C.sem("g_rm")
        g_rv = C.sem("g_rv")
        g_rs = C.sem("g_rs")
        g_ms = C.sem("g_ms")
        e_mm = C.sem("g_emm")
        e_ev = C.sem("g_eev")
        e_so = [C.sem("g_es0"), C.sem("g_es1")]

        at = [E(nc.sbuf_tensor(f"{tag}_a{c}", [128, T], BF16)) for c in range(8)]
        vt = [E(nc.sbuf_tensor(f"{tag}_v{i}", [128, D], BF16)) for i in range(16)]
        Yf = [E(nc.sbuf_tensor(f"{tag}_Y{c}", [128, T], BF16)) for c in range(8)]
        wv = [E(nc.sbuf_tensor(f"{tag}_wv{c}", [128, D], BF16)) for c in range(8)]
        wqk = [E(nc.sbuf_tensor(f"{tag}_wqk{i}", [128, D], BF16)) for i in range(2)]
        wstg = [E(nc.sbuf_tensor(f"{tag}_ws{i}", [128, D], BF16)) for i in range(2)]
        braw = E(nc.sbuf_tensor(f"{tag}_braw", [16, 128], F32))
        bcols = E(nc.sbuf_tensor(f"{tag}_bcols", [128, 16], F32))
        qf = E(nc.sbuf_tensor(f"{tag}_qf", [128, T], BF16))
        kp = E(nc.sbuf_tensor(f"{tag}_kp", [128, T], BF16))
        Lc = E(nc.sbuf_tensor(f"{tag}_Lc", [128, T], F32))
        s1 = E(nc.sbuf_tensor(f"{tag}_s1", [128, T], F32))
        s2 = E(nc.sbuf_tensor(f"{tag}_s2", [128, T], F32))
        qlv = [E(nc.sbuf_tensor(f"{tag}_ql{i}", [128, T], BF16)) for i in range(4)]
        klv = [E(nc.sbuf_tensor(f"{tag}_kl{i}", [128, T], BF16)) for i in range(4)]
        qcs = E(nc.sbuf_tensor(f"{tag}_qcs", [128, CH], BF16))
        ecs = E(nc.sbuf_tensor(f"{tag}_ecs", [128, CH], F32))
        dcs = E(nc.sbuf_tensor(f"{tag}_dcs", [128, CH], F32))
        khf = E(nc.sbuf_tensor(f"{tag}_khf", [128, CH], BF16))
        khtm = E(nc.sbuf_tensor(f"{tag}_khtm", [128, 64], BF16))
        rsel = E(nc.sbuf_tensor(f"{tag}_rsel", [128, CH], BF16))
        zero_b = E(nc.sbuf_tensor(f"{tag}_zerob", [128, CH], BF16))
        mcau = E(nc.sbuf_tensor(f"{tag}_mcau", [128, CH], U8))
        mcf = E(nc.sbuf_tensor(f"{tag}_mcf", [128, CH], F32))
        pcc = E(nc.sbuf_tensor(f"{tag}_pcc", [64, 2], F32))
        Hst = [[E(nc.sbuf_tensor(f"{tag}_H{h}{i}", [64, 64], F32))
                for i in range(2)] for h in range(2)]
        Htmp = E(nc.sbuf_tensor(f"{tag}_Htmp", [64, 64], F32))
        s3 = E(nc.sbuf_tensor(f"{tag}_s3", [128, T], F32))
        Hbf2 = E(nc.sbuf_tensor(f"{tag}_Hb2", [128, 64], BF16))
        sst = [E(nc.sbuf_tensor(f"{tag}_sst{i}", [128, 512], F32))
               for i in range(2)]
        ps_p = [E(nc.psum_tensor(f"{tag}_psp{i}", [128, 512], F32))
                for i in range(2)]
        ps_r = [E(nc.psum_tensor(f"{tag}_psr{i}", [128, 512], F32))
                for i in range(2)]
        ps_y = [E(nc.psum_tensor(f"{tag}_psy{i}", [128, 512], F32))
                for i in range(2)]
        ps_h = E(nc.psum_tensor(f"{tag}_psh", [64, 512], F32))
        ps_t = E(nc.psum_tensor(f"{tag}_pst", [128, 512], F32))

        LEV = [16, 32, 64]

        with nc.Block() as blk:
            @blk.sync
            def _(sy):
                g_bl.inc(sy.dma_start(
                    out=braw[0:8, :],
                    in_=keyb_ap.rearrange("(a b) -> a b", b=128)), 16)
                g_bl.inc(sy.dma_start(
                    out=braw[8:16, :],
                    in_=outb_ap.rearrange("(a b) -> a b", b=128)), 16)

            @blk.vector
            def _(v):
                v.memset(zero_b[:], 0.0)
                v.memset(s1[:, 0:CH], 1.0)

            @blk.tensor
            def _(t):
                t.wait_ge(g_bl.h, bl0_ + 32)
                g_bt.inc(t.transpose(ps_p[0][:, 0:16], braw[:],
                                     C.ident_f[0:16, 0:16]), 1)

        with nc.Block() as blk:
            @blk.gpsimd
            def _(g):
                g.affine_select(out=mcf[:], in_=s1[:, 0:CH],
                                pattern=[[1, CH]], compare_op=ALU.is_ge,
                                fill=0.0, base=0, channel_multiplier=-1)

            @blk.vector
            def _(v):
                v.wait_ge(g_bt.h, bt0_ + 1)
                g_bt.inc(v.tensor_copy(out=bcols[:], in_=ps_p[0][:, 0:16]), 1)

        with nc.Block() as blk:
            @blk.gpsimd
            def _(g):
                g_ms.inc(g.tensor_copy(out=mcau[:], in_=mcf[:]), 1)

        tr0 = g_tr.n
        te0 = [g_te[0].n, g_te[1].n]
        wv_ready = []
        with nc.Block() as blk:
            @blk.gpsimd
            def _(g):
                for cb in range(8):
                    if cb >= 2:
                        g.wait_ge(g_tr.h, tr0 + (cb - 1) * 8)
                    wv_ready.append(g_w[cb % 2].inc(g.dma_start(
                        out=wstg[cb % 2][:],
                        in_=vqw_ap[cb * 128:(cb + 1) * 128, :]), 16))

            @blk.tensor
            def _(t):
                for cb in range(8):
                    t.wait_ge(g_w[cb % 2].h, wv_ready[cb])
                    for c in range(8):
                        k = cb * 8 + c
                        if k >= 2:
                            t.wait_ge(g_te[k % 2].h, te0[k % 2] + k // 2)
                        g_tr.inc(t.transpose(
                            ps_r[k % 2][:].bitcast(BF16)[:, 0:128],
                            wstg[cb % 2][:, c * 128:(c + 1) * 128],
                            C.ident_b[:]), 1)

            @blk.vector
            def _(v):
                for cb in range(8):
                    for c in range(8):
                        k = cb * 8 + c
                        v.wait_ge(g_tr.h, tr0 + k + 1)
                        g_te[k % 2].inc(v.tensor_copy(
                            out=wv[c][:, cb * 128:(cb + 1) * 128],
                            in_=ps_r[k % 2][:].bitcast(BF16)[:, 0:128]), 1)

        for s_ in range(SPC):
            t0 = s_ * T
            ld0, mm0, ev0 = g_ld.n, g_mm.n, g_ev.n
            with nc.Block() as blk:
                @blk.sync
                def _(sy):
                    for c in range(8):
                        g_ld.inc(sy.dma_start(
                            out=at[c][:],
                            in_=xn_ap[c * 128:(c + 1) * 128, t0:t0 + T]), 16)

                @blk.tensor
                def _(t):
                    t.wait_ge(g_ld.h, ld0 + 128)
                    for i16 in range(16):
                        for fh in range(2):
                            g_ = i16 * 2 + fh
                            if g_ >= 2:
                                t.wait_ge(g_ev.h, ev0 + g_ - 1)
                            pb = ps_p[g_ % 2]
                            for c in range(8):
                                i_ = t.matmul(
                                    pb[:], at[c][:, i16 * 128:(i16 + 1) * 128],
                                    wv[c][:, fh * 512:(fh + 1) * 512],
                                    start=(c == 0), stop=(c == 7))
                            g_mm.inc(i_, 1)

                @blk.vector
                def _(v):
                    for i16 in range(16):
                        for fh in range(2):
                            g_ = i16 * 2 + fh
                            v.wait_ge(g_mm.h, mm0 + g_ + 1)
                            g_ev.inc(v.tensor_copy(
                                out=vt[i16][:, fh * 512:(fh + 1) * 512],
                                in_=ps_p[g_ % 2][:]), 1)

            for hp in range(8):
                mmh0, evh0 = g_mm.n, g_ev.n
                trh0 = g_tr.n
                teh0 = [g_te[0].n, g_te[1].n]
                wr = []
                with nc.Block() as blk:
                    @blk.gpsimd
                    def _(g):
                        for j in range(2):
                            wap = keyw_ap if j == 0 else vqw_ap
                            rw = hp * 128 if j == 0 else 1024 + hp * 128
                            wr.append(g_w[j % 2].inc(g.dma_start(
                                out=wstg[j % 2][:],
                                in_=wap[rw:rw + 128, :]), 16))

                    @blk.tensor
                    def _(t):
                        for j in range(2):
                            t.wait_ge(g_w[j % 2].h, wr[j])
                            if j >= 1:
                                t.wait_ge(g_te[0].h, teh0[0] + 8)
                            for c in range(8):
                                k = j * 8 + c
                                if c >= 2:
                                    t.wait_ge(g_te[j % 2].h,
                                              teh0[j % 2] + (j // 2) * 8 + c - 1)
                                g_tr.inc(t.transpose(
                                    ps_r[c % 2][:].bitcast(BF16)[:, 0:128],
                                    wstg[j % 2][:, c * 128:(c + 1) * 128],
                                    C.ident_b[:]), 1)
                            t.wait_ge(g_te[j % 2].h, teh0[j % 2] + (j // 2 + 1) * 8)
                            for tch in range(4):
                                g_ = j * 4 + tch
                                if g_ >= 2:
                                    t.wait_ge(g_ev.h, evh0 + g_ - 1)
                                pb = ps_p[g_ % 2]
                                for c in range(8):
                                    i_ = t.matmul(
                                        pb[:],
                                        wqk[j % 2][:, c * 128:(c + 1) * 128],
                                        at[c][:, tch * 512:(tch + 1) * 512],
                                        start=(c == 0), stop=(c == 7))
                                g_mm.inc(i_, 1)

                    @blk.vector
                    def _(v):
                        for j in range(2):
                            for c in range(8):
                                k = j * 8 + c
                                v.wait_ge(g_tr.h, trh0 + k + 1)
                                g_te[j % 2].inc(v.tensor_copy(
                                    out=wqk[j % 2][:, c * 128:(c + 1) * 128],
                                    in_=ps_r[c % 2][:].bitcast(BF16)[:, 0:128]),
                                    1)
                            for tch in range(4):
                                g_ = j * 4 + tch
                                v.wait_ge(g_mm.h, mmh0 + g_ + 1)
                                if j == 0:
                                    g_ev.inc(v.tensor_scalar(
                                        out=s3[:, tch * 512:(tch + 1) * 512],
                                        in0=ps_p[g_ % 2][:],
                                        scalar1=bcols[:, hp:hp + 1],
                                        scalar2=None, op0=ALU.add), 1)
                                else:
                                    g_ev.inc(v.tensor_copy(
                                        out=qf[:, tch * 512:(tch + 1) * 512],
                                        in_=ps_p[g_ % 2][:]), 1)

                kv0, ks0 = g_kv.n, g_ks.n
                with nc.Block() as blk:
                    @blk.scalar
                    def _(s):
                        g_ks.inc(s.activation(out=kp[:], in_=s3[:],
                                              func=AF.Sigmoid, scale=-1.0), 1)
                        g_ks.inc(s.activation(out=s1[:], in_=s3[:],
                                              func=AF.Abs), 1)
                        s.wait_ge(g_ks.h, g_ks.n)
                        g_ks.inc(s.activation(out=s2[:], in_=s1[:], func=AF.Exp,
                                              scale=-1.0), 1)
                        s.wait_ge(g_ks.h, g_ks.n)
                        g_ks.inc(s.activation(out=s1[:], in_=s2[:], func=AF.Ln,
                                              bias=1.0), 1)

                    @blk.vector
                    def _(v):
                        def ve(instr):
                            g_kv.inc(instr, 1)
                            v.wait_ge(g_kv.h, g_kv.n)
                        v.wait_ge(g_ks.h, ks0 + 4)
                        ve(v.tensor_scalar_min(out=Lc[:], in0=s3[:],
                                               scalar1=0.0))
                        ve(v.tensor_tensor(out=s2[:], in0=Lc[:], in1=s1[:],
                                           op=ALU.subtract))
                        ve(v.tensor_tensor_scan(out=Lc[:], data0=s2[:],
                                                data1=s2[:], initial=0.0,
                                                op0=ALU.add, op1=ALU.bypass))

                bd0, ks1 = g_bd.n, g_ks.n
                with nc.Block() as blk:
                    @blk.vector
                    def _(v):
                        def ve(instr):
                            g_bd.inc(instr, 1)
                            v.wait_ge(g_bd.h, g_bd.n)
                        for li in range(4):
                            if li < 3:
                                ell = LEV[li]
                                nb = T // (2 * ell)
                                lmv = Lc[:, ell - 1::2 * ell]
                                lmv = lmv.unsqueeze(2).broadcast_to(
                                    [128, nb, 2 * ell])
                                d3 = s1[:].rearrange("p (b w) -> p b w",
                                                     w=2 * ell)
                                l3 = Lc[:].rearrange("p (b w) -> p b w",
                                                     w=2 * ell)
                            else:
                                nb = T // 16
                                lmv = Lc[:, 0::16]
                                lmv = lmv.unsqueeze(2).broadcast_to(
                                    [128, nb, 16])
                                d3 = s1[:].rearrange("p (b w) -> p b w", w=16)
                                l3 = Lc[:].rearrange("p (b w) -> p b w", w=16)
                            ve(v.tensor_tensor(out=d3, in0=l3, in1=lmv,
                                               op=ALU.subtract))
                            ve(v.tensor_scalar_min(out=s2[:], in0=s1[:],
                                                   scalar1=0.0))
                            v.wait_ge(g_ks.h, ks1 + 2 * li + 1)
                            ve(v.tensor_tensor(out=qlv[li][:], in0=qf[:],
                                               in1=s3[:], op=ALU.mult))
                            if li < 3:
                                ve(v.tensor_scalar_max(out=s2[:], in0=s1[:],
                                                       scalar1=0.0))
                            else:
                                ve(v.tensor_copy(out=s2[:], in_=s1[:]))
                            v.wait_ge(g_ks.h, ks1 + 2 * li + 2)
                            ve(v.tensor_tensor(out=klv[li][:], in0=kp[:],
                                               in1=s3[:], op=ALU.mult))

                    @blk.scalar
                    def _(s):
                        for li in range(4):
                            s.wait_ge(g_bd.h, bd0 + li * 5 + 2)
                            g_ks.inc(s.activation(out=s3[:], in_=s2[:],
                                                  func=AF.Exp), 1)
                            s.wait_ge(g_bd.h, bd0 + li * 5 + 4)
                            g_ks.inc(s.activation(out=s3[:], in_=s2[:],
                                                  func=AF.Exp, scale=-1.0), 1)

                with nc.Block() as blk:
                    @blk.vector
                    def _(v):
                        for h2 in range(2):
                            v.memset(Hst[h2][0][:], 0.0)
                        v.memset(Hbf2[:], 0.0)

                rm0, rv0, rs0 = g_rm.n, g_rv.n, g_rs.n
                with nc.Block() as blk:
                    @blk.scalar
                    def _(s):
                        for ch in range(NCH):
                            for h2 in range(2):
                                k = ch * 2 + h2
                                hb = h2 * 64
                                s.wait_ge(g_rv.h, rv0 + k * 11 + 1)
                                g_rs.inc(s.activation(out=ecs[hb:hb + 64, :],
                                                      in_=dcs[hb:hb + 64, :],
                                                      func=AF.Exp), 1)
                                s.wait_ge(g_rv.h, rv0 + k * 11 + 3)
                                g_rs.inc(s.activation(out=ecs[hb:hb + 64, :],
                                                      in_=dcs[hb:hb + 64, :],
                                                      func=AF.Exp, scale=-1.0),
                                         1)
                                s.wait_ge(g_rv.h, rv0 + k * 11 + 5)
                                g_rs.inc(s.activation(out=pcc[:, 1:2],
                                                      in_=pcc[:, 0:1],
                                                      func=AF.Exp), 1)

                    @blk.vector
                    def _(v):
                        for ch in range(NCH):
                            cs, ce = ch * CH, (ch + 1) * CH
                            for h2 in range(2):
                                k = ch * 2 + h2
                                hb = h2 * 64
                                v.wait_ge(g_rv.h, g_rv.n)
                                rc_ = max(cs - 1, 0)
                                g_rv.inc(v.tensor_scalar(
                                    out=dcs[hb:hb + 64, :],
                                    in0=Lc[hb:hb + 64, cs:ce],
                                    scalar1=Lc[hb:hb + 64, rc_:rc_ + 1],
                                    scalar2=None, op0=ALU.subtract), 1)
                                v.wait_ge(g_rs.h, rs0 + k * 3 + 1)
                                if k >= 1:
                                    v.wait_ge(g_rm.h, rm0 + (k - 1) * 11 + 1)
                                g_rv.inc(v.tensor_tensor(
                                    out=qcs[hb:hb + 64, :],
                                    in0=qf[hb:hb + 64, cs:ce],
                                    in1=ecs[hb:hb + 64, :], op=ALU.mult), 1)
                                v.wait_ge(g_rv.h, g_rv.n)
                                g_rv.inc(v.tensor_scalar(
                                    out=dcs[hb:hb + 64, :],
                                    in0=Lc[hb:hb + 64, cs:ce],
                                    scalar1=Lc[hb:hb + 64, ce - 1:ce],
                                    scalar2=None, op0=ALU.subtract), 1)
                                v.wait_ge(g_rs.h, rs0 + k * 3 + 2)
                                if k >= 1:
                                    v.wait_ge(g_rm.h, rm0 + (k - 1) * 11 + 10)
                                g_rv.inc(v.tensor_tensor(
                                    out=khf[hb:hb + 64, :],
                                    in0=kp[hb:hb + 64, cs:ce],
                                    in1=ecs[hb:hb + 64, :], op=ALU.mult), 1)
                                v.wait_ge(g_rv.h, g_rv.n)
                                g_rv.inc(v.tensor_tensor(
                                    out=pcc[:, 0:1],
                                    in0=Lc[hb:hb + 64, ce - 1:ce],
                                    in1=Lc[hb:hb + 64, rc_:rc_ + 1],
                                    op=ALU.subtract), 1)
                                v.wait_ge(g_rm.h, rm0 + k * 11 + 9)
                                v.wait_ge(g_rv.h, g_rv.n)
                                g_rv.inc(v.select(
                                    out=rsel[:], mask=mcau[:],
                                    on_true=ps_r[k % 2][:, 0:CH],
                                    on_false=zero_b[:], add_drain=True), 1)
                                v.wait_ge(g_rm.h, rm0 + k * 11 + 10)
                                g_rv.inc(v.tensor_copy(
                                    out=khtm[:],
                                    in_=ps_t[:].bitcast(BF16)[:, 0:64]), 1)
                                v.wait_ge(g_rm.h, rm0 + k * 11 + 11)
                                v.wait_ge(g_rs.h, rs0 + k * 3 + 3)
                                cur = ch % 2
                                g_rv.inc(v.tensor_scalar(
                                    out=Htmp[:],
                                    in0=Hst[h2][cur][:],
                                    scalar1=pcc[:, 1:2], scalar2=None,
                                    op0=ALU.mult), 1)
                                v.wait_ge(g_rv.h, g_rv.n)
                                g_rv.inc(v.tensor_tensor(
                                    out=Hst[h2][1 - cur][:],
                                    in0=Htmp[:],
                                    in1=ps_h[:, 0:64], op=ALU.add), 1)
                                v.wait_ge(g_rv.h, g_rv.n)
                                g_rv.inc(v.tensor_copy(
                                    out=Hbf2[hb:hb + 64, :],
                                    in_=Hst[h2][1 - cur][:]), 1)
                                v.wait_ge(g_rv.h, g_rv.n)
                                g_rv.inc(v.tensor_copy(
                                    out=Yf[hp][hb:hb + 64, cs:ce],
                                    in_=ps_y[k % 2][0:64, 0:CH]), 1)

                    @blk.tensor
                    def _(t):
                        for ch in range(NCH):
                            cs, ce = ch * CH, (ch + 1) * CH
                            for h2 in range(2):
                                k = ch * 2 + h2
                                hb = h2 * 64
                                hgl = (hp * 2 + h2) * 64
                                t.wait_ge(g_rv.h, rv0 + k * 11 + 2)
                                g_rm.inc(t.matmul(ps_y[k % 2][0:64, 0:CH],
                                                  Hbf2[hb:hb + 64, :],
                                                  qcs[hb:hb + 64, :],
                                                  start=True, stop=False), 1)
                                g_rm.inc(t.matmul(
                                    ps_r[k % 2][:, 0:CH],
                                    klv[3][hb:hb + 64, cs:ce],
                                    qlv[3][hb:hb + 64, cs:ce],
                                    start=True, stop=True), 1)
                                for kk in range(4):
                                    g_rm.inc(t.matmul(
                                        ps_r[k % 2][32 * kk:32 * kk + 16,
                                                    32 * kk + 16:32 * kk + 32],
                                        klv[0][hb:hb + 64,
                                               cs + 32 * kk:cs + 32 * kk + 16],
                                        qlv[0][hb:hb + 64, cs + 32 * kk + 16:
                                               cs + 32 * kk + 32],
                                        start=True, stop=True,
                                        tile_position=(hb, 32 * kk)), 1)
                                for kk in range(2):
                                    g_rm.inc(t.matmul(
                                        ps_r[k % 2][64 * kk:64 * kk + 32,
                                                    64 * kk + 32:64 * kk + 64],
                                        klv[1][hb:hb + 64,
                                               cs + 64 * kk:cs + 64 * kk + 32],
                                        qlv[1][hb:hb + 64, cs + 64 * kk + 32:
                                               cs + 64 * kk + 64],
                                        start=True, stop=True), 1)
                                g_rm.inc(t.matmul(
                                    ps_r[k % 2][0:64, 64:128],
                                    klv[2][hb:hb + 64, cs:cs + 64],
                                    qlv[2][hb:hb + 64, cs + 64:cs + 128],
                                    start=True, stop=True), 1)
                                t.wait_ge(g_rv.h, rv0 + k * 11 + 4)
                                g_rm.inc(t.transpose(
                                    ps_t[:].bitcast(BF16)[:, 0:64],
                                    khf[hb:hb + 64, :],
                                    C.ident_b[hb:hb + 64, hb:hb + 64]), 1)
                                t.wait_ge(g_rv.h, rv0 + k * 11 + 7)
                                t.matmul(ps_y[k % 2][0:64, 0:CH],
                                         vt[ch][:, hgl:hgl + 64],
                                         rsel[:], start=False, stop=True)
                                g_rm.inc(t.matmul(
                                    ps_h[:, 0:64], khtm[:],
                                    vt[ch][:, hgl:hgl + 64],
                                    start=True, stop=True), 1)

            mmE0, evE0 = e_mm.n, e_ev.n
            trE0 = g_tr.n
            teE0 = [g_te[0].n, g_te[1].n]
            soE0 = [e_so[0].n, e_so[1].n]
            wp_ready = []
            with nc.Block() as blk:
                @blk.gpsimd
                def _(g):
                    def wpdma(o8):
                        wp_ready.append(g_w[o8 % 2].inc(g.dma_start(
                            out=wstg[o8 % 2][:],
                            in_=outw_ap[o8 * 128:(o8 + 1) * 128, :]), 16))
                    wpdma(0)
                    wpdma(1)
                    for o8 in range(8):
                        for tch in range(4):
                            j = o8 * 4 + tch
                            g.wait_ge(e_ev.h, evE0 + j + 1)
                            e_so[j % 2].inc(g.dma_start(
                                out=s_ap[o8 * 128:(o8 + 1) * 128,
                                         t0 + tch * 512:t0 + (tch + 1) * 512],
                                in_=sst[j % 2][:]), 16)
                        if o8 + 2 < 8:
                            g.wait_ge(e_mm.h, mmE0 + (o8 + 1) * 4)
                            wpdma(o8 + 2)

                @blk.tensor
                def _(t):
                    for o8 in range(8):
                        t.wait_ge(g_w[o8 % 2].h, wp_ready[o8])
                        if o8 >= 1:
                            t.wait_ge(g_te[(o8 - 1) % 2].h,
                                      teE0[(o8 - 1) % 2] + ((o8 - 1) // 2 + 1) * 8)
                        for c in range(8):
                            kq = o8 * 8 + c
                            if c >= 2:
                                t.wait_ge(g_te[o8 % 2].h,
                                          teE0[o8 % 2] + (o8 // 2) * 8 + c - 1)
                            g_tr.inc(t.transpose(
                                ps_r[c % 2][:].bitcast(BF16)[:, 0:128],
                                wstg[o8 % 2][:, c * 128:(c + 1) * 128],
                                C.ident_b[:]), 1)
                        t.wait_ge(g_te[o8 % 2].h, teE0[o8 % 2] + (o8 // 2 + 1) * 8)
                        for tch in range(4):
                            j = o8 * 4 + tch
                            if j >= 2:
                                t.wait_ge(e_ev.h, evE0 + j - 1)
                            pb = ps_p[j % 2]
                            for c in range(8):
                                i_ = t.matmul(
                                    pb[:], wqk[o8 % 2][:, c * 128:(c + 1) * 128],
                                    Yf[c][:, tch * 512:(tch + 1) * 512],
                                    start=(c == 0), stop=(c == 7))
                            e_mm.inc(i_, 1)

                @blk.vector
                def _(v):
                    for o8 in range(8):
                        for c in range(8):
                            kq = o8 * 8 + c
                            v.wait_ge(g_tr.h, trE0 + kq + 1)
                            g_te[o8 % 2].inc(v.tensor_copy(
                                out=wqk[o8 % 2][:, c * 128:(c + 1) * 128],
                                in_=ps_r[c % 2][:].bitcast(BF16)[:, 0:128]), 1)
                        for tch in range(4):
                            j = o8 * 4 + tch
                            v.wait_ge(e_mm.h, mmE0 + j + 1)
                            v.wait_ge(e_so[j % 2].h, soE0[j % 2] + (j // 2) * 16)
                            e_ev.inc(v.tensor_scalar(
                                out=sst[j % 2][:], in0=ps_p[j % 2][:],
                                scalar1=bcols[:, 8 + o8:9 + o8], scalar2=None,
                                op0=ALU.add), 1)

        with nc.Block() as blk:
            @blk.gpsimd
            def _(g):
                g.wait_ge(e_so[0].h, e_so[0].n)
                g.wait_ge(e_so[1].h, e_so[1].n)



AF32 = np.float32
ABF = ml_dtypes.bfloat16
AFM = [D, N]

_WSPECS = [
    ("wa",   "L0_attn_W_attn", [D, 3 * D], 0),
    ("ba",   "L0_attn_b_attn", [3 * D], 0),
    ("wp",   "L0_attn_W_proj", [D, D], 0),
    ("bp",   "L0_attn_b_proj", [D], 0),
    ("m0w1", "L0_ffn_W1", [8, D, 4096], 0),
    ("m0b1", "L0_ffn_b1", [8, 4096], 0),
    ("m0w2", "L0_ffn_W2", [8, 4096, D], 0),
    ("m0b2", "L0_ffn_b2", [8, D], 0),
    ("wq1",  "L1_attn_W_Q", [D, D], 0),
    ("wk1",  "L1_attn_W_K", [D, D], 0),
    ("wv1",  "L1_attn_W_V", [D, D], 0),
    ("wo1",  "L1_attn_W_O", [D, D], 0),
    ("n1wc", "L1_ffn_Wc", [D, 32, 4], 0),
    ("n1bc", "L1_ffn_bc", [32, 4], 0),
    ("n1w1", "L1_ffn_W1", [32, 4, D, 32], 0),
    ("n1b1", "L1_ffn_b1", [32, 4, 32], 0),
    ("n1w2", "L1_ffn_W2", [32, 4, D, 32], 0),
    ("n1b2", "L1_ffn_b2", [32, 4, D], 0),
    ("kw2",  "L2_attn_key_w", [D, D], 0),
    ("kb2",  "L2_attn_key_b", [D], 0),
    ("vq2",  "L2_attn_vq_w", [2 * D, D], 0),
    ("ow2",  "L2_attn_out_w", [D, D], 0),
    ("ob2",  "L2_attn_out_b", [D], 0),
    ("lg2",  "L2_ffn_ln_gamma", [D], 0),
    ("lb2",  "L2_ffn_ln_beta", [D], 0),
    ("f2q1", "L2_ffn_fc1_q", [D, 4096], 1),
    ("f2s1", "L2_ffn_fc1_scales", [8, 4096], 0),
    ("f2b1", "L2_ffn_fc1_bias", [4096], 0),
    ("f2q2", "L2_ffn_fc2_q", [4096, D], 1),
    ("f2s2", "L2_ffn_fc2_scales", [32, D], 0),
    ("f2b2", "L2_ffn_fc2_bias", [D], 0),
    ("win3", "L3_attn_W_in", [D, 4096], 0),
    ("cw3",  "L3_attn_conv_w", [2048, 4], 0),
    ("cb3",  "L3_attn_conv_b", [2048], 0),
    ("wx3",  "L3_attn_W_x", [2048, 96], 0),
    ("wdt3", "L3_attn_W_dt", [64, 2048], 0),
    ("bdt3", "L3_attn_b_dt", [2048], 0),
    ("alog3", "L3_attn_A_log", [2048, 16], 0),
    ("dv3",  "L3_attn_D", [2048], 0),
    ("wout3", "L3_attn_W_out", [2048, D], 0),
    ("gw3",  "L3_ffn_gate_w", [D, 8], 0),
    ("gb3",  "L3_ffn_gate_b", [8], 0),
    ("m3w1", "L3_ffn_W1", [8, D, 4096], 0),
    ("m3b1", "L3_ffn_b1", [8, 4096], 0),
    ("m3w2", "L3_ffn_W2", [8, 4096, D], 0),
    ("m3b2", "L3_ffn_b2", [8, D], 0),
]


def build_fused():
    nc = bass.Bass("TRN2", target_bir_lowering=False)
    i = {"x": nc.dram_tensor("x", [N, D], F32, kind="ExternalInput").ap()}
    for bn, _k, sh, isint in _WSPECS:
        i[bn] = nc.dram_tensor(bn, sh, I32 if isint else F32,
                               kind="ExternalInput").ap()
    out = nc.dram_tensor("xo", [N, D], F32, kind="ExternalOutput").ap()
    xfm = [nc.dram_tensor(f"xfm{p}", AFM, F32, kind="Internal").ap()
           for p in range(2)]
    xn = nc.dram_tensor("xn_d", AFM, BF16, kind="Internal").ap()
    xn2 = nc.dram_tensor("xn2_d", AFM, BF16, kind="Internal").ap()
    sf = nc.dram_tensor("sf_d", AFM, F32, kind="Internal").ap()
    xr = nc.dram_tensor("xr_d", [1, N], F32, kind="Internal").ap()

    with contextlib.ExitStack() as stack:
        C = Ctx(nc, stack)
        common_setup(C)
        emit_pro(C, i["x"], xfm[0])
        cur = 0
        for li in range(4):
            a, b = xfm[cur], xfm[1 - cur]
            emit_ln(C, f"l{li}n", a, xn, xr if li == 0 else None)
            if li == 0:
                emit_attn0(C, "a0", xn, i["wa"][:, 0:1024],
                           i["wa"][:, 1024:2048], i["wa"][:, 2048:3072],
                           i["wp"], i["ba"], i["bp"], sf)
                emit_moe(C, "m0", xn, i["m0w1"], i["m0b1"], i["m0w2"],
                         i["m0b2"], a, sf, b, "relu", mk_routing_l0(xr))
            elif li == 1:
                emit_attn0(C, "a1", xn, i["wq1"], i["wk1"], i["wv1"],
                           i["wo1"], None, None, sf, wt_trans=True, rope=True)
                emit_ffn1(C, "n1", xn, i["n1wc"], i["n1bc"], i["n1w1"],
                          i["n1b1"], i["n1w2"], i["n1b2"], a, sf, b)
            elif li == 2:
                emit_attn2(C, "g2", xn, i["kw2"], i["kb2"], i["vq2"],
                           i["ow2"], i["ob2"], sf)
                emit_ln(C, "ln2b", xn, xn2, None, in_bf=True,
                        gamma_ap=i["lg2"], beta_ap=i["lb2"])
                emit_ffn2(C, "f2", xn, xn2, i["f2q1"], i["f2s1"], i["f2b1"],
                          i["f2q2"], i["f2s2"], i["f2b2"], a, sf, b)
            else:
                emit_attn3(C, "a3", xn, i["win3"], i["cw3"], i["cb3"],
                           i["wx3"], i["wdt3"], i["bdt3"], i["alog3"],
                           i["dv3"], i["wout3"], sf)
                emit_moe(C, "m3", xn, i["m3w1"], i["m3b1"], i["m3w2"],
                         i["m3b2"], a, sf, b, "silu",
                         mk_routing_l3(xn, i["gw3"], i["gb3"]))
            cur = 1 - cur
        emit_epi(C, xfm[cur], out)
    return nc


def kernel(**inp):
    x = np.ascontiguousarray(np.asarray(inp["L0_attn_x"], AF32))
    xs = [np.ascontiguousarray(x[2 * c:2 * c + 2].reshape(N, D))
          for c in range(NCORES)]
    wmap = {}
    for bn, k, _sh, isint in _WSPECS:
        wmap[bn] = np.ascontiguousarray(
            np.asarray(inp[k], np.int32 if isint else AF32))
    nc = build_fused()
    feeds = [dict(wmap, x=xs[c]) for c in range(NCORES)]
    res = run_bass_kernel_spmd(nc, feeds, core_ids=list(range(NCORES)))
    out = np.concatenate([np.asarray(rc["xo"]).reshape(SPC, T, D)
                          for rc in res.results], axis=0)
    return out
